# version 01
# baseline (speedup 1.0000x reference)
import contextlib
import math
import numpy as np
import ml_dtypes
import concourse.bass as bass
import concourse.mybir as mybir
from concourse.bass_utils import run_bass_kernel_spmd

F32 = mybir.dt.float32
BF16 = mybir.dt.bfloat16
AF = mybir.ActivationFunctionType
ALU = mybir.AluOpType
AX = mybir.AxisListType
BF = ml_dtypes.bfloat16

NCORES = 8
BPC = 4
T = 2048
D = 1024
NT = BPC * T
H, HD, CK = 16, 64, 64
NCHUNK = T // CK
NEG = -1.0e30

ENGINES = ("sync", "scalar", "vector", "tensor", "gpsimd")

ALL_STAGES = ("s1", "s2", "s3", "s4", "s5", "s6", "s7", "s8")
DEBUG_NAMES = set()
DEV_DUMP = False
BISECT_EVAC = False
BISECT_OMM = False
BISECT_POJ = False
BISECT_TS = False
BISECT_NOACT = False
S2_NB = 4
S2_HP = 8
S2_DO_P3 = True
S2_PA = True
S2_PB = True
S2_OG = True
S2_PB_S = True
S2_PB_O = True
S2_PB_U = True
S2_PB_EV = True
S3_ROUTER = True
S3_SHARED = True
S3_NEXP = 8
S3_G0 = True
S3_QL = 3
S2_DO_S2B = True


class P:

    def __init__(self, nc, stack):
        self.nc = nc
        self.stack = stack
        self.plan = {e: [] for e in ENGINES}
        self.meta = {e: [] for e in ENGINES}
        self.semh = {}
        self.cnt = {}
        self.sem_queues = {}

    def sem(self, name):
        if name not in self.semh:
            self.semh[name] = self.stack.enter_context(self.nc.semaphore(name))
            self.cnt[name] = 0
        return self.semh[name]

    def wait(self, eng, name, val=None):
        h = self.sem(name)
        v = self.cnt[name] if val is None else val
        self.plan[eng].append(lambda e, h=h, v=v: e.wait_ge(h, v))
        self.meta[eng].append(("wait", name, v))
        return v

    def dma(self, q, out, in_, sem, **kw):
        h = self.sem(sem)
        self.cnt[sem] += 16
        qc = "sw" if q == "gpsimd" else "hw"
        self.sem_queues.setdefault(sem, set()).add(q)
        assert all(("sw" if x == "gpsimd" else "hw") == qc for x in self.sem_queues[sem]), \
            f"sem {sem} mixes SWDGE and HWDGE queues"

        def f(e, out=out, in_=in_, h=h, kw=kw):
            e.dma_start(out=out, in_=in_, **kw).then_inc(h, 16)
        self.plan[q].append(f)
        self.meta[q].append(("inc", sem, 16))
        return self.cnt[sem]

    def dmat(self, q, out, in_, sem):
        h = self.sem(sem)
        self.cnt[sem] += 16
        self.sem_queues.setdefault(sem, set()).add(q)
        def f(e, out=out, in_=in_, h=h):
            e.dma_start_transpose(out, in_).then_inc(h, 16)
        self.plan[q].append(f)
        self.meta[q].append(("inc", sem, 16))
        return self.cnt[sem]

    def op(self, eng, meth, *args, sem=None, **kw):
        h = None
        if sem is not None:
            h = self.sem(sem)
            self.cnt[sem] += 1
        def f(e, meth=meth, args=args, kw=kw, h=h):
            ins = getattr(e, meth)(*args, **kw)
            if h is not None:
                ins.then_inc(h, 1)
        self.plan[eng].append(f)
        self.meta[eng].append(("inc", sem, 1) if sem is not None else ("op", meth, 0))
        return self.cnt.get(sem)


def check_deadlock(p):
    pos = {e: 0 for e in ENGINES}
    sems = {}
    progress = True
    while progress:
        progress = False
        for e in ENGINES:
            m = p.meta[e]
            while pos[e] < len(m):
                kind, a, b = m[pos[e]]
                if kind == "wait" and sems.get(a, 0) < b:
                    break
                if kind == "inc":
                    sems[a] = sems.get(a, 0) + b
                pos[e] += 1
                progress = True
    stuck = {e: pos[e] for e in ENGINES if pos[e] < len(p.meta[e])}
    if stuck:
        for e, i in stuck.items():
            kind, a, b = p.meta[e][i]
            print(f"DEADLOCK {e}@{i}/{len(p.meta[e])}: waiting {a} >= {b} "
                  f"(have {sems.get(a, 0)})")
        raise RuntimeError("plan deadlock")


class Ctx:

    def __init__(self, stages, core_inputs_names):
        self.stages = set(stages)
        self.nc = bass.Bass("TRN2", target_bir_lowering=False)
        self.stack = contextlib.ExitStack()
        self.p = P(self.nc, self.stack)
        self.dram = {}
        self.sb = {}
        self.ps = {}
        self.ext_in = []
        self.ext_out = []
        self._input_names = set(core_inputs_names)

    def d(self, name, shape, dtype, producer, consumers):
        if name in self.dram:
            return self.dram[name]
        if producer is None:
            kind = "ExternalInput"
            self.ext_in.append(name)
        elif producer in self.stages:
            if (name == "out_tm" or name in DEBUG_NAMES
                    or any(c not in self.stages for c in consumers)):
                kind = "ExternalOutput"
                self.ext_out.append(name)
            else:
                kind = "Internal"
        else:
            kind = "ExternalInput"
            self.ext_in.append(name)
        t = self.nc.dram_tensor(name, list(shape), dtype, kind=kind)
        ap = t.ap()
        self.dram[name] = ap
        return ap

    def sbuf(self, name, shape, dtype):
        if name not in self.sb:
            h = self.nc.alloc_sbuf_tensor(name, list(shape), dtype)
            self.sb[name] = h[:]
        return self.sb[name]

    def psum(self, name, shape, dtype=F32):
        if name not in self.ps:
            h = self.nc.alloc_psum_tensor(name, list(shape), dtype)
            self.ps[name] = h[:]
        return self.ps[name]

    def finalize(self):
        nc, p = self.nc, self.p
        for sem, queues in p.sem_queues.items():
            for q in queues:
                p.wait(q, sem)
        with nc.Block() as blk:
            for eng in ENGINES:
                fns = p.plan[eng]
                if not fns:
                    continue
                getattr(blk, eng)(lambda e, fns=fns: [f(e) for f in fns])



def host_prepare(inputs):
    f32 = np.float32
    asc = np.ascontiguousarray

    shared = {}
    shared["w0_write_T"] = asc(inputs["L0_attn_W_write"].T)
    shared["w0_out_T"] = asc(inputs["L0_attn_W_out"].T)
    shared["w0_gba_T"] = asc(np.concatenate(
        [inputs["L0_attn_W_gate"].T, inputs["L0_attn_W_beta"].T,
         inputs["L0_attn_W_alpha"].T], axis=1))
    shared["A_log_bc"] = asc(np.broadcast_to(inputs["L0_attn_A_log"][None, :], (128, 16))).astype(f32)
    shared["dt_bias_bc"] = asc(np.broadcast_to(inputs["L0_attn_dt_bias"][None, :], (128, 16))).astype(f32)
    shared["sw1"] = asc(inputs["L0_ffn_shared_w1"])
    shared["sw2"] = asc(inputs["L0_ffn_shared_w2"])
    shared["ew1"] = asc(inputs["L0_ffn_expert_w1"])
    shared["ew2"] = asc(inputs["L0_ffn_expert_w2"])
    shared["router_T"] = asc(inputs["L0_ffn_router_w"].T)
    shared["gamma_bc"] = asc(np.broadcast_to(inputs["L0_ffn_ln_gamma"][None, :], (128, 1024))).astype(f32)
    shared["beta_bc"] = asc(np.broadcast_to(inputs["L0_ffn_ln_beta"][None, :], (128, 1024))).astype(f32)
    shared["wq_T"] = asc(inputs["L1_attn_Wq"].T)
    shared["wk_T"] = asc(inputs["L1_attn_Wk"].T)
    shared["wv_T"] = asc(inputs["L1_attn_Wv"].T)
    shared["wo_T"] = asc(inputs["L1_attn_Wo"].T)
    shared["W1"] = asc(inputs["L1_ffn_W1"])
    shared["W2"] = asc(inputs["L1_ffn_W2"])
    shared["b1c"] = asc(inputs["L1_ffn_b1"].reshape(8, 32, 128).transpose(0, 2, 1))
    shared["b2c"] = asc(inputs["L1_ffn_b2"].reshape(8, 8, 128).transpose(0, 2, 1))

    shared["ident_bf"] = np.eye(128, dtype=BF)
    shared["ident_f32"] = np.eye(128, dtype=f32)
    ut = np.triu(np.ones((CK, CK), f32))
    sl = np.tril(np.ones((CK, CK), f32), -1)
    al = np.ones((CK, CK), f32)
    z = np.zeros((CK, CK), f32)
    shared["ut2"] = asc(np.block([[ut, z], [z, ut]]))
    shared["sl2"] = asc(np.block([[sl, z], [z, sl]]))
    shared["al2"] = asc(np.block([[al, z], [z, al]]))
    shared["halfsel"] = asc(np.block([[np.ones((1, 64), f32), np.zeros((1, 64), f32)],
                                      [np.zeros((1, 64), f32), np.ones((1, 64), f32)]]))
    mT = np.triu(np.ones((CK, CK), f32), 1)
    shared["maskT2"] = asc(np.concatenate([mT, mT], axis=0))
    mp = np.where(np.triu(np.ones((CK, CK), f32), 1) > 0, 0.0, NEG).astype(f32)
    shared["maskPenT"] = asc(mp.T)
    shared["ident64"] = np.eye(64, dtype=f32)
    shared["ones_row"] = np.ones((1, 128), f32)
    amask = np.zeros((4, 128, 512), f32)
    for m in range(4):
        for pp in range(128):
            for c in range(512):
                if c > m * 128 + pp:
                    amask[m, pp, c] = NEG
    shared["amask"] = amask

    x = inputs["L0_attn_out"]
    in_maps = []
    for c in range(NCORES):
        m = dict(shared)
        m["x_tm"] = asc(x[c * BPC:(c + 1) * BPC])
        in_maps.append(m)
    return in_maps



def dram_common(ctx):
    pass


def load_const(ctx, name, shape, dtype=F32, sb_dtype=None, q="sync", sem="cld"):
    ap = ctx.d(name, shape, dtype, None, ())
    sbd = sb_dtype or dtype
    sb = ctx.sbuf("c_" + name, shape, sbd)
    if sbd != dtype:
        ctx.p.dma("gpsimd", sb, ap, sem + "_g")
    else:
        ctx.p.dma(q, sb, ap, sem)
    return sb


def stage1(ctx):
    p = ctx.p
    x_tm = ctx.d("x_tm", (BPC, T, D), F32, None, ())
    xn_f = ctx.d("xn_tm_f32", (BPC, T, D), F32, "s1", ("s4",))
    xn_b = ctx.d("xn_tm_bf", (BPC, T, D), BF16, "s1", ("s1",))
    h_b = ctx.d("h_tm_bf", (BPC, T, D), BF16, "s1", ("s1",))
    xn_sh = ctx.d("xn_tm_shift", (BPC, T, D), BF16, "s1", ("s2",))
    xn_fm = ctx.d("xn_fm_bf", (BPC, D, T), BF16, "s1", ("s2",))
    h_fm = ctx.d("h_fm_bf", (BPC, D, T), BF16, "s1", ("s3",))
    gamma = ctx.d("gamma_bc", (128, D), F32, None, ())
    beta = ctx.d("beta_bc", (128, D), F32, None, ())

    gm = ctx.sbuf("s1_gamma", (128, D), F32)
    bt = ctx.sbuf("s1_beta", (128, D), F32)
    p.dma("sync", gm, gamma, "s1cl")
    p.dma("sync", bt, beta, "s1cl")

    zrow = ctx.sbuf("s1_zrow", (1, D), BF16)
    p.op("vector", "memset", zrow, 0.0, sem="s1z")
    p.wait("gpsimd", "s1z")
    for b in range(BPC):
        p.dma("gpsimd", xn_sh[b, 0:1, :], zrow, "s1sh")

    NB = 3
    xt = [ctx.sbuf(f"s1_x{i}", (128, D), F32) for i in range(NB)]
    bns = [ctx.sbuf(f"s1_bn{i}", (128, 12), F32) for i in range(NB)]
    sts = [ctx.sbuf(f"s1_stats{i}", (128, 16), F32) for i in range(NB)]
    xnt = [ctx.sbuf(f"s1_xn{i}", (128, D), F32) for i in range(NB)]
    xnb = [ctx.sbuf(f"s1_xnb{i}", (128, D), BF16) for i in range(NB)]
    ht = [ctx.sbuf(f"s1_h{i}", (128, D), F32) for i in range(NB)]
    ht2 = [ctx.sbuf(f"s1_h2{i}", (128, D), F32) for i in range(NB)]
    hb = [ctx.sbuf(f"s1_hb{i}", (128, D), BF16) for i in range(NB)]

    done_x_s = [0] * NB
    done_x_v = [0] * NB
    done_xn = [0] * NB
    done_xn_c = [0] * NB
    done_xnb = [0] * NB
    done_hb = [0] * NB
    done_ht = [0] * NB
    done_st = [0] * NB
    done_sqv = [0] * NB

    EPS = 1e-5
    for it in range(NT // 128):
        b, r = divmod(it, T // 128)
        pr = it % NB
        xti, xni, xbi, hbi = xt[pr], xnt[pr], xnb[pr], hb[pr]
        bn, st = bns[pr], sts[pr]
        if it >= NB:
            p.wait("sync", "s1xn", done_x_s[pr])
            p.wait("sync", "s1v", done_x_v[pr])
        ldv = p.dma("sync", xti, x_tm[b, r * 128:(r + 1) * 128, :], f"s1ld{pr}")

        if it >= NB:
            p.wait("vector", "s1hx", done_st[pr])
        if it == 0:
            p.wait("vector", "s1cl")
        p.wait("vector", f"s1ld{pr}", ldv)
        p.op("vector", "bn_stats", bn[:, 0:6], xti[:, 0:512])
        p.op("vector", "bn_stats", bn[:, 6:12], xti[:, 512:1024], sem="s1v")
        done_x_v[pr] = p.cnt["s1v"]
        p.wait("vector", "s1v")
        p.op("vector", "bn_aggr", st[:, 0:2], bn, sem="s1v")
        p.wait("vector", "s1v")
        p.op("vector", "tensor_scalar_add", st[:, 2:3], st[:, 1:2], EPS, sem="s1v")
        p.wait("scalar", "s1v")
        p.op("scalar", "activation", st[:, 3:4], st[:, 2:3], AF.Sqrt, sem="s1s")
        p.wait("vector", "s1s")
        p.op("vector", "reciprocal", st[:, 4:5], st[:, 3:4], sem="s1v")
        p.wait("vector", "s1v")
        p.op("vector", "tensor_scalar", st[:, 5:6], st[:, 0:1], st[:, 4:5], -1.0,
             op0=ALU.mult, op1=ALU.mult)
        p.op("vector", "tensor_scalar", st[:, 9:10], st[:, 4:5], st[:, 4:5], None,
             op0=ALU.mult, sem="s1v")
        p.wait("vector", "s1v")
        p.op("vector", "tensor_scalar", st[:, 13:14], st[:, 1:2], st[:, 9:10], None,
             op0=ALU.mult, sem="s1v")
        p.wait("vector", "s1v")
        p.op("vector", "tensor_scalar_add", st[:, 14:15], st[:, 13:14], EPS, sem="s1v")
        p.wait("scalar", "s1v")
        p.op("scalar", "activation", st[:, 10:11], st[:, 14:15], AF.Sqrt, sem="s1s")
        p.wait("vector", "s1s")
        p.op("vector", "reciprocal", st[:, 6:7], st[:, 10:11], sem="s1f")

        if it >= NB:
            p.wait("scalar", f"s1stf{pr}", done_xn[pr])
            p.wait("scalar", "s1xb", done_xn_c[pr])
        p.wait("scalar", "s1v")
        p.op("scalar", "activation", xni, xti, AF.Identity,
             scale=st[:, 4:5], bias=st[:, 5:6], sem="s1xn")
        p.wait("sync", "s1xn")
        p.dma("sync", xn_f[b, r * 128:(r + 1) * 128, :], xni, f"s1stf{pr}")

        if it >= NB:
            p.wait("vector", f"s1stx{pr}", done_xnb[pr])
        p.wait("vector", "s1xn")
        p.op("vector", "tensor_copy", xbi, xni, sem="s1xb")
        done_xn_c[pr] = p.cnt["s1xb"]
        p.wait("gpsimd", "s1xb")
        p.dma("gpsimd", xn_b[b, r * 128:(r + 1) * 128, :], xbi, f"s1stx{pr}")
        if r == T // 128 - 1:
            p.dma("gpsimd", xn_sh[b, r * 128 + 1:T, :], xbi[0:127, :], f"s1stx{pr}")
        else:
            p.dma("gpsimd", xn_sh[b, r * 128 + 1:r * 128 + 129, :], xbi, f"s1stx{pr}")
        done_xnb[pr] = p.cnt[f"s1stx{pr}"]

        if it >= NB:
            p.wait("scalar", "s1hb", done_ht[pr])
        p.wait("scalar", "s1f")
        p.op("scalar", "activation", ht[pr], xni, AF.Identity, scale=st[:, 6:7], sem="s1hx")
        done_x_s[pr] = p.cnt["s1xn"]
        done_xn[pr] = p.cnt[f"s1stf{pr}"]
        done_st[pr] = p.cnt["s1hx"]
        if it >= NB:
            p.wait("vector", f"s1sth{pr}", done_hb[pr])
        p.wait("vector", "s1hx")
        p.op("vector", "tensor_tensor", ht2[pr], ht[pr], gm, op=ALU.mult, sem="s1hg")
        p.wait("vector", "s1hg")
        p.op("vector", "tensor_tensor", hbi, ht2[pr], bt, op=ALU.add, sem="s1hb")
        done_ht[pr] = p.cnt["s1hb"]
        p.wait("gpsimd", "s1hb")
        p.dma("gpsimd", h_b[b, r * 128:(r + 1) * 128, :], hbi, f"s1sth{pr}")
        done_hb[pr] = p.cnt[f"s1sth{pr}"]

    trb = [ctx.sbuf(f"s1_tr{i}", (128, T), BF16) for i in range(2)]
    done_tr = [0, 0]
    for nm in ("s1stx0", "s1stx1", "s1stx2", "s1sth0", "s1sth1", "s1sth2"):
        p.wait("sync", nm)
    i = 0
    for b in range(BPC):
        for src, dst in ((xn_b, xn_fm), (h_b, h_fm)):
            for ki in range(8):
                pr = i % 2
                if i >= 2:
                    p.wait("sync", f"s1trs{pr}", done_tr[pr])
                ldv = p.dmat("sync", trb[pr], src[b, :, ki * 128:(ki + 1) * 128], f"s1trl{pr}")
                p.wait("scalar", f"s1trl{pr}", ldv)
                p.dma("scalar", dst[b, ki * 128:(ki + 1) * 128, :], trb[pr], f"s1trs{pr}")
                done_tr[pr] = p.cnt[f"s1trs{pr}"]
                i += 1


def stage2(ctx):
    p = ctx.p
    xn_fm = ctx.d("xn_fm_bf", (BPC, D, T), BF16, "s1", ("s2",))
    xn_sh = ctx.d("xn_tm_shift", (BPC, T, D), BF16, "s1", ("s2",))
    og_d = ctx.d("og_tm_bf", (BPC, T, D), BF16, "s2", ("s2",))
    sproj_d = ctx.d("sproj_fm_bf", (BPC, D, T), BF16, "s2", ("s4",))
    a_fm_d = ctx.d("a_fm_hbm", (BPC, H, T), F32, "s2", ("s2",))
    lam_fm_d = ctx.d("lam_fm_hbm", (BPC, H, NCHUNK), F32, "s2", ("s2",))

    wgba = ctx.sbuf("s2_wgba", (128, 8 * 48), BF16)
    p.dma("gpsimd", wgba.rearrange("q (ki c) -> q ki c", ki=8),
          ctx.d("w0_gba_T", (D, 48), F32, None, ()).rearrange("(ki q) c -> q ki c", q=128),
          "cldg")
    wwT = ctx.sbuf("s2_wwT", (128, 8 * 1024), BF16)
    p.dma("gpsimd", wwT.rearrange("q (ki c) -> q ki c", ki=8),
          ctx.d("w0_write_T", (D, D), F32, None, ()).rearrange("(ki q) c -> q ki c", q=128),
          "cldg")
    woT = ctx.sbuf("s2_woT", (128, 8 * 1024), BF16)
    p.dma("gpsimd", woT.rearrange("q (ki c) -> q ki c", ki=8),
          ctx.d("w0_out_T", (D, D), F32, None, ()).rearrange("(ki q) c -> q ki c", q=128),
          "cldg")
    alog = load_const(ctx, "A_log_bc", (128, 16))
    dtb = load_const(ctx, "dt_bias_bc", (128, 16))
    ut2 = load_const(ctx, "ut2", (128, 128))
    sl2 = load_const(ctx, "sl2", (128, 128))
    al2 = load_const(ctx, "al2", (128, 128))
    halfsel = load_const(ctx, "halfsel", (2, 128))
    maskPenT = load_const(ctx, "maskPenT", (64, 64))
    ident64 = load_const(ctx, "ident64", (64, 64))
    ones_row = load_const(ctx, "ones_row", (1, 128))
    identf = load_const(ctx, "ident_f32", (128, 128))

    nea = ctx.sbuf("s2_nea", (128, 16), F32)
    nea0 = ctx.sbuf("s2_nea0", (128, 16), F32)
    p.wait("scalar", "cld")
    p.wait("scalar", "cldg")
    p.wait("scalar", "cld_g")
    p.op("scalar", "activation", nea0, alog, AF.Exp, sem="d_s")
    p.wait("vector", "d_s")
    p.op("vector", "tensor_scalar_mul", nea, nea0, -1.0, sem="d_nea")

    gate_b = ctx.sbuf("s2_gate", (128, 256), F32)
    beta_b = ctx.sbuf("s2_beta", (128, 256), F32)
    decr_b = ctx.sbuf("s2_decr", (128, 256), F32)
    a_b = ctx.sbuf("s2_a", (128, 256), F32)
    na_b = ctx.sbuf("s2_na", (128, 256), F32)
    dwe_b = ctx.sbuf("s2_dwe", (128, 256), F32)
    lam_b = ctx.sbuf("s2_lam", (128, 256), F32)
    eag_b = ctx.sbuf("s2_eag", (128, 256), F32)
    tmp16 = ctx.sbuf("s2_tmp16", (128, 16 * 7), F32)
    vb_b = ctx.sbuf("s2_vb", (128, 16 * 1024), BF16)
    Xf = [ctx.sbuf(f"s2_Xf{i}", (128, 8 * 128), BF16) for i in range(2)]
    aev = ctx.sbuf("s2_aev", (16, 128), F32)
    xnK = ctx.sbuf("s2_xnK", (128, 2049), BF16)
    xnT = ctx.sbuf("s2_xnT", (128, 16 * 128), BF16)
    af = [ctx.sbuf(f"s2_af{i}", (1, T), F32) for i in range(2)]
    lamf = ctx.sbuf("s2_lamf", (2, NCHUNK), F32)
    lam2 = ctx.sbuf("s2_lam2", (128, NCHUNK), F32)
    Esb = ctx.sbuf("s2_E", (128, 64), F32)
    J_all = ctx.sbuf("s2_J", (128, 32 * 64), BF16)
    wdw2 = ctx.sbuf("s2_wdw", (128, 128), BF16)
    S2 = [ctx.sbuf(f"s2_S{i}", (128, 64), F32) for i in range(2)]
    S2bf = [ctx.sbuf(f"s2_Sbf{i}", (128, 64), BF16) for i in range(2)]
    otmp = ctx.sbuf("s2_otmp", (128, 128), F32)
    og_hp = ctx.sbuf("s2_og", (128, 16 * 128), BF16)
    og2_hp = ctx.sbuf("s2_og2", (128, 16 * 128), BF16)
    ogf = ctx.sbuf("s2_ogf", (128, 8 * T), BF16)
    spj = [ctx.sbuf(f"s2_spj{i}", (128, 512), BF16) for i in range(2)]

    pS = ctx.psum("pS", (128, 512), F32)
    pA = ctx.psum("pA", (128, 512), F32)
    pB = ctx.psum("pB", (128, 512), F32)
    pC = ctx.psum("pC", (128, 512), F32)
    pD = ctx.psum("pD", (128, 512), F32)

    pE = ctx.psum("pE", (128, 512), F32)
    p48 = pS[:, 0:48]
    pc = pS[:, 64:112]
    pat = pS[0:16, 128:256]
    pL = pS[:, 256:288]
    pM = pC[:, 0:64]
    pB2 = pC[:, 64:128]
    pU = pE[:, 0:64]
    pOI = pD[:, 0:128]
    pOJ = pD[:, 128:256]

    p.wait("tensor", "cld")
    p.wait("tensor", "cldg")
    p.wait("tensor", "cld_g")
    p.wait("tensor", "d_nea")

    done_xf = [0, 0]
    done_p48_v = 0
    done_p48_s = 0
    done_pv = [0, 0]
    done_pc = 0
    done_pM = 0
    done_spj = [0, 0]

    for b in range(S2_NB):
        for tt in range(16):
            pr = tt % 2
            p.wait("tensor", "d_spev")
            p.wait("sync", "d_mmv", done_xf[pr])
            ldv = p.dma("sync", Xf[pr].rearrange("q (ki t) -> q ki t", ki=8),
                        xn_fm[b].rearrange("(ki q) t -> q ki t", q=128)[:, :, tt * 128:(tt + 1) * 128],
                        f"d_xf{pr}")
            p.wait("tensor", f"d_xf{pr}", ldv)
            p.wait("tensor", "d_v", done_p48_v)
            p.wait("tensor", "d_gb", done_p48_s)
            for ki in range(8):
                p.op("tensor", "matmul", p48, Xf[pr][:, ki * 128:(ki + 1) * 128],
                     wgba[:, ki * 48:(ki + 1) * 48], start=(ki == 0), stop=(ki == 7),
                     sem="d_mm48" if ki == 7 else None)
            p.wait("scalar", "d_mm48")
            p.op("scalar", "activation", gate_b[:, tt * 16:(tt + 1) * 16], p48[:, 0:16],
                 AF.Sigmoid)
            p.op("scalar", "activation", beta_b[:, tt * 16:(tt + 1) * 16], p48[:, 16:32],
                 AF.Sigmoid, sem="d_gb")
            done_p48_s = p.cnt["d_gb"]
            p.wait("vector", "d_mm48")
            p.op("vector", "tensor_tensor", tmp16[:, 0:16], p48[:, 32:48], dtb, op=ALU.add,
                 sem="d_v")
            done_p48_v = p.cnt["d_v"]
            p.wait("vector", "d_v")
            p.op("vector", "tensor_scalar_mul", tmp16[:, 16:32], tmp16[:, 0:16], -1.0, sem="d_v")
            p.wait("vector", "d_v")
            p.op("vector", "tensor_tensor", tmp16[:, 32:48], tmp16[:, 0:16], tmp16[:, 16:32],
                 op=ALU.min, sem="d_v")
            p.wait("scalar", "d_v")
            p.op("scalar", "activation", tmp16[:, 48:64], tmp16[:, 32:48], AF.Exp, sem="d_s")
            p.wait("scalar", "d_s")
            p.op("scalar", "activation", tmp16[:, 64:80], tmp16[:, 48:64], AF.Ln, bias=1.0,
                 sem="d_s")
            p.op("vector", "tensor_single_scalar", tmp16[:, 80:96], tmp16[:, 0:16], 0.0,
                 ALU.max, sem="d_v")
            p.wait("vector", "d_s")
            p.wait("vector", "d_v")
            p.op("vector", "tensor_tensor", tmp16[:, 96:112], tmp16[:, 80:96], tmp16[:, 64:80],
                 op=ALU.add, sem="d_v")
            p.wait("vector", "d_v")
            p.op("vector", "tensor_tensor", decr_b[:, tt * 16:(tt + 1) * 16], tmp16[:, 96:112],
                 nea, op=ALU.mult, sem="d_dec")

            for jh in range(2):
                pv = [pA, pB][jh]
                p.wait("tensor", "d_vev", done_pv[jh])
                for ki in range(8):
                    p.op("tensor", "matmul", pv, Xf[pr][:, ki * 128:(ki + 1) * 128],
                         wwT[:, ki * 1024 + jh * 512: ki * 1024 + jh * 512 + 512],
                         start=(ki == 0), stop=(ki == 7), sem="d_mmv" if ki == 7 else None)
                done_xf[pr] = p.cnt["d_mmv"]
                p.wait("scalar", "d_mmv")
                p.wait("scalar", "d_gb", done_p48_s)
                for h8 in range(8):
                    h = jh * 8 + h8
                    p.op("scalar", "activation",
                         vb_b[:, tt * 1024 + h * 64: tt * 1024 + h * 64 + 64],
                         pv[:, h8 * 64:(h8 + 1) * 64], AF.Identity,
                         scale=beta_b[:, tt * 16 + h: tt * 16 + h + 1],
                         sem="d_vev" if h8 == 7 else None)
                done_pv[jh] = p.cnt["d_vev"]

        for tt in range(16):
            cs = tt * 16
            p.wait("tensor", "d_dec")
            p.wait("tensor", "d_gb")
            p.wait("tensor", "d_vev")
            p.wait("tensor", "d_aev", done_pc)
            rhs = decr_b[:, cs:cs + 16]
            p.op("tensor", "matmul", pc[:, 0:16], ut2, rhs, start=True, stop=True)
            p.op("tensor", "matmul", pc[:, 16:32], sl2, rhs, start=True, stop=True)
            p.op("tensor", "matmul", pc[:, 32:48], al2, rhs, start=True, stop=True, sem="d_mmc")
            p.wait("scalar", "d_mmc")
            p.wait("scalar", "d_eag")
            p.op("scalar", "copy", a_b[:, cs:cs + 16], pc[:, 0:16])
            p.op("scalar", "activation", dwe_b[:, cs:cs + 16], pc[:, 16:32], AF.Exp)
            p.op("scalar", "activation", lam_b[:, cs:cs + 16], pc[:, 32:48], AF.Exp)
            p.op("scalar", "activation", tmp16[:, 96:112], pc[:, 0:16], AF.Exp, sem="d_s2")
            p.wait("vector", "d_s2")
            p.op("vector", "tensor_scalar_mul", na_b[:, cs:cs + 16], a_b[:, cs:cs + 16], -1.0)
            p.op("vector", "tensor_tensor", eag_b[:, cs:cs + 16], tmp16[:, 96:112],
                 gate_b[:, cs:cs + 16], op=ALU.mult, sem="d_eag")
            p.wait("tensor", "d_s2")
            p.op("tensor", "transpose", pat, a_b[:, cs:cs + 16], identf, sem="d_tra")
            p.wait("scalar", "d_tra")
            p.wait("scalar", "d_ast")
            p.op("scalar", "copy", aev, pat, sem="d_aev")
            done_pc = p.cnt["d_aev"]
            p.wait("sync", "d_aev")
            p.dma("sync", a_fm_d[b][:, tt * 128:(tt + 1) * 128], aev, "d_ast")
        p.wait("gpsimd", "d_s2")
        for par in range(2):
            p.dma("gpsimd",
                  lam_fm_d[b].rearrange("h (tt par) -> par tt h", par=2)[par:par + 1],
                  lam_b[par * 64:par * 64 + 1, :].rearrange("one (tt h) -> one tt h", h=16),
                  "d_lst", allow_slow_non_contiguous=True)

        if DEV_DUMP:
            p.wait("gpsimd", "d_eag")
            p.wait("gpsimd", "d_dec")
            p.wait("gpsimd", "d_vev")
            for nm, buf in (("dbg_dec", decr_b), ("dbg_a", a_b), ("dbg_dwe", dwe_b),
                            ("dbg_lam", lam_b), ("dbg_eag", eag_b), ("dbg_na", na_b),
                            ("dbg_gate", gate_b), ("dbg_beta", beta_b)):
                dd = ctx.d(nm, (BPC, 128, 256), F32, "s2", ("zz",))
                p.dma("gpsimd", dd[b], buf, "d_dbg")
            dv = ctx.d("dbg_vb", (BPC, 128, 16 * 1024), BF16, "s2", ("zz",))
            p.dma("gpsimd", dv[b], vb_b, "d_dbg")

        p.wait("sync", "d_ast")
        p.wait("sync", "d_lst")
        for hp in range(S2_HP if S2_DO_P3 else 0):
            h0g = hp * 2
            p.op("vector", "memset", xnK[:, 0:1], 0.0, sem="d_mz")
            p.wait("sync", "d_mz")
            p.wait("sync", "d_mmO")
            p.wait("sync", "d_mmU")
            l1 = p.dma("sync", xnK[:, 1:2049], xn_fm[b][hp * 128:(hp + 1) * 128, :], "d_hp")
            l2 = p.dma("sync", xnT.rearrange("q (tt j) -> q tt j", tt=16),
                       xn_sh[b].rearrange("(tt q) j -> q tt j", q=128)[:, :, hp * 128:(hp + 1) * 128],
                       "d_hp")
            l3 = p.dma("sync", af[0], a_fm_d[b][h0g:h0g + 1, :], "d_hp")
            l4 = p.dma("sync", af[1], a_fm_d[b][h0g + 1:h0g + 2, :], "d_hp")
            l5 = p.dma("sync", lamf, lam_fm_d[b][h0g:h0g + 2, :], "d_hp")
            p.wait("tensor", "d_hp", l5)
            p.wait("tensor", "d_aev")
            p.wait("scalar", "d_hp", l5)
            p.wait("scalar", "d_eag")
            p.op("tensor", "matmul", pL, halfsel, lamf, start=True, stop=True, sem="d_mmL")
            p.wait("vector", "d_mmL")
            p.op("vector", "tensor_copy", lam2, pL, sem="d_lam2")

            p.wait("vector", "d_mmO")
            for cp in range(16 if S2_PA else 0):
                for hh in range(2):
                    jslot = (cp * 2 + hh) * 64
                    p.wait("tensor", "d_Jev", done_pM)
                    p.wait("scalar", "d_Jev", done_pM)
                    for par in range(2):
                        i = cp * 2 + par
                        p.op("tensor", "matmul", pM[par * 64:(par + 1) * 64, :],
                             xnK[hh * 64:(hh + 1) * 64, i * 64: i * 64 + 64],
                             xnK[hh * 64:(hh + 1) * 64, 1 + i * 64: 1 + i * 64 + 64],
                             start=True, stop=True)
                        p.op("tensor", "matmul", pB2[par * 64:(par + 1) * 64, :],
                             ones_row[0:1, 0:64],
                             af[hh][0:1, i * 64: i * 64 + 64], start=True, stop=False)
                        p.op("tensor", "matmul", pB2[par * 64:(par + 1) * 64, :],
                             maskPenT, ident64, start=False, stop=True,
                             sem="d_mmJ" if par == 1 else None)
                    p.wait("scalar", "d_mmJ")
                    p.op("scalar", "activation", Esb, pB2, AF.Exp,
                         bias=na_b[:, cp * 16 + h0g + hh: cp * 16 + h0g + hh + 1], sem="d_E")
                    p.wait("vector", "d_E")
                    p.op("vector", "tensor_tensor", J_all[:, jslot:jslot + 64], Esb, pM,
                         op=ALU.mult, sem="d_Jev")
                    done_pM = p.cnt["d_Jev"]

            p.wait("vector", "d_ogst")
            p.op("vector", "memset", og2_hp[:, 0:128], 0.0, sem="d_og2z")
            p.op("vector", "memset", S2[1], 0.0, sem="d_v")
            p.wait("vector", "d_v")
            jv = p.cnt["d_Jev"]
            done_wdw = p.cnt.get("d_mmU", 0)
            done_pU = 0
            done_sbf = 0
            done_sbf_prev = 0
            done_mmO_prev = p.cnt.get("d_mmO", 0)
            done_pO = p.cnt.get("d_oev", 0)
            for i in range(NCHUNK if S2_PB else 0):
                par, cp = i % 2, i // 2
                cur, nxt = S2[(i + 1) % 2], S2[i % 2]
                sbf_cur = S2bf[i % 2]
                sbf_prev = S2bf[(i + 1) % 2]
                p.wait("scalar", "d_mmU", done_wdw)
                for hh in range(2):
                    h = h0g + hh
                    p.op("scalar", "activation",
                         wdw2[par * 64:(par + 1) * 64, hh * 64:(hh + 1) * 64],
                         xnT[par * 64:(par + 1) * 64,
                             cp * 128 + hh * 64: cp * 128 + hh * 64 + 64],
                         AF.Identity,
                         scale=dwe_b[par * 64:(par + 1) * 64, cp * 16 + h: cp * 16 + h + 1],
                         sem="d_wdw" if hh == 1 else None)
                p.wait("tensor", "d_wdw")
                p.wait("tensor", "d_Supd", done_pU)
                for hh in range(2 if S2_PB_U else 0):
                    h = h0g + hh
                    p.op("tensor", "matmul", pU[hh * 64:(hh + 1) * 64, :],
                         wdw2[par * 64:(par + 1) * 64, hh * 64:(hh + 1) * 64],
                         vb_b[par * 64:(par + 1) * 64,
                              cp * 1024 + h * 64: cp * 1024 + h * 64 + 64],
                         start=True, stop=True, sem="d_mmU" if hh == 1 else None)
                done_wdw = p.cnt["d_mmU"]
                p.wait("tensor", "d_oev", done_pO)
                for hh in range(2 if S2_PB_O else 0):
                    if i > 0 and not BISECT_OMM:
                        p.wait("tensor", "d_Sbf", done_sbf_prev)
                        p.op("tensor", "matmul",
                             pOI[par * 64:(par + 1) * 64, hh * 64:(hh + 1) * 64],
                             xnK[hh * 64:(hh + 1) * 64, 1 + i * 64: 1 + i * 64 + 64],
                             sbf_prev[hh * 64:(hh + 1) * 64, :], start=True, stop=True)
                    p.wait("tensor", "d_Jev", jv)
                    p.op("tensor", "matmul",
                         pOJ[par * 64:(par + 1) * 64, hh * 64:(hh + 1) * 64],
                         J_all[par * 64:(par + 1) * 64,
                               (cp * 2 + hh) * 64:(cp * 2 + hh) * 64 + 64],
                         vb_b[par * 64:(par + 1) * 64,
                              cp * 1024 + (h0g + hh) * 64: cp * 1024 + (h0g + hh) * 64 + 64],
                         start=True, stop=True, sem="d_mmO" if hh == 1 else None)
                done_mmO = p.cnt["d_mmO"]
                if not S2_PB_S:
                    continue
                p.wait("vector", "d_mmU")
                p.wait("vector", "d_Sbf", done_sbf)
                p.wait("vector", "d_lam2")
                p.op("vector", "scalar_tensor_tensor", nxt, cur, lam2[:, i:i + 1], pU,
                     op0=ALU.mult, op1=ALU.add, sem="d_Supd")
                done_pU = p.cnt["d_Supd"]
                p.wait("scalar", "d_Supd")
                p.wait("scalar", "d_mmO", done_mmO_prev)
                p.op("scalar", "copy", sbf_cur, nxt, sem="d_Sbf")
                done_sbf_prev = done_sbf
                done_sbf = p.cnt["d_Sbf"]
                done_mmO_prev = done_mmO
                p.wait("scalar", "d_mmO")
                p.wait("scalar", "d_oev", done_pO)
                for hh in range(2 if (S2_PB_O and S2_PB_EV) else 0):
                    h = h0g + hh
                    gcol = gate_b[par * 64:(par + 1) * 64, cp * 16 + h: cp * 16 + h + 1]
                    ecol = eag_b[par * 64:(par + 1) * 64, cp * 16 + h: cp * 16 + h + 1]
                    dst = og_hp[par * 64:(par + 1) * 64,
                                cp * 128 + hh * 64: cp * 128 + hh * 64 + 64]
                    dst2 = og2_hp[par * 64:(par + 1) * 64,
                                  cp * 128 + hh * 64: cp * 128 + hh * 64 + 64]
                    p.op("scalar", "activation", dst,
                         pOJ[par * 64:(par + 1) * 64, hh * 64:(hh + 1) * 64],
                         AF.Identity, scale=gcol,
                         sem=None if i > 0 else ("d_oev" if hh == 1 else None))
                    if i > 0:
                        p.op("scalar", "activation", dst2,
                             pOI[par * 64:(par + 1) * 64, hh * 64:(hh + 1) * 64],
                             AF.Identity, scale=ecol,
                             sem="d_oev" if hh == 1 else None)
                done_pO = p.cnt["d_oev"]

            if DEV_DUMP and b == 0:
                dj = ctx.d("dbg_J", (8, 128, 32 * 64), BF16, "s2", ("zz",))
                p.wait("gpsimd", "d_Jev")
                p.dma("gpsimd", dj[hp], J_all, "d_dbg")
            if not S2_OG:
                continue
            p.wait("gpsimd", "d_oev")
            v1 = p.dma("gpsimd",
                  og_d[b].rearrange("(tt q) j -> q tt j", q=128)[:, :, hp * 128:(hp + 1) * 128],
                  og_hp.rearrange("q (tt c) -> q tt c", tt=16), "d_ogst")
            p.wait("gpsimd", "d_ogst", v1)
            p.wait("gpsimd", "d_og2z")
            p.dma("gpsimd",
                  og_d[b].rearrange("(tt q) j -> q tt j", q=128)[:, :, hp * 128:(hp + 1) * 128],
                  og2_hp.rearrange("q (tt c) -> q tt c", tt=16), "d_ogst",
                  accum_op=ALU.add)

        if not S2_DO_S2B:
            continue
        p.wait("sync", "d_ogst")
        p.wait("sync", "d_mmS")
        for ki in range(8):
            p.dmat("sync", ogf[:, ki * T:(ki + 1) * T], og_d[b][:, ki * 128:(ki + 1) * 128],
                   "d_ogl")
        lv = p.cnt["d_ogl"]
        for ts_ in range(4):
            for mj in range(8):
                po = [pA, pB][mj % 2]
                spr = spj[mj % 2]
                p.wait("tensor", f"d_spst{mj % 2}", done_spj[mj % 2])
                p.wait("tensor", "d_ogl", lv)
                p.wait("tensor", "d_vev")
                for ki in range(8):
                    p.op("tensor", "matmul", po,
                         woT[:, ki * 1024 + mj * 128: ki * 1024 + mj * 128 + 128],
                         ogf[:, ki * T + ts_ * 512: ki * T + ts_ * 512 + 512],
                         start=(ki == 0), stop=(ki == 7), sem="d_mmS" if ki == 7 else None)
                p.wait("vector", "d_mmS")
                p.op("vector", "tensor_copy", spr, po, sem="d_spev")
                p.wait("gpsimd", "d_spev")
                p.dma("gpsimd", sproj_d[b][mj * 128:(mj + 1) * 128, ts_ * 512:(ts_ + 1) * 512],
                      spr, f"d_spst{mj % 2}")
                done_spj[mj % 2] = p.cnt[f"d_spst{mj % 2}"]


def stage3(ctx):
    p = ctx.p
    h_fm = ctx.d("h_fm_bf", (BPC, D, T), BF16, "s1", ("s3",))
    acc0 = ctx.d("acc0_fm", (BPC, D, T), F32, "s3", ("s3",))
    g0_d = ctx.d("g0_bf", (BPC, D, T), BF16, "s3", ("s4",))
    gates_d = ctx.d("gates0", (8, NT), F32, "s3", ("s3",))
    sw1_d = ctx.d("sw1", (2048, D), F32, None, ())
    sw2_d = ctx.d("sw2", (D, 2048), F32, None, ())
    ew1_d = ctx.d("ew1", (8, 2048, D), F32, None, ())
    ew2_d = ctx.d("ew2", (8, D, 2048), F32, None, ())
    rt_d = ctx.d("router_T", (D, 8), F32, None, ())

    identb = load_const(ctx, "ident_bf", (128, 128), BF16, sem="cld3")
    identf = load_const(ctx, "ident_f32", (128, 128), sem="cld3")
    ones_row = load_const(ctx, "ones_row", (1, 128), sem="cld3")

    rt = ctx.sbuf("s3_rt", (128, 8 * 8), BF16)
    p.dma("gpsimd", rt.rearrange("q (ki c) -> q ki c", ki=8),
          rt_d.rearrange("(ki q) c -> q ki c", q=128), "cld3g")

    pS = ctx.psum("pS", (128, 512), F32)
    pA = ctx.psum("pA", (128, 512), F32)
    pB = ctx.psum("pB", (128, 512), F32)
    pC = ctx.psum("pC", (128, 512), F32)
    pT = ctx.psum("pT2", (128, 512), BF16)
    pT3 = ctx.psum("pT3", (128, 512), BF16)

    hbuf = [ctx.sbuf(f"s3_h{i}", (128, 8 * 512), BF16) for i in range(2)]
    h1b = ctx.sbuf("s3_h1big", (128, 16 * 512), BF16)
    wqA = ctx.sbuf("s3_wqA", (128, 128 * 128), BF16)
    wqB = ctx.sbuf("s3_wqB", (128, 128 * 128), BF16)
    wnat = [ctx.sbuf(f"s3_wn{i}", (128, 2048), F32) for i in range(2)]
    qst = ctx.sbuf("s3_qst", (128, 8), F32)
    qt1 = ctx.sbuf("s3_qt1", (128, 2048), F32)
    qt2 = ctx.sbuf("s3_qt2", (128, 2048), F32)
    qt3 = ctx.sbuf("s3_qt3", (128, 2048), F32)
    qr = ctx.sbuf("s3_qr", (128, 2048), BF16)
    lg = ctx.sbuf("s3_lg", (8, 512), F32)
    lt = ctx.sbuf("s3_lt", (128, 8), F32)
    rst = ctx.sbuf("s3_rst", (128, 8), F32)
    exr = ctx.sbuf("s3_exr", (128, 8), F32)
    gtm = ctx.sbuf("s3_gtm", (128, 8), F32)
    gtm0 = ctx.sbuf("s3_gtm0", (128, 8), F32)
    grow = ctx.sbuf("s3_grow", (1, 512), F32)
    gbc = ctx.sbuf("s3_gbc", (128, 512), F32)
    ev = [ctx.sbuf(f"s3_ev{i}", (128, 512), F32) for i in range(2)]
    gl = [ctx.sbuf(f"s3_gl{i}", (128, 512), F32) for i in range(2)]
    gq = [ctx.sbuf(f"s3_gq{i}", (128, 512), BF16) for i in range(2)]

    p.wait("tensor", "cld3")
    p.wait("tensor", "cld3g")
    p.wait("tensor", "cld3_g")

    dn = {"hb": [0, 0], "ff": [0, 0], "ev": [0, 0], "evd": [0, 0],
          "acc_prev": 0, "gbc": 0, "gbc_war": 0, "h1": 0, "wn": [0, 0], "accb": [], "gmm": 0,
          "qr": 0, "plg": 0, "ptr": 0, "gst": 0}

    for gtt in range(16 if S3_ROUTER else 0):
        b, ts_ = divmod(gtt, 4)
        pr = gtt % 2
        p.wait("sync", "f_mm", dn["hb"][pr])
        ldv = p.dma("sync", hbuf[pr].rearrange("q (ki t) -> q ki t", ki=8),
                    h_fm[b].rearrange("(ki q) t -> q ki t", q=128)[:, :, ts_ * 512:(ts_ + 1) * 512],
                    f"f_hl{pr}")
        p.wait("tensor", f"f_hl{pr}", ldv)
        p.wait("tensor", "f_lg", dn["plg"])
        p.wait("tensor", "f_ltc", dn["ptr"])
        for ki in range(8):
            p.op("tensor", "matmul", pS[0:8, 0:512], rt[:, ki * 8:(ki + 1) * 8],
                 hbuf[pr][:, ki * 512:(ki + 1) * 512], start=(ki == 0), stop=(ki == 7),
                 sem="f_mm" if ki == 7 else None)
        dn["hb"][pr] = p.cnt["f_mm"]
        p.wait("scalar", "f_mm")
        p.op("scalar", "copy", lg, pS[0:8, 0:512], sem="f_lg")
        dn["plg"] = p.cnt["f_lg"]
        for j in range(4):
            p.wait("tensor", "f_lg", dn["plg"])
            p.wait("tensor", "f_ltc", dn["ptr"])
            p.op("tensor", "transpose", pS[0:128, 0:8], lg[:, j * 128:(j + 1) * 128],
                 identf[0:8, 0:8], sem="f_tr")
            p.wait("scalar", "f_tr")
            p.wait("scalar", "f_gdone")
            p.op("scalar", "copy", lt, pS[0:128, 0:8], sem="f_ltc")
            dn["ptr"] = p.cnt["f_ltc"]
            p.wait("vector", "f_ltc")
            p.op("vector", "reduce_max", rst[:, 0:1], lt, AX.X, sem="f_v")
            p.wait("vector", "f_v")
            p.op("vector", "tensor_scalar", gtm0, lt, rst[:, 0:1], None, op0=ALU.is_ge)
            p.op("vector", "tensor_scalar_mul", rst[:, 1:2], rst[:, 0:1], -1.0, sem="f_v")
            p.wait("scalar", "f_v")
            p.op("scalar", "activation", exr, lt, AF.Exp,
                 bias=rst[:, 1:2], accum_out=rst[:, 3:4], sem="f_s")
            p.wait("vector", "f_s")
            p.op("vector", "reciprocal", rst[:, 4:5], rst[:, 3:4], sem="f_v")
            p.wait("vector", "f_v")
            p.wait("vector", "f_gst", dn["gst"])
            p.op("vector", "tensor_scalar", gtm, gtm0, rst[:, 4:5], None, op0=ALU.mult,
                 sem="f_gdone")
            p.wait("gpsimd", "f_gdone")
            tr0 = gtt * 512 + j * 128
            p.dma("gpsimd", gates_d[:, tr0:tr0 + 128].rearrange("e t -> t e"), gtm,
                  "f_gst", allow_slow_non_contiguous=True)
            dn["gst"] = p.cnt["f_gst"]

    def quant_matrix(w_d, RR, CC, dest, slot_of):
        for fr in range(RR // 128):
            pr = fr % 2
            wn = wnat[pr]
            p.wait("sync", "q_done", dn["wn"][pr])
            lv = p.dma("sync", wn[:, 0:CC], w_d[fr * 128:(fr + 1) * 128, :], f"q_ld{pr}")
            if S3_QL < 2:
                continue
            p.wait("vector", f"q_ld{pr}", lv)
            p.op("vector", "reduce_sum", qst[:, 0:1], wn[:, 0:CC], AX.X,
                 apply_absolute_value=True, sem="q_v")
            p.wait("vector", "q_v")
            p.op("vector", "tensor_scalar_mul", qst[:, 1:2], qst[:, 0:1], 1.0 / CC, sem="q_v")
            p.wait("vector", "q_v")
            p.op("vector", "tensor_scalar_max", qst[:, 2:3], qst[:, 1:2], 1e-5, sem="q_v")
            p.wait("vector", "q_v")
            p.op("vector", "tensor_scalar_mul", qst[:, 3:4], qst[:, 2:3], 0.5)
            p.op("vector", "tensor_scalar_mul", qst[:, 4:5], qst[:, 2:3], -0.5, sem="q_v")
            p.wait("vector", "q_v")
            p.op("vector", "tensor_scalar", qt1[:, 0:CC], wn[:, 0:CC], qst[:, 3:4], None,
                 op0=ALU.is_gt)
            p.op("vector", "tensor_scalar", qt2[:, 0:CC], wn[:, 0:CC], qst[:, 4:5], None,
                 op0=ALU.is_lt, sem="q_v")
            p.wait("vector", "q_v")
            p.op("vector", "tensor_tensor", qt3[:, 0:CC], qt1[:, 0:CC], qt2[:, 0:CC],
                 op=ALU.subtract, sem="q_v")
            p.wait("vector", "q_v")
            p.wait("vector", "q_tr", dn["qr"])
            p.op("vector", "tensor_scalar", qr[:, 0:CC], qt3[:, 0:CC], qst[:, 2:3], None,
                 op0=ALU.mult, sem="q_done")
            dn["wn"][pr] = p.cnt["q_done"]
            if S3_QL < 3:
                continue
            p.wait("tensor", "q_done")
            for sub in range(CC // 128):
                p.wait("tensor", "q_ev", p.cnt.get("q_ev", 0))
                k = p.cnt.get("q_ev", 0)
                ptk = [pT, pT3][k % 2][:, ((k // 2) % 2) * 128:((k // 2) % 2) * 128 + 128]
                p.op("tensor", "transpose", ptk,
                     qr[:, sub * 128:(sub + 1) * 128], identb, sem="q_tr")
                p.wait("vector", "q_tr")
                slot = slot_of(fr, sub)
                p.op("vector", "tensor_copy", dest[:, slot * 128:(slot + 1) * 128],
                     ptk, sem="q_ev")
            dn["qr"] = p.cnt["q_tr"]
        p.wait("tensor", "q_ev")

    def ffn_pass(e):
        for gtt in range(16):
            b, ts_ = divmod(gtt, 4)
            pr = gtt % 2
            p.wait("sync", "f_mm", dn["hb"][pr])
            ldv = p.dma("sync", hbuf[pr].rearrange("q (ki t) -> q ki t", ki=8),
                        h_fm[b].rearrange("(ki q) t -> q ki t", q=128)
                        [:, :, ts_ * 512:(ts_ + 1) * 512], f"f_hl{pr}")
            if e >= 0:
                p.wait("sync", "f_mmg", dn["gmm"])
                glv = p.dma("sync", grow, gates_d[e:e + 1, gtt * 512:(gtt + 1) * 512], "f_gl")
                p.wait("tensor", "f_gl", glv)
                p.wait("tensor", "f_gbc", dn["gbc"])
                p.op("tensor", "matmul", pC[:, 0:512], ones_row[0:1, 0:128],
                     grow, start=True, stop=True, sem="f_mmg")
                dn["gmm"] = p.cnt["f_mmg"]
                p.wait("scalar", "f_mmg")
                p.wait("scalar", "f_ev", dn["gbc_war"])
                p.op("scalar", "copy", gbc, pC[:, 0:512], sem="f_gbc")
                dn["gbc"] = p.cnt["f_gbc"]
            p.wait("tensor", f"f_hl{pr}", ldv)
            p.wait("scalar", "f_mm2", dn["h1"])
            for fj in range(16):
                pp = [pA, pB][fj % 2]
                p.wait("tensor", "f_h1ev", dn["ff"][fj % 2])
                p.wait("tensor", "f_ev", dn["ev"][fj % 2])
                for ki in range(8):
                    p.op("tensor", "matmul", pp,
                         wqA[:, (ki * 16 + fj) * 128:(ki * 16 + fj) * 128 + 128],
                         hbuf[pr][:, ki * 512:(ki + 1) * 512], start=(ki == 0), stop=(ki == 7),
                         sem="f_mm" if ki == 7 else None)
                p.wait("scalar", "f_mm")
                p.op("scalar", "activation", h1b[:, fj * 512:(fj + 1) * 512], pp, AF.Silu,
                     sem="f_h1ev")
                dn["ff"][fj % 2] = p.cnt["f_h1ev"]
            dn["hb"][pr] = p.cnt["f_mm"]
            p.wait("tensor", "f_h1ev")
            for mj in range(8):
                pp = [pA, pB][mj % 2]
                p.wait("tensor", "f_ev", dn["ev"][mj % 2])
                for fk in range(16):
                    p.op("tensor", "matmul", pp,
                         wqB[:, (fk * 8 + mj) * 128:(fk * 8 + mj) * 128 + 128],
                         h1b[:, fk * 512:(fk + 1) * 512], start=(fk == 0), stop=(fk == 15),
                         sem="f_mm2" if fk == 15 else None)
                evb = ev[mj % 2]
                p.wait("scalar", "f_mm2")
                p.wait("scalar", f"f_acst{mj % 2}", dn["evd"][mj % 2])
                if e < 0:
                    p.op("scalar", "copy", evb, pp, sem="f_ev")
                else:
                    glb = gl[mj % 2]
                    p.op("scalar", "copy", glb, pp, sem="f_ev0")
                    p.wait("vector", "f_ev0")
                    p.wait("vector", "f_gbc", dn["gbc"])
                    p.wait("vector", f"f_acst{mj % 2}", dn["evd"][mj % 2])
                    p.op("vector", "tensor_tensor", evb, glb, gbc, op=ALU.mult, sem="f_ev")
                dn["ev"][mj % 2] = p.cnt["f_ev"]
                p.wait("gpsimd", "f_ev")
                if e >= 0 and dn["accb"]:
                    for nm, vv in dn["accb"]:
                        p.wait("gpsimd", nm, vv)
                    dn["accb"] = []
                p.dma("gpsimd", acc0[b][mj * 128:(mj + 1) * 128, ts_ * 512:(ts_ + 1) * 512],
                      evb, f"f_acst{mj % 2}", **({} if e < 0 else {"accum_op": ALU.add}))
                dn["evd"][mj % 2] = p.cnt[f"f_acst{mj % 2}"]
            dn["h1"] = p.cnt["f_mm2"]
            dn["gbc_war"] = p.cnt["f_ev"]
        dn["accb"] = [("f_acst0", p.cnt.get("f_acst0", 0)), ("f_acst1", p.cnt.get("f_acst1", 0))]

    quant_matrix(sw1_d, 2048, D, wqA, lambda fr, sub: sub * 16 + fr)
    quant_matrix(sw2_d, D, 2048, wqB, lambda fr, sub: sub * 8 + fr)
    p.wait("sync", "f_gst")
    if S3_SHARED:
        ffn_pass(-1)
    for e in range(S3_NEXP):
        p.wait("sync", "f_mm")
        p.wait("sync", "f_mm2")
        quant_matrix(ew1_d[e], 2048, D, wqA, lambda fr, sub: sub * 16 + fr)
        quant_matrix(ew2_d[e], D, 2048, wqB, lambda fr, sub: sub * 8 + fr)
        ffn_pass(e)

    p.wait("sync", "f_acst0")
    p.wait("sync", "f_acst1")
    if not S3_G0:
        return
    dng = [0, 0]
    dng2 = [0, 0]
    for b in range(BPC):
        for pt_ in range(8):
            for ts_ in range(4):
                pr = ts_ % 2
                p.wait("sync", "g_sil", dng[pr])
                lv = p.dma("sync", gl[pr], acc0[b][pt_ * 128:(pt_ + 1) * 128,
                                                   ts_ * 512:(ts_ + 1) * 512], f"g_ld{pr}")
                p.wait("scalar", f"g_ld{pr}", lv)
                p.wait("scalar", f"g_st{pr}", dng2[pr])
                p.op("scalar", "activation", gq[pr], gl[pr], AF.Silu, sem="g_sil")
                dng[pr] = p.cnt["g_sil"]
                p.wait("gpsimd", "g_sil")
                p.dma("gpsimd", g0_d[b][pt_ * 128:(pt_ + 1) * 128, ts_ * 512:(ts_ + 1) * 512],
                      gq[pr], f"g_st{pr}")
                dng2[pr] = p.cnt[f"g_st{pr}"]


def stage4(ctx):
    p = ctx.p
    x_tm = ctx.d("x_tm", (BPC, T, D), F32, None, ())
    xn_f = ctx.d("xn_tm_f32", (BPC, T, D), F32, "s1", ("s4",))
    sproj_d = ctx.d("sproj_fm_bf", (BPC, D, T), BF16, "s2", ("s4",))
    g0_d = ctx.d("g0_bf", (BPC, D, T), BF16, "s3", ("s4",))
    x1_d = ctx.d("x1_tm", (BPC, T, D), F32, "s4", ("s8",))
    xn2b_d = ctx.d("xn2_tm_bf", (BPC, T, D), BF16, "s4", ("s4",))
    xn2fm_d = ctx.d("xn2_fm_bf", (BPC, D, T), BF16, "s4", ("s5", "s7"))
    wsel_d = ctx.d("wsel8", (8, NT), F32, "s4", ("s7",))

    NB = 3
    xt = [ctx.sbuf(f"s4_x{i}", (128, D), F32) for i in range(NB)]
    xnt = [ctx.sbuf(f"s4_xn{i}", (128, D), F32) for i in range(NB)]
    spt = [ctx.sbuf(f"s4_sp{i}", (128, D), BF16) for i in range(NB)]
    g0t = [ctx.sbuf(f"s4_g0{i}", (128, D), BF16) for i in range(NB)]
    sb_ = [ctx.sbuf(f"s4_s{i}", (128, D), F32) for i in range(NB)]
    sgt = [ctx.sbuf(f"s4_sg{i}", (128, D), F32) for i in range(NB)]
    x1t = [ctx.sbuf(f"s4_x1{i}", (128, D), F32) for i in range(NB)]
    bns = [ctx.sbuf(f"s4_bn{i}", (128, 12), F32) for i in range(NB)]
    sts = [ctx.sbuf(f"s4_st{i}", (128, 16), F32) for i in range(NB)]
    xn2t = [ctx.sbuf(f"s4_x2n{i}", (128, D), F32) for i in range(NB)]
    xn2b = [ctx.sbuf(f"s4_x2b{i}", (128, D), BF16) for i in range(NB)]
    s01 = ctx.sbuf("s4_s01", (128, 64), F32)
    hshA = ctx.sbuf("s4_hA", (128, 64), F32)
    hshB = ctx.sbuf("s4_hB", (128, 64), F32)
    hshC = ctx.sbuf("s4_hC", (128, 64), F32)
    hshD = ctx.sbuf("s4_hD", (128, 64), F32)
    wrow = ctx.sbuf("s4_wrow", (128, 64), F32)
    wrow2 = ctx.sbuf("s4_wrow2", (128, 64), F32)

    dx = [0] * NB
    dxn = [0] * NB
    dsp = [0] * NB
    dg0 = [0] * NB
    dst_ = [0] * NB
    dx1 = [0] * NB
    dx2b = [0] * NB
    ds01 = 0
    EPS = 1e-5

    for it in range(NT // 128):
        b, r = divmod(it, T // 128)
        pr = it % NB
        t0 = r * 128
        if it >= NB:
            p.wait("sync", "c_v", dx[pr])
        lx = p.dma("sync", xt[pr], x_tm[b, t0:t0 + 128, :], f"c_lx{pr}")
        ln = p.dma("sync", xnt[pr], xn_f[b, t0:t0 + 128, :], f"c_ln{pr}")
        lsp = p.dmat("sync", spt[pr], sproj_d[b][:, t0:t0 + 128], f"c_lsp{pr}")
        lg0 = p.dmat("sync", g0t[pr], g0_d[b][:, t0:t0 + 128], f"c_lg0{pr}")
        p.wait("vector", f"c_ln{pr}", ln)
        p.wait("vector", f"c_lsp{pr}", lsp)
        p.wait("vector", f"c_lx{pr}", lx)
        p.wait("vector", f"c_lg0{pr}", lg0)
        if it >= NB:
            p.wait("vector", f"c_sxb{pr}", dx2b[pr])
            p.wait("vector", f"c_sx1{pr}", dx1[pr])
        p.op("vector", "tensor_tensor", sb_[pr], xnt[pr], spt[pr], op=ALU.add, sem="c_v")
        p.wait("vector", "c_v")
        p.op("vector", "tensor_tensor", sgt[pr], sb_[pr], g0t[pr], op=ALU.mult, sem="c_v")
        p.wait("vector", "c_v")
        p.op("vector", "tensor_tensor", x1t[pr], xt[pr], sgt[pr], op=ALU.add, sem="c_v")
        dx[pr] = p.cnt["c_v"]
        p.wait("sync", "c_v")
        p.dma("sync", x1_d[b, t0:t0 + 128, :], x1t[pr], f"c_sx1{pr}")
        dx1[pr] = p.cnt[f"c_sx1{pr}"]

        bn, st = bns[pr], sts[pr]
        if it >= NB:
            p.wait("vector", "c_hx", dst_[pr])
        p.wait("vector", "c_v")
        p.op("vector", "bn_stats", bn[:, 0:6], x1t[pr][:, 0:512])
        p.op("vector", "bn_stats", bn[:, 6:12], x1t[pr][:, 512:1024], sem="c_v")
        p.wait("vector", "c_v")
        p.op("vector", "bn_aggr", st[:, 0:2], bn, sem="c_v")
        p.wait("vector", "c_v")
        p.op("vector", "tensor_scalar_add", st[:, 2:3], st[:, 1:2], EPS, sem="c_v")
        p.wait("scalar", "c_v")
        p.op("scalar", "activation", st[:, 3:4], st[:, 2:3], AF.Sqrt, sem="c_s")
        p.wait("vector", "c_s")
        p.op("vector", "reciprocal", st[:, 4:5], st[:, 3:4], sem="c_v")
        p.wait("vector", "c_v")
        p.op("vector", "tensor_scalar", st[:, 5:6], st[:, 0:1], st[:, 4:5], -1.0,
             op0=ALU.mult, op1=ALU.mult, sem="c_v")
        p.wait("scalar", "c_v")
        p.op("scalar", "activation", xn2t[pr], x1t[pr], AF.Identity,
             scale=st[:, 4:5], bias=st[:, 5:6], sem="c_hx")
        dst_[pr] = p.cnt["c_hx"]
        p.wait("vector", "c_hx")
        p.op("vector", "tensor_tensor", s01[:, it % 64:it % 64 + 1] if False else s01[:, it:it + 1] if it < 64 else None,
             xn2t[pr][:, 0:1], xn2t[pr][:, 1:2], op=ALU.add, sem="c_s01")
        p.op("vector", "tensor_copy", xn2b[pr], xn2t[pr], sem="s4xb")
        p.wait("gpsimd", "s4xb")
        p.dma("gpsimd", xn2b_d[b, t0:t0 + 128, :], xn2b[pr], f"c_sxb{pr}")
        dx2b[pr] = p.cnt[f"c_sxb{pr}"]

    p.wait("vector", "c_s01")
    def fence(name="h_v"):
        p.wait("vector", name)
    p.op("vector", "tensor_scalar_mul", hshA, s01, 0.125, sem="h_v"); fence()
    p.op("vector", "tensor_scalar", hshB, hshA, 1.0, None, op0=ALU.is_ge, sem="h_v"); fence()
    for k in range(2, 8):
        p.op("vector", "tensor_scalar", hshC, hshA, float(k), None, op0=ALU.is_ge, sem="h_v"); fence()
        p.op("vector", "tensor_tensor", hshD, hshB, hshC, op=ALU.add, sem="h_v"); fence()
        hshB, hshD = hshD, hshB
    for k in range(0, 8):
        p.op("vector", "tensor_scalar", hshC, hshA, float(-k), None, op0=ALU.is_lt, sem="h_v"); fence()
        p.op("vector", "tensor_tensor", hshD, hshB, hshC, op=ALU.subtract, sem="h_v"); fence()
        hshB, hshD = hshD, hshB
    p.op("vector", "scalar_tensor_tensor", hshA, hshB, -8.0, s01,
         op0=ALU.mult, op1=ALU.add, sem="h_v"); fence()
    p.op("vector", "tensor_scalar", hshB, hshA, 1.0, None, op0=ALU.is_ge, sem="h_v"); fence()
    for k in range(2, 8):
        p.op("vector", "tensor_scalar", hshC, hshA, float(k), None, op0=ALU.is_ge, sem="h_v"); fence()
        p.op("vector", "tensor_tensor", hshD, hshB, hshC, op=ALU.add, sem="h_v"); fence()
        hshB, hshD = hshD, hshB
    p.op("vector", "tensor_scalar", hshC, s01, 0.0, None, op0=ALU.is_lt, sem="h_v"); fence()
    p.op("vector", "tensor_tensor", hshA, hshB, hshC, op=ALU.add, sem="h_v"); fence()
    p.op("vector", "tensor_scalar", hshC, hshA, 8.0, None, op0=ALU.is_ge, sem="h_v"); fence()
    p.op("vector", "scalar_tensor_tensor", hshB, hshC, -8.0, hshA,
         op0=ALU.mult, op1=ALU.add, sem="h_v"); fence()
    dws = 0
    for e in range(8):
        p.op("vector", "tensor_scalar", hshC, hshB, float(e), None, op0=ALU.is_equal, sem="h_v"); fence()
        p.op("vector", "tensor_scalar", hshD, hshB, float((e + 7) % 8), None,
             op0=ALU.is_equal, sem="h_v"); fence()
        p.op("vector", "tensor_tensor", wrow, hshC, hshD, op=ALU.add, sem="h_v"); fence()
        p.wait("vector", "h_ws", dws)
        p.op("vector", "tensor_scalar_mul", wrow2, wrow, 0.5, sem="h_v"); fence()
        p.wait("gpsimd", "h_v")
        p.dma("gpsimd", wsel_d[e].rearrange("(tt q) -> q tt", q=128), wrow2, "h_ws",
              allow_slow_non_contiguous=True)
        dws = p.cnt["h_ws"]

    trb = [ctx.sbuf(f"s4_tr{i}", (128, T), BF16) for i in range(2)]
    done_tr = [0, 0]
    for nm in [f"c_sxb{i}" for i in range(NB)]:
        p.wait("sync", nm)
    i = 0
    for b in range(BPC):
        for ki in range(8):
            pr = i % 2
            if i >= 2:
                p.wait("sync", f"c_trs{pr}", done_tr[pr])
            ldv = p.dmat("sync", trb[pr], xn2b_d[b, :, ki * 128:(ki + 1) * 128], f"c_trl{pr}")
            p.wait("scalar", f"c_trl{pr}", ldv)
            p.dma("scalar", xn2fm_d[b, ki * 128:(ki + 1) * 128, :], trb[pr], f"c_trs{pr}")
            done_tr[pr] = p.cnt[f"c_trs{pr}"]
            i += 1


def stage5(ctx):
    p = ctx.p
    xn2fm_d = ctx.d("xn2_fm_bf", (BPC, D, T), BF16, "s4", ("s5", "s7"))
    q_d = ctx.d("q_fm", (BPC, D, T), BF16, "s5", ("s6",))
    k_d = ctx.d("k_fm", (BPC, 256, T), BF16, "s5", ("s6",))
    v_d = ctx.d("v1_tm", (BPC, T, 256), BF16, "s5", ("s6",))
    wq_d = ctx.d("wq_T", (D, D), F32, None, ())
    wk_d = ctx.d("wk_T", (D, 256), F32, None, ())
    wv_d = ctx.d("wv_T", (D, 256), F32, None, ())

    wq = ctx.sbuf("s5_wq", (128, 8 * 1024), BF16)
    wk = ctx.sbuf("s5_wk", (128, 8 * 256), BF16)
    wv = ctx.sbuf("s5_wv", (128, 8 * 256), BF16)
    p.dma("gpsimd", wq.rearrange("q (ki c) -> q ki c", ki=8),
          wq_d.rearrange("(ki q) c -> q ki c", q=128), "a_wl")
    p.dma("gpsimd", wk.rearrange("q (ki c) -> q ki c", ki=8),
          wk_d.rearrange("(ki q) c -> q ki c", q=128), "a_wl")
    p.dma("gpsimd", wv.rearrange("q (ki c) -> q ki c", ki=8),
          wv_d.rearrange("(ki q) c -> q ki c", q=128), "a_wl")

    pA = ctx.psum("pA", (128, 512), F32)
    pB = ctx.psum("pB", (128, 512), F32)
    hbuf = [ctx.sbuf(f"s3_h{i}", (128, 8 * 512), BF16) for i in range(2)]
    evq = [ctx.sbuf(f"s5_ev{i}", (128, 512), BF16) for i in range(2)]

    p.wait("tensor", "a_wl")
    dhb = [0, 0]
    dev = [0, 0]
    devd = [0, 0]
    for gtt in range(16):
        b, ts_ = divmod(gtt, 4)
        pr = gtt % 2
        p.wait("sync", "a_mm", dhb[pr])
        ldv = p.dma("sync", hbuf[pr].rearrange("q (ki t) -> q ki t", ki=8),
                    xn2fm_d[b].rearrange("(ki q) t -> q ki t", q=128)
                    [:, :, ts_ * 512:(ts_ + 1) * 512], f"a_hl{pr}")
        p.wait("tensor", f"a_hl{pr}", ldv)
        jobs = []
        for mj in range(8):
            jobs.append(("q", mj, q_d[b][mj * 128:(mj + 1) * 128, ts_ * 512:(ts_ + 1) * 512],
                         [wq[:, ki * 1024 + mj * 128: ki * 1024 + mj * 128 + 128] for ki in range(8)]))
        for mj in range(2):
            jobs.append(("k", mj, k_d[b][mj * 128:(mj + 1) * 128, ts_ * 512:(ts_ + 1) * 512],
                         [wk[:, ki * 256 + mj * 128: ki * 256 + mj * 128 + 128] for ki in range(8)]))
        for ji, (kind, mj, dram_out, wblocks) in enumerate(jobs):
            pp = [pA, pB][ji % 2]
            p.wait("tensor", "a_ev", dev[ji % 2])
            for ki in range(8):
                p.op("tensor", "matmul", pp, wblocks[ki],
                     hbuf[pr][:, ki * 512:(ki + 1) * 512], start=(ki == 0), stop=(ki == 7),
                     sem="a_mm" if ki == 7 else None)
            p.wait("scalar", "a_mm")
            p.wait("scalar", f"a_st{ji % 2}", devd[ji % 2])
            p.op("scalar", "copy", evq[ji % 2], pp, sem="a_ev")
            dev[ji % 2] = p.cnt["a_ev"]
            p.wait("gpsimd", "a_ev")
            p.dma("gpsimd", dram_out, evq[ji % 2], f"a_st{ji % 2}")
            devd[ji % 2] = p.cnt[f"a_st{ji % 2}"]
        for sub in range(4):
            ji = 10 + sub
            pp = [pA, pB][ji % 2]
            p.wait("tensor", "a_ev", dev[ji % 2])
            for ki in range(8):
                p.op("tensor", "matmul", pp[:, 0:256],
                     hbuf[pr][:, ki * 512 + sub * 128: ki * 512 + sub * 128 + 128],
                     wv[:, ki * 256:(ki + 1) * 256], start=(ki == 0), stop=(ki == 7),
                     sem="a_mm" if ki == 7 else None)
            p.wait("scalar", "a_mm")
            p.wait("scalar", f"a_st{ji % 2}", devd[ji % 2])
            p.op("scalar", "copy", evq[ji % 2][:, 0:256], pp[:, 0:256], sem="a_ev")
            dev[ji % 2] = p.cnt["a_ev"]
            p.wait("gpsimd", "a_ev")
            p.dma("gpsimd", v_d[b][ts_ * 512 + sub * 128: ts_ * 512 + sub * 128 + 128, :],
                  evq[ji % 2][:, 0:256], f"a_st{ji % 2}")
            devd[ji % 2] = p.cnt[f"a_st{ji % 2}"]
        dhb[pr] = p.cnt["a_mm"]


def stage6(ctx):
    p = ctx.p
    q_d = ctx.d("q_fm", (BPC, D, T), BF16, "s5", ("s6",))
    k_d = ctx.d("k_fm", (BPC, 256, T), BF16, "s5", ("s6",))
    v_d = ctx.d("v1_tm", (BPC, T, 256), BF16, "s5", ("s6",))
    o_d = ctx.d("o1_fm", (BPC, D, T), BF16, "s6", ("s6",))
    s2_d = ctx.d("s2_fm_bf", (BPC, D, T), BF16, "s6", ("s8",))
    am_d = ctx.d("amask", (4, 128, 512), F32, None, ())
    wo_d = ctx.d("wo_T", (D, D), F32, None, ())
    identb = load_const(ctx, "ident_bf", (128, 128), BF16, sem="at_c")

    amask = ctx.sbuf("s6_am", (128, 4 * 512), F32)
    p.dma("sync", amask.rearrange("q (m c) -> q m c", m=4),
          am_d.rearrange("m q c -> q m c"), "at_c")
    wo = ctx.sbuf("s5_wq", (128, 8 * 1024), BF16)
    p.dma("gpsimd", wo.rearrange("q (ki c) -> q ki c", ki=8),
          wo_d.rearrange("(ki q) c -> q ki c", q=128), "at_cg")

    pA = ctx.psum("pA", (128, 512), F32)
    pB = ctx.psum("pB", (128, 512), F32)
    pT = ctx.psum("pT2", (128, 512), BF16)
    pT3 = ctx.psum("pT3", (128, 512), BF16)
    pD = ctx.psum("pD", (128, 512), F32)

    qh = ctx.sbuf("s6_qh", (64, T), BF16)
    kg = ctx.sbuf("s6_kg", (64, T), BF16)
    vb = ctx.sbuf("s6_vb", (128, 16 * 64), BF16)
    Ssb = ctx.sbuf("s6_S", (128, T), F32)
    Psb = ctx.sbuf("s6_P", (128, T), BF16)
    P2sb = ctx.sbuf("s6_P2", (128, T), BF16)
    Pt = [ctx.sbuf(f"s6_Pt{i}", (128, 128), BF16) for i in range(2)]
    ob = ctx.sbuf("s6_ob", (64, T), BF16)
    sst = ctx.sbuf("s6_sst", (128, 8), F32)
    hbuf = [ctx.sbuf(f"s3_h{i}", (128, 8 * 512), BF16) for i in range(2)]
    evq = [ctx.sbuf(f"s5_ev{i}", (128, 512), BF16) for i in range(2)]

    p.wait("tensor", "at_c")
    p.wait("tensor", "at_cg")
    dsc = [0, 0]
    dptev = [0, 0]
    dob = 0
    dqh = 0
    for b in range(BPC):
        for h in range(16):
            g = h // 4
            p.wait("sync", "at_mm", dqh)
            p.wait("sync", "at_obst", p.cnt.get("at_obst", 0))
            l1 = p.dma("sync", qh, q_d[b][h * 64:(h + 1) * 64, :], "at_l")
            l2 = p.dma("sync", kg, k_d[b][g * 64:(g + 1) * 64, :], "at_l")
            l3 = p.dma("sync", vb.rearrange("q (tt e) -> q tt e", tt=16),
                       v_d[b].rearrange("(tt q) j -> q tt j", q=128)
                       [:, :, g * 64:(g + 1) * 64], "at_l")
            p.wait("tensor", "at_l", l3)
            for qt in range(16):
                nk5 = qt // 4 + 1
                L = nk5 * 512
                for kt in range(nk5):
                    pp = [pA, pB][kt % 2]
                    p.wait("tensor", "at_sev", dsc[kt % 2])
                    p.op("tensor", "matmul", pp, qh[:, qt * 128:(qt + 1) * 128],
                         kg[:, kt * 512:(kt + 1) * 512], start=True, stop=True, sem="at_mm")
                    p.wait("vector", "at_mm")
                    if kt == qt // 4:
                        p.op("vector", "scalar_tensor_tensor", Ssb[:, kt * 512:(kt + 1) * 512],
                             pp, 0.125, amask[:, (qt % 4) * 512:(qt % 4) * 512 + 512],
                             op0=ALU.mult, op1=ALU.add, sem="at_sev")
                    else:
                        p.op("vector", "tensor_scalar_mul", Ssb[:, kt * 512:(kt + 1) * 512],
                             pp, 0.125, sem="at_sev")
                    dsc[kt % 2] = p.cnt["at_sev"]
                p.wait("vector", "at_sev")
                p.op("vector", "reduce_max", sst[:, 0:1], Ssb[:, 0:L], AX.X,
                     negate=True, sem="at_v")
                p.wait("scalar", "at_v")
                p.op("scalar", "activation", Psb[:, 0:L], Ssb[:, 0:L], AF.Exp,
                     bias=sst[:, 0:1], accum_out=sst[:, 1:2], sem="at_s")
                p.wait("vector", "at_s")
                p.op("vector", "reciprocal", sst[:, 2:3], sst[:, 1:2], sem="at_v")
                p.wait("vector", "at_v")
                p.op("vector", "tensor_scalar", P2sb[:, 0:L], Psb[:, 0:L], sst[:, 2:3],
                     None, op0=ALU.mult, sem="at_p2")
                nkt = qt + 1
                p.wait("tensor", "at_p2")
                for ktt in range(nkt):
                    pr2 = ktt % 2
                    ptk = [pT, pT3][pr2][:, 0:128]
                    p.wait("tensor", "at_ptev", p.cnt.get("at_ptev", 0))
                    p.op("tensor", "transpose", ptk, P2sb[:, ktt * 128:(ktt + 1) * 128],
                         identb, sem="at_tr")
                    p.wait("scalar", "at_tr")
                    p.op("scalar", "copy", Pt[pr2], ptk, sem="at_ptev")
                    p.wait("tensor", "at_ptev")
                    p.op("tensor", "matmul", pD[0:64, 0:128], vb[:, ktt * 64:(ktt + 1) * 64],
                         Pt[pr2], start=(ktt == 0), stop=(ktt == nkt - 1),
                         sem="at_mmo" if ktt == nkt - 1 else None)
                p.wait("scalar", "at_mmo")
                p.op("scalar", "copy", ob[:, qt * 128:(qt + 1) * 128], pD[0:64, 0:128],
                     sem="at_oev")
            dqh = p.cnt["at_mm"]
            p.wait("gpsimd", "at_oev")
            p.dma("gpsimd", o_d[b][h * 64:(h + 1) * 64, :], ob, "at_obst")

    p.wait("sync", "at_obst")
    dhb = [0, 0]
    dev = [0, 0]
    devd = [0, 0]
    for gtt in range(16):
        b, ts_ = divmod(gtt, 4)
        pr = gtt % 2
        p.wait("sync", "at_mm", dhb[pr])
        ldv = p.dma("sync", hbuf[pr].rearrange("q (ki t) -> q ki t", ki=8),
                    o_d[b].rearrange("(ki q) t -> q ki t", q=128)
                    [:, :, ts_ * 512:(ts_ + 1) * 512], f"at_ol{pr}")
        p.wait("tensor", f"at_ol{pr}", ldv)
        for mj in range(8):
            pp = [pA, pB][mj % 2]
            p.wait("tensor", "at_ev2", dev[mj % 2])
            p.wait("tensor", "at_sev")
            for ki in range(8):
                p.op("tensor", "matmul", pp, wo[:, ki * 1024 + mj * 128: ki * 1024 + mj * 128 + 128],
                     hbuf[pr][:, ki * 512:(ki + 1) * 512], start=(ki == 0), stop=(ki == 7),
                     sem="at_mm" if ki == 7 else None)
            p.wait("scalar", "at_mm")
            p.wait("scalar", f"at_s2st{mj % 2}", devd[mj % 2])
            p.op("scalar", "copy", evq[mj % 2], pp, sem="at_ev2")
            dev[mj % 2] = p.cnt["at_ev2"]
            p.wait("gpsimd", "at_ev2")
            p.dma("gpsimd", s2_d[b][mj * 128:(mj + 1) * 128, ts_ * 512:(ts_ + 1) * 512],
                  evq[mj % 2], f"at_s2st{mj % 2}")
            devd[mj % 2] = p.cnt[f"at_s2st{mj % 2}"]
        dhb[pr] = p.cnt["at_mm"]


def stage7(ctx):
    p = ctx.p
    xn2fm_d = ctx.d("xn2_fm_bf", (BPC, D, T), BF16, "s4", ("s5", "s7"))
    acc1 = ctx.d("acc1_fm", (BPC, D, T), F32, "s7", ("s7",))
    g1_d = ctx.d("g1_bf", (BPC, D, T), BF16, "s7", ("s8",))
    wsel_d = ctx.d("wsel8", (8, NT), F32, "s4", ("s7",))
    W1_d = ctx.d("W1", (8, D, 4096), F32, None, ())
    W2_d = ctx.d("W2", (8, 4096, D), F32, None, ())
    b1_d = ctx.d("b1c", (8, 128, 32), F32, None, ())
    b2_d = ctx.d("b2c", (8, 128, 8), F32, None, ())
    ones_row = load_const(ctx, "ones_row", (1, 128), sem="l_c")

    pA = ctx.psum("pA", (128, 512), F32)
    pB = ctx.psum("pB", (128, 512), F32)
    pC = ctx.psum("pC", (128, 512), F32)
    hbuf = [ctx.sbuf(f"s3_h{i}", (128, 8 * 512), BF16) for i in range(2)]
    h1b = ctx.sbuf("s3_h1big", (128, 16 * 512), BF16)
    wqA = ctx.sbuf("s3_wqA", (128, 128 * 128), BF16)
    wqB = ctx.sbuf("s3_wqB", (128, 128 * 128), BF16)
    b1sb = ctx.sbuf("s7_b1", (128, 32), F32)
    b2sb = ctx.sbuf("s7_b2", (128, 8), F32)
    grow = ctx.sbuf("s3_grow", (1, 512), F32)
    gbc = ctx.sbuf("s3_gbc", (128, 512), F32)
    ev = [ctx.sbuf(f"s3_ev{i}", (128, 512), F32) for i in range(2)]
    gl = [ctx.sbuf(f"s3_gl{i}", (128, 512), F32) for i in range(2)]
    gq = [ctx.sbuf(f"s3_gq{i}", (128, 512), BF16) for i in range(2)]

    dn = {"hb": [0, 0], "ff": [0, 0], "ev": [0, 0], "evd": [0, 0],
          "gbc": 0, "gbc_war": 0, "h1": 0, "gmm": 0, "accb": [], "wl": 0, "bl": 0}

    p.wait("tensor", "l_c")
    for e in range(8):
        for ha in range(2):
            p.wait("gpsimd", "l_mm", dn["wl"])
            p.wait("gpsimd", "l_mm2", dn["wl2"] if "wl2" in dn else 0)
            w1v = p.dma("gpsimd", wqA.rearrange("q (ki f) -> q ki f", ki=8),
                        W1_d[e].rearrange("(ki q) f -> q ki f", q=128)
                        [:, :, ha * 2048:(ha + 1) * 2048], "l_w1")
            w2v = p.dma("gpsimd", wqB.rearrange("q (fk m) -> q fk m", fk=16),
                        W2_d[e][ha * 2048:(ha + 1) * 2048, :]
                        .rearrange("(fk q) m -> q fk m", q=128), "l_w2")
            if ha == 0:
                p.wait("sync", "l_h1ev", dn["bl"])
                p.wait("sync", "l_ev0", dn["bl2"] if "bl2" in dn else 0)
                b1v = p.dma("sync", b1sb, b1_d[e], "l_b1")
                b2v = p.dma("sync", b2sb, b2_d[e], "l_b1")
            p.wait("tensor", "l_w1", w1v)
            p.wait("tensor", "l_w2", w2v)
            p.wait("scalar", "l_b1")
            for gtt in range(16):
                b, ts_ = divmod(gtt, 4)
                pr = gtt % 2
                p.wait("sync", "l_mm", dn["hb"][pr])
                ldv = p.dma("sync", hbuf[pr].rearrange("q (ki t) -> q ki t", ki=8),
                            xn2fm_d[b].rearrange("(ki q) t -> q ki t", q=128)
                            [:, :, ts_ * 512:(ts_ + 1) * 512], f"l_hl{pr}")
                if True:
                    p.wait("sync", "l_mmg", dn["gmm"])
                    glv = p.dma("sync", grow, wsel_d[e:e + 1, gtt * 512:(gtt + 1) * 512],
                                "l_gl")
                    p.wait("tensor", "l_gl", glv)
                    p.wait("tensor", "l_gbc", dn["gbc"])
                    p.op("tensor", "matmul", pC[:, 0:512], ones_row[0:1, 0:128],
                         grow, start=True, stop=True, sem="l_mmg")
                    dn["gmm"] = p.cnt["l_mmg"]
                    p.wait("scalar", "l_mmg")
                    p.wait("scalar", "l_ev", dn["gbc_war"])
                    p.op("scalar", "copy", gbc, pC[:, 0:512], sem="l_gbc")
                    dn["gbc"] = p.cnt["l_gbc"]
                p.wait("tensor", f"l_hl{pr}", ldv)
                p.wait("scalar", "l_mm2", dn["h1"])
                for fj in range(16):
                    pp = [pA, pB][fj % 2]
                    p.wait("tensor", "l_h1ev", dn["ff"][fj % 2])
                    p.wait("tensor", "l_ev", dn["ev"][fj % 2])
                    for ki in range(8):
                        p.op("tensor", "matmul", pp,
                             wqA[:, (ki * 16 + fj) * 128:(ki * 16 + fj) * 128 + 128],
                             hbuf[pr][:, ki * 512:(ki + 1) * 512],
                             start=(ki == 0), stop=(ki == 7),
                             sem="l_mm" if ki == 7 else None)
                    p.wait("scalar", "l_mm")
                    p.op("scalar", "activation", h1b[:, fj * 512:(fj + 1) * 512], pp,
                         AF.Relu, bias=b1sb[:, ha * 16 + fj: ha * 16 + fj + 1],
                         sem="l_h1ev")
                    dn["ff"][fj % 2] = p.cnt["l_h1ev"]
                dn["hb"][pr] = p.cnt["l_mm"]
                p.wait("tensor", "l_h1ev")
                for mj in range(8):
                    pp = [pA, pB][mj % 2]
                    p.wait("tensor", "l_ev", dn["ev"][mj % 2])
                    for fk in range(16):
                        p.op("tensor", "matmul", pp,
                             wqB[:, (fk * 8 + mj) * 128:(fk * 8 + mj) * 128 + 128],
                             h1b[:, fk * 512:(fk + 1) * 512],
                             start=(fk == 0), stop=(fk == 15),
                             sem="l_mm2" if fk == 15 else None)
                    evb = ev[mj % 2]
                    glb = gl[mj % 2]
                    p.wait("scalar", "l_mm2")
                    p.wait("scalar", f"l_acst{mj % 2}", dn["evd"][mj % 2])
                    if ha == 1:
                        p.op("scalar", "activation", glb, pp, AF.Identity,
                             bias=b2sb[:, mj:mj + 1], sem="l_ev0")
                    else:
                        p.op("scalar", "copy", glb, pp, sem="l_ev0")
                    p.wait("vector", "l_ev0")
                    p.wait("vector", "l_gbc", dn["gbc"])
                    p.wait("vector", f"l_acst{mj % 2}", dn["evd"][mj % 2])
                    p.op("vector", "tensor_tensor", evb, glb, gbc, op=ALU.mult, sem="l_ev")
                    dn["ev"][mj % 2] = p.cnt["l_ev"]
                    p.wait("gpsimd", "l_ev")
                    if dn["accb"]:
                        for nm, vv in dn["accb"]:
                            p.wait("gpsimd", nm, vv)
                        dn["accb"] = []
                    first = (e == 0 and ha == 0)
                    p.dma("gpsimd", acc1[b][mj * 128:(mj + 1) * 128, ts_ * 512:(ts_ + 1) * 512],
                          evb, f"l_acst{mj % 2}",
                          **({} if first else {"accum_op": ALU.add}))
                    dn["evd"][mj % 2] = p.cnt[f"l_acst{mj % 2}"]
                dn["h1"] = p.cnt["l_mm2"]
                dn["gbc_war"] = p.cnt["l_ev"]
            dn["wl"] = p.cnt["l_mm"]
            dn["wl2"] = p.cnt["l_mm2"]
            if ha == 1:
                dn["bl"] = p.cnt["l_h1ev"]
                dn["bl2"] = p.cnt["l_ev0"]
            dn["accb"] = [("l_acst0", p.cnt.get("l_acst0", 0)),
                          ("l_acst1", p.cnt.get("l_acst1", 0))]

    p.wait("sync", "l_acst0")
    p.wait("sync", "l_acst1")
    dng = [0, 0]
    dng2 = [0, 0]
    for b in range(BPC):
        for pt_ in range(8):
            for ts_ in range(4):
                pr = ts_ % 2
                p.wait("sync", "l_sil", dng[pr])
                lv = p.dma("sync", gl[pr], acc1[b][pt_ * 128:(pt_ + 1) * 128,
                                                   ts_ * 512:(ts_ + 1) * 512], f"l_gld{pr}")
                p.wait("scalar", f"l_gld{pr}", lv)
                p.wait("scalar", f"l_gst{pr}", dng2[pr])
                p.op("scalar", "activation", gq[pr], gl[pr], AF.Silu, sem="l_sil")
                dng[pr] = p.cnt["l_sil"]
                p.wait("gpsimd", "l_sil")
                p.dma("gpsimd", g1_d[b][pt_ * 128:(pt_ + 1) * 128, ts_ * 512:(ts_ + 1) * 512],
                      gq[pr], f"l_gst{pr}")
                dng2[pr] = p.cnt[f"l_gst{pr}"]


def stage8(ctx):
    p = ctx.p
    x1_d = ctx.d("x1_tm", (BPC, T, D), F32, "s4", ("s8",))
    s2_d = ctx.d("s2_fm_bf", (BPC, D, T), BF16, "s6", ("s8",))
    g1_d = ctx.d("g1_bf", (BPC, D, T), BF16, "s7", ("s8",))
    out_d = ctx.d("out_tm", (BPC, T, D), F32, "s8", ())

    NB = 3
    x1t = [ctx.sbuf(f"s8_x1{i}", (128, D), F32) for i in range(NB)]
    s2t = [ctx.sbuf(f"s8_s2{i}", (128, D), BF16) for i in range(NB)]
    g1t = [ctx.sbuf(f"s8_g1{i}", (128, D), BF16) for i in range(NB)]
    sgt = [ctx.sbuf(f"s8_sg{i}", (128, D), F32) for i in range(NB)]
    x2t = [ctx.sbuf(f"s8_x2{i}", (128, D), F32) for i in range(NB)]

    dv = [0] * NB
    dst8 = [0] * NB
    for it in range(NT // 128):
        b, r = divmod(it, T // 128)
        pr = it % NB
        t0 = r * 128
        if it >= NB:
            p.wait("sync", "o_v", dv[pr])
        lx = p.dma("sync", x1t[pr], x1_d[b, t0:t0 + 128, :], f"o_lx{pr}")
        ls = p.dmat("sync", s2t[pr], s2_d[b][:, t0:t0 + 128], f"o_ls{pr}")
        lg = p.dmat("sync", g1t[pr], g1_d[b][:, t0:t0 + 128], f"o_lg{pr}")
        p.wait("vector", f"o_lx{pr}", lx)
        p.wait("vector", f"o_ls{pr}", ls)
        p.wait("vector", f"o_lg{pr}", lg)
        if it >= NB:
            p.wait("vector", f"o_st{pr}", dst8[pr])
        p.op("vector", "tensor_tensor", sgt[pr], s2t[pr], g1t[pr], op=ALU.mult, sem="o_v")
        p.wait("vector", "o_v")
        p.op("vector", "tensor_tensor", x2t[pr], x1t[pr], sgt[pr], op=ALU.add, sem="o_v")
        dv[pr] = p.cnt["o_v"]
        p.wait("sync", "o_v")
        p.dma("sync", out_d[b, t0:t0 + 128, :], x2t[pr], f"o_st{pr}")
        dst8[pr] = p.cnt[f"o_st{pr}"]


STAGE_FNS = {"s1": stage1, "s2": stage2, "s3": stage3, "s4": stage4,
             "s5": stage5, "s6": stage6, "s7": stage7, "s8": stage8}


def build_launch(stages, core_input_names):
    ctx = Ctx(stages, core_input_names)
    with ctx.stack:
        dram_common(ctx)
        for s in ALL_STAGES:
            if s in ctx.stages:
                STAGE_FNS[s](ctx)
        check_deadlock(ctx.p)
        ctx.finalize()
    return ctx


LAUNCH_PLAN = [(s,) for s in ALL_STAGES]


def kernel(**inputs):
    in_maps = host_prepare(inputs)
    core_input_names = set(in_maps[0].keys())
    state = [dict(m) for m in in_maps]
    for stages in LAUNCH_PLAN:
        ctx = build_launch(stages, core_input_names)
        launch_maps = [{k: st[k] for k in ctx.ext_in} for st in state]
        res = run_bass_kernel_spmd(ctx.nc, launch_maps, core_ids=list(range(NCORES)))
        for c in range(NCORES):
            for name in ctx.ext_out:
                state[c][name] = np.asarray(res.results[c][name])
    out = np.concatenate([state[c]["out_tm"] for c in range(NCORES)], axis=0)
    return out.astype(np.float32)


# version 02
# speedup vs baseline: 1.0299x; 1.0299x over previous
import contextlib
import math
import numpy as np
import ml_dtypes
import concourse.bass as bass
import concourse.mybir as mybir
from concourse.bass_utils import run_bass_kernel_spmd

F32 = mybir.dt.float32
BF16 = mybir.dt.bfloat16
AF = mybir.ActivationFunctionType
ALU = mybir.AluOpType
AX = mybir.AxisListType
BF = ml_dtypes.bfloat16

NCORES = 8
BPC = 4
T = 2048
D = 1024
NT = BPC * T
H, HD, CK = 16, 64, 64
NCHUNK = T // CK
NEG = -1.0e30

ENGINES = ("sync", "scalar", "vector", "tensor", "gpsimd")

ALL_STAGES = ("s1", "s2", "s3", "s4", "s5", "s6", "s7", "s8")
DEBUG_NAMES = set()
DEV_DUMP = False
BISECT_EVAC = False
BISECT_OMM = False
BISECT_POJ = False
BISECT_TS = False
BISECT_NOACT = False
S2_NB = 4
S2_HP = 8
S2_DO_P3 = True
S2_PA = True
S2_PB = True
S2_OG = True
S2_PB_S = True
S2_PB_O = True
S2_PB_U = True
S2_PB_EV = True
S3_ROUTER = True
S3_SHARED = True
S3_NEXP = 8
S3_G0 = True
S3_QL = 3
S2_DO_S2B = True


class P:

    def __init__(self, nc, stack):
        self.nc = nc
        self.stack = stack
        self.plan = {e: [] for e in ENGINES}
        self.meta = {e: [] for e in ENGINES}
        self.handles = []
        self.hcnt = []
        self.epoch = 0
        self.name2h = {}
        self.pool_ptr = 0
        self.sem_queues = {}
        self.dma_handles = {}
        self._barn = 0

    def _hid(self, name):
        key = (self.epoch, name)
        if key not in self.name2h:
            if self.pool_ptr >= len(self.handles):
                h = self.stack.enter_context(self.nc.semaphore(f"s{len(self.handles)}"))
                self.handles.append(h)
                self.hcnt.append(0)
            self.name2h[key] = self.pool_ptr
            self.pool_ptr += 1
        return self.name2h[key]

    class _CntView:
        def __init__(self, p):
            self.p = p

        def __getitem__(self, name):
            return self.p.hcnt[self.p._hid(name)]

        def get(self, name, default=0):
            return self.p.hcnt[self.p._hid(name)]

        def __setitem__(self, name, v):
            self.p.hcnt[self.p._hid(name)] = v

        def __contains__(self, name):
            return (self.p.epoch, name) in self.p.name2h

    @property
    def cnt(self):
        return P._CntView(self)

    def sem(self, name):
        return self.handles[self._hid(name)]

    def wait(self, eng, name, val=None):
        hid = self._hid(name)
        h = self.handles[hid]
        v = self.hcnt[hid] if val is None else val
        self.plan[eng].append(lambda e, h=h, v=v: e.wait_ge(h, v))
        self.meta[eng].append(("wait", hid, v))
        return v

    def dma(self, q, out, in_, sem, **kw):
        hid = self._hid(sem)
        h = self.handles[hid]
        self.hcnt[hid] += 16
        qc = "sw" if q == "gpsimd" else "hw"
        key = (self.epoch, sem)
        self.sem_queues.setdefault(key, set()).add(q)
        assert all(("sw" if x == "gpsimd" else "hw") == qc for x in self.sem_queues[key]), \
            f"sem {sem} mixes SWDGE and HWDGE queues"
        self.dma_handles.setdefault(hid, set()).add(q)
        def f(e, out=out, in_=in_, h=h, kw=kw):
            e.dma_start(out=out, in_=in_, **kw).then_inc(h, 16)
        self.plan[q].append(f)
        self.meta[q].append(("inc", hid, 16))
        return self.hcnt[hid]

    def dmat(self, q, out, in_, sem):
        hid = self._hid(sem)
        h = self.handles[hid]
        self.hcnt[hid] += 16
        key = (self.epoch, sem)
        self.sem_queues.setdefault(key, set()).add(q)
        self.dma_handles.setdefault(hid, set()).add(q)
        def f(e, out=out, in_=in_, h=h):
            e.dma_start_transpose(out, in_).then_inc(h, 16)
        self.plan[q].append(f)
        self.meta[q].append(("inc", hid, 16))
        return self.hcnt[hid]

    def op(self, eng, meth, *args, sem=None, **kw):
        h = None
        if sem is not None:
            hid = self._hid(sem)
            h = self.handles[hid]
            self.hcnt[hid] += 1
        def f(e, meth=meth, args=args, kw=kw, h=h):
            ins = getattr(e, meth)(*args, **kw)
            if h is not None:
                ins.then_inc(h, 1)
        self.plan[eng].append(f)
        if sem is not None:
            self.meta[eng].append(("inc", hid, 1))
            return self.hcnt[hid]
        self.meta[eng].append(("op", -1, 0))
        return None

    def barrier(self):
        for (ep, name), queues in list(self.sem_queues.items()):
            if ep != self.epoch:
                continue
            for q in queues:
                self.wait(q, name)
        self._barn += 1
        hid = self._hid(f"__bar{self._barn}")
        h = self.handles[hid]
        for e in ENGINES:
            self.hcnt[hid] += 1
            def f(en, h=h):
                en.sem_inc(h, 1)
            self.plan[e].append(f)
            self.meta[e].append(("inc", hid, 1))
        tgt = self.hcnt[hid]
        for e in ENGINES:
            self.plan[e].append(lambda en, h=h, v=tgt: en.wait_ge(h, v))
            self.meta[e].append(("wait", hid, tgt))
        self.epoch += 1
        self.pool_ptr = 0

    def final_epilogue(self):
        for hid, queues in self.dma_handles.items():
            h = self.handles[hid]
            v = self.hcnt[hid]
            for q in queues:
                self.plan[q].append(lambda e, h=h, v=v: e.wait_ge(h, v))
                self.meta[q].append(("wait", hid, v))


def check_deadlock(p):
    pos = {e: 0 for e in ENGINES}
    sems = {}
    progress = True
    while progress:
        progress = False
        for e in ENGINES:
            m = p.meta[e]
            while pos[e] < len(m):
                kind, a, b = m[pos[e]]
                if kind == "wait" and sems.get(a, 0) < b:
                    break
                if kind == "inc":
                    sems[a] = sems.get(a, 0) + b
                pos[e] += 1
                progress = True
    stuck = {e: pos[e] for e in ENGINES if pos[e] < len(p.meta[e])}
    if stuck:
        for e, i in stuck.items():
            kind, a, b = p.meta[e][i]
            print(f"DEADLOCK {e}@{i}/{len(p.meta[e])}: waiting h{a} >= {b} "
                  f"(have {sems.get(a, 0)})")
        raise RuntimeError("plan deadlock")


class Ctx:

    def __init__(self, stages, core_inputs_names):
        self.stages = set(stages)
        self.nc = bass.Bass("TRN2", target_bir_lowering=False)
        self.stack = contextlib.ExitStack()
        self.p = P(self.nc, self.stack)
        self.dram = {}
        self.sb = {}
        self.ps = {}
        self.ext_in = []
        self.ext_out = []
        self._input_names = set(core_inputs_names)
        self.fused = len(self.stages) > 1
        self.cursor = 20480

    def d(self, name, shape, dtype, producer, consumers):
        if name in self.dram:
            return self.dram[name]
        if producer is None:
            kind = "ExternalInput"
            self.ext_in.append(name)
        elif producer in self.stages:
            if (name == "out_tm" or name in DEBUG_NAMES
                    or any(c not in self.stages for c in consumers)):
                kind = "ExternalOutput"
                self.ext_out.append(name)
            else:
                kind = "Internal"
        else:
            kind = "ExternalInput"
            self.ext_in.append(name)
        t = self.nc.dram_tensor(name, list(shape), dtype, kind=kind)
        ap = t.ap()
        self.dram[name] = ap
        return ap

    def sbuf(self, name, shape, dtype):
        if name not in self.sb:
            if self.fused:
                nb = shape[1] * (2 if dtype == BF16 else 4)
                nb = (nb + 63) & ~63
                assert self.cursor + nb <= 196608 - 128, f"sbuf overflow at {name}"
                h = self.nc.alloc_sbuf_tensor_at(name, list(shape), dtype,
                                                 offset=self.cursor)
                self.cursor += nb
            else:
                h = self.nc.alloc_sbuf_tensor(name, list(shape), dtype)
            self.sb[name] = h[:]
        return self.sb[name]

    def new_stage(self):
        if self.fused:
            self.cursor = 20480
            self.sb = {}

    def psum(self, name, shape, dtype=F32):
        if name not in self.ps:
            h = self.nc.alloc_psum_tensor(name, list(shape), dtype)
            self.ps[name] = h[:]
        return self.ps[name]

    def finalize(self):
        nc, p = self.nc, self.p
        p.final_epilogue()
        with nc.Block() as blk:
            for eng in ENGINES:
                fns = p.plan[eng]
                if not fns:
                    continue
                getattr(blk, eng)(lambda e, fns=fns: [f(e) for f in fns])



def host_prepare(inputs):
    f32 = np.float32
    asc = np.ascontiguousarray

    shared = {}
    shared["w0_write_T"] = asc(inputs["L0_attn_W_write"].T)
    shared["w0_out_T"] = asc(inputs["L0_attn_W_out"].T)
    shared["w0_gba_T"] = asc(np.concatenate(
        [inputs["L0_attn_W_gate"].T, inputs["L0_attn_W_beta"].T,
         inputs["L0_attn_W_alpha"].T], axis=1))
    shared["A_log_bc"] = asc(np.broadcast_to(inputs["L0_attn_A_log"][None, :], (128, 16))).astype(f32)
    shared["dt_bias_bc"] = asc(np.broadcast_to(inputs["L0_attn_dt_bias"][None, :], (128, 16))).astype(f32)
    shared["sw1"] = asc(inputs["L0_ffn_shared_w1"])
    shared["sw2"] = asc(inputs["L0_ffn_shared_w2"])
    shared["ew1"] = asc(inputs["L0_ffn_expert_w1"])
    shared["ew2"] = asc(inputs["L0_ffn_expert_w2"])
    shared["router_T"] = asc(inputs["L0_ffn_router_w"].T)
    shared["gamma_bc"] = asc(np.broadcast_to(inputs["L0_ffn_ln_gamma"][None, :], (128, 1024))).astype(f32)
    shared["beta_bc"] = asc(np.broadcast_to(inputs["L0_ffn_ln_beta"][None, :], (128, 1024))).astype(f32)
    shared["wq_T"] = asc(inputs["L1_attn_Wq"].T)
    shared["wk_T"] = asc(inputs["L1_attn_Wk"].T)
    shared["wv_T"] = asc(inputs["L1_attn_Wv"].T)
    shared["wo_T"] = asc(inputs["L1_attn_Wo"].T)
    shared["W1"] = asc(inputs["L1_ffn_W1"])
    shared["W2"] = asc(inputs["L1_ffn_W2"])
    shared["b1c"] = asc(inputs["L1_ffn_b1"].reshape(8, 32, 128).transpose(0, 2, 1))
    shared["b2c"] = asc(inputs["L1_ffn_b2"].reshape(8, 8, 128).transpose(0, 2, 1))

    shared["ident_bf"] = np.eye(128, dtype=BF)
    shared["ident_f32"] = np.eye(128, dtype=f32)
    ut = np.triu(np.ones((CK, CK), f32))
    sl = np.tril(np.ones((CK, CK), f32), -1)
    al = np.ones((CK, CK), f32)
    z = np.zeros((CK, CK), f32)
    shared["ut2"] = asc(np.block([[ut, z], [z, ut]]))
    shared["sl2"] = asc(np.block([[sl, z], [z, sl]]))
    shared["al2"] = asc(np.block([[al, z], [z, al]]))
    shared["halfsel"] = asc(np.block([[np.ones((1, 64), f32), np.zeros((1, 64), f32)],
                                      [np.zeros((1, 64), f32), np.ones((1, 64), f32)]]))
    mT = np.triu(np.ones((CK, CK), f32), 1)
    shared["maskT2"] = asc(np.concatenate([mT, mT], axis=0))
    mp = np.where(np.triu(np.ones((CK, CK), f32), 1) > 0, 0.0, NEG).astype(f32)
    shared["maskPenT"] = asc(mp.T)
    shared["ident64"] = np.eye(64, dtype=f32)
    shared["ones_row"] = np.ones((1, 128), f32)
    amask = np.zeros((4, 128, 512), f32)
    for m in range(4):
        for pp in range(128):
            for c in range(512):
                if c > m * 128 + pp:
                    amask[m, pp, c] = NEG
    shared["amask"] = amask

    x = inputs["L0_attn_out"]
    in_maps = []
    for c in range(NCORES):
        m = dict(shared)
        m["x_tm"] = asc(x[c * BPC:(c + 1) * BPC])
        in_maps.append(m)
    return in_maps



def dram_common(ctx):
    pass


def load_const(ctx, name, shape, dtype=F32, sb_dtype=None, q="sync", sem="cld"):
    ap = ctx.d(name, shape, dtype, None, ())
    sbd = sb_dtype or dtype
    sb = ctx.sbuf("c_" + name, shape, sbd)
    if sbd != dtype:
        ctx.p.dma("gpsimd", sb, ap, sem + "_g")
    else:
        ctx.p.dma(q, sb, ap, sem)
    return sb


def stage1(ctx):
    p = ctx.p
    x_tm = ctx.d("x_tm", (BPC, T, D), F32, None, ())
    xn_f = ctx.d("xn_tm_f32", (BPC, T, D), F32, "s1", ("s4",))
    xn_b = ctx.d("xn_tm_bf", (BPC, T, D), BF16, "s1", ("s1",))
    h_b = ctx.d("h_tm_bf", (BPC, T, D), BF16, "s1", ("s1",))
    xn_sh = ctx.d("xn_tm_shift", (BPC, T, D), BF16, "s1", ("s2",))
    xn_fm = ctx.d("xn_fm_bf", (BPC, D, T), BF16, "s1", ("s2",))
    h_fm = ctx.d("h_fm_bf", (BPC, D, T), BF16, "s1", ("s3",))
    gamma = ctx.d("gamma_bc", (128, D), F32, None, ())
    beta = ctx.d("beta_bc", (128, D), F32, None, ())

    gm = ctx.sbuf("s1_gamma", (128, D), F32)
    bt = ctx.sbuf("s1_beta", (128, D), F32)
    p.dma("sync", gm, gamma, "s1cl")
    p.dma("sync", bt, beta, "s1cl")

    zrow = ctx.sbuf("s1_zrow", (1, D), BF16)
    p.op("vector", "memset", zrow, 0.0, sem="s1z")
    p.wait("gpsimd", "s1z")
    for b in range(BPC):
        p.dma("gpsimd", xn_sh[b, 0:1, :], zrow, "s1sh")

    NB = 3
    xt = [ctx.sbuf(f"s1_x{i}", (128, D), F32) for i in range(NB)]
    bns = [ctx.sbuf(f"s1_bn{i}", (128, 12), F32) for i in range(NB)]
    sts = [ctx.sbuf(f"s1_stats{i}", (128, 16), F32) for i in range(NB)]
    xnt = [ctx.sbuf(f"s1_xn{i}", (128, D), F32) for i in range(NB)]
    xnb = [ctx.sbuf(f"s1_xnb{i}", (128, D), BF16) for i in range(NB)]
    ht = [ctx.sbuf(f"s1_h{i}", (128, D), F32) for i in range(NB)]
    ht2 = [ctx.sbuf(f"s1_h2{i}", (128, D), F32) for i in range(NB)]
    hb = [ctx.sbuf(f"s1_hb{i}", (128, D), BF16) for i in range(NB)]

    done_x_s = [0] * NB
    done_x_v = [0] * NB
    done_xn = [0] * NB
    done_xn_c = [0] * NB
    done_xnb = [0] * NB
    done_hb = [0] * NB
    done_ht = [0] * NB
    done_st = [0] * NB
    done_sqv = [0] * NB

    EPS = 1e-5
    for it in range(NT // 128):
        b, r = divmod(it, T // 128)
        pr = it % NB
        xti, xni, xbi, hbi = xt[pr], xnt[pr], xnb[pr], hb[pr]
        bn, st = bns[pr], sts[pr]
        if it >= NB:
            p.wait("sync", "s1xn", done_x_s[pr])
            p.wait("sync", "s1v", done_x_v[pr])
        ldv = p.dma("sync", xti, x_tm[b, r * 128:(r + 1) * 128, :], f"s1ld{pr}")

        if it >= NB:
            p.wait("vector", "s1hx", done_st[pr])
        if it == 0:
            p.wait("vector", "s1cl")
        p.wait("vector", f"s1ld{pr}", ldv)
        p.op("vector", "bn_stats", bn[:, 0:6], xti[:, 0:512])
        p.op("vector", "bn_stats", bn[:, 6:12], xti[:, 512:1024], sem="s1v")
        done_x_v[pr] = p.cnt["s1v"]
        p.wait("vector", "s1v")
        p.op("vector", "bn_aggr", st[:, 0:2], bn, sem="s1v")
        p.wait("vector", "s1v")
        p.op("vector", "tensor_scalar_add", st[:, 2:3], st[:, 1:2], EPS, sem="s1v")
        p.wait("scalar", "s1v")
        p.op("scalar", "activation", st[:, 3:4], st[:, 2:3], AF.Sqrt, sem="s1s")
        p.wait("vector", "s1s")
        p.op("vector", "reciprocal", st[:, 4:5], st[:, 3:4], sem="s1v")
        p.wait("vector", "s1v")
        p.op("vector", "tensor_scalar", st[:, 5:6], st[:, 0:1], st[:, 4:5], -1.0,
             op0=ALU.mult, op1=ALU.mult)
        p.op("vector", "tensor_scalar", st[:, 9:10], st[:, 4:5], st[:, 4:5], None,
             op0=ALU.mult, sem="s1v")
        p.wait("vector", "s1v")
        p.op("vector", "tensor_scalar", st[:, 13:14], st[:, 1:2], st[:, 9:10], None,
             op0=ALU.mult, sem="s1v")
        p.wait("vector", "s1v")
        p.op("vector", "tensor_scalar_add", st[:, 14:15], st[:, 13:14], EPS, sem="s1v")
        p.wait("scalar", "s1v")
        p.op("scalar", "activation", st[:, 10:11], st[:, 14:15], AF.Sqrt, sem="s1s")
        p.wait("vector", "s1s")
        p.op("vector", "reciprocal", st[:, 6:7], st[:, 10:11], sem="s1f")

        if it >= NB:
            p.wait("scalar", f"s1stf{pr}", done_xn[pr])
            p.wait("scalar", "s1xb", done_xn_c[pr])
        p.wait("scalar", "s1v")
        p.op("scalar", "activation", xni, xti, AF.Identity,
             scale=st[:, 4:5], bias=st[:, 5:6], sem="s1xn")
        p.wait("sync", "s1xn")
        p.dma("sync", xn_f[b, r * 128:(r + 1) * 128, :], xni, f"s1stf{pr}")

        if it >= NB:
            p.wait("vector", f"s1stx{pr}", done_xnb[pr])
        p.wait("vector", "s1xn")
        p.op("vector", "tensor_copy", xbi, xni, sem="s1xb")
        done_xn_c[pr] = p.cnt["s1xb"]
        p.wait("gpsimd", "s1xb")
        p.dma("gpsimd", xn_b[b, r * 128:(r + 1) * 128, :], xbi, f"s1stx{pr}")
        if r == T // 128 - 1:
            p.dma("gpsimd", xn_sh[b, r * 128 + 1:T, :], xbi[0:127, :], f"s1stx{pr}")
        else:
            p.dma("gpsimd", xn_sh[b, r * 128 + 1:r * 128 + 129, :], xbi, f"s1stx{pr}")
        done_xnb[pr] = p.cnt[f"s1stx{pr}"]

        if it >= NB:
            p.wait("scalar", "s1hb", done_ht[pr])
        p.wait("scalar", "s1f")
        p.op("scalar", "activation", ht[pr], xni, AF.Identity, scale=st[:, 6:7], sem="s1hx")
        done_x_s[pr] = p.cnt["s1xn"]
        done_xn[pr] = p.cnt[f"s1stf{pr}"]
        done_st[pr] = p.cnt["s1hx"]
        if it >= NB:
            p.wait("vector", f"s1sth{pr}", done_hb[pr])
        p.wait("vector", "s1hx")
        p.op("vector", "tensor_tensor", ht2[pr], ht[pr], gm, op=ALU.mult, sem="s1hg")
        p.wait("vector", "s1hg")
        p.op("vector", "tensor_tensor", hbi, ht2[pr], bt, op=ALU.add, sem="s1hb")
        done_ht[pr] = p.cnt["s1hb"]
        p.wait("gpsimd", "s1hb")
        p.dma("gpsimd", h_b[b, r * 128:(r + 1) * 128, :], hbi, f"s1sth{pr}")
        done_hb[pr] = p.cnt[f"s1sth{pr}"]

    trb = [ctx.sbuf(f"s1_tr{i}", (128, T), BF16) for i in range(2)]
    done_tr = [0, 0]
    for nm in ("s1stx0", "s1stx1", "s1stx2", "s1sth0", "s1sth1", "s1sth2"):
        p.wait("sync", nm)
    i = 0
    for b in range(BPC):
        for src, dst in ((xn_b, xn_fm), (h_b, h_fm)):
            for ki in range(8):
                pr = i % 2
                if i >= 2:
                    p.wait("sync", f"s1trs{pr}", done_tr[pr])
                ldv = p.dmat("sync", trb[pr], src[b, :, ki * 128:(ki + 1) * 128], f"s1trl{pr}")
                p.wait("scalar", f"s1trl{pr}", ldv)
                p.dma("scalar", dst[b, ki * 128:(ki + 1) * 128, :], trb[pr], f"s1trs{pr}")
                done_tr[pr] = p.cnt[f"s1trs{pr}"]
                i += 1


def stage2(ctx):
    p = ctx.p
    xn_fm = ctx.d("xn_fm_bf", (BPC, D, T), BF16, "s1", ("s2",))
    xn_sh = ctx.d("xn_tm_shift", (BPC, T, D), BF16, "s1", ("s2",))
    og_d = ctx.d("og_tm_bf", (BPC, T, D), BF16, "s2", ("s2",))
    sproj_d = ctx.d("sproj_fm_bf", (BPC, D, T), BF16, "s2", ("s4",))
    a_fm_d = ctx.d("a_fm_hbm", (BPC, H, T), F32, "s2", ("s2",))
    lam_fm_d = ctx.d("lam_fm_hbm", (BPC, H, NCHUNK), F32, "s2", ("s2",))

    wgba = ctx.sbuf("s2_wgba", (128, 8 * 48), BF16)
    p.dma("gpsimd", wgba.rearrange("q (ki c) -> q ki c", ki=8),
          ctx.d("w0_gba_T", (D, 48), F32, None, ()).rearrange("(ki q) c -> q ki c", q=128),
          "cldg")
    wwT = ctx.sbuf("s2_wwT", (128, 8 * 1024), BF16)
    p.dma("gpsimd", wwT.rearrange("q (ki c) -> q ki c", ki=8),
          ctx.d("w0_write_T", (D, D), F32, None, ()).rearrange("(ki q) c -> q ki c", q=128),
          "cldg")
    woT = ctx.sbuf("s2_woT", (128, 8 * 1024), BF16)
    p.dma("gpsimd", woT.rearrange("q (ki c) -> q ki c", ki=8),
          ctx.d("w0_out_T", (D, D), F32, None, ()).rearrange("(ki q) c -> q ki c", q=128),
          "cldg")
    alog = load_const(ctx, "A_log_bc", (128, 16))
    dtb = load_const(ctx, "dt_bias_bc", (128, 16))
    ut2 = load_const(ctx, "ut2", (128, 128))
    sl2 = load_const(ctx, "sl2", (128, 128))
    al2 = load_const(ctx, "al2", (128, 128))
    halfsel = load_const(ctx, "halfsel", (2, 128))
    maskPenT = load_const(ctx, "maskPenT", (64, 64))
    ident64 = load_const(ctx, "ident64", (64, 64))
    ones_row = load_const(ctx, "ones_row", (1, 128))
    identf = load_const(ctx, "ident_f32", (128, 128))

    nea = ctx.sbuf("s2_nea", (128, 16), F32)
    nea0 = ctx.sbuf("s2_nea0", (128, 16), F32)
    p.wait("scalar", "cld")
    p.wait("scalar", "cldg")
    p.wait("scalar", "cld_g")
    p.op("scalar", "activation", nea0, alog, AF.Exp, sem="d_s")
    p.wait("vector", "d_s")
    p.op("vector", "tensor_scalar_mul", nea, nea0, -1.0, sem="d_nea")

    gate_b = ctx.sbuf("s2_gate", (128, 256), F32)
    beta_b = ctx.sbuf("s2_beta", (128, 256), F32)
    decr_b = ctx.sbuf("s2_decr", (128, 256), F32)
    a_b = ctx.sbuf("s2_a", (128, 256), F32)
    na_b = ctx.sbuf("s2_na", (128, 256), F32)
    dwe_b = ctx.sbuf("s2_dwe", (128, 256), F32)
    lam_b = ctx.sbuf("s2_lam", (128, 256), F32)
    eag_b = ctx.sbuf("s2_eag", (128, 256), F32)
    tmp16 = ctx.sbuf("s2_tmp16", (128, 16 * 7), F32)
    vb_b = ctx.sbuf("s2_vb", (128, 16 * 1024), BF16)
    Xf = [ctx.sbuf(f"s2_Xf{i}", (128, 8 * 128), BF16) for i in range(2)]
    aev = ctx.sbuf("s2_aev", (16, 128), F32)
    xnK = ctx.sbuf("s2_xnK", (128, 2049), BF16)
    xnT = ctx.sbuf("s2_xnT", (128, 16 * 128), BF16)
    af = [ctx.sbuf(f"s2_af{i}", (1, T), F32) for i in range(2)]
    lamf = ctx.sbuf("s2_lamf", (2, NCHUNK), F32)
    lam2 = ctx.sbuf("s2_lam2", (128, NCHUNK), F32)
    Esb = ctx.sbuf("s2_E", (128, 64), F32)
    J_all = ctx.sbuf("s2_J", (128, 32 * 64), BF16)
    wdw2 = ctx.sbuf("s2_wdw", (128, 128), BF16)
    S2 = [ctx.sbuf(f"s2_S{i}", (128, 64), F32) for i in range(2)]
    S2bf = [ctx.sbuf(f"s2_Sbf{i}", (128, 64), BF16) for i in range(2)]
    otmp = ctx.sbuf("s2_otmp", (128, 128), F32)
    og_hp = ctx.sbuf("s2_og", (128, 16 * 128), BF16)
    og2_hp = ctx.sbuf("s2_og2", (128, 16 * 128), BF16)
    ogf = ctx.sbuf("s2_ogf", (128, 8 * T), BF16)
    spj = [ctx.sbuf(f"s2_spj{i}", (128, 512), BF16) for i in range(2)]

    pS = ctx.psum("pS", (128, 512), F32)
    pA = ctx.psum("pA", (128, 512), F32)
    pB = ctx.psum("pB", (128, 512), F32)
    pC = ctx.psum("pC", (128, 512), F32)
    pD = ctx.psum("pD", (128, 512), F32)

    pE = ctx.psum("pE", (128, 512), F32)
    p48 = pS[:, 0:48]
    pc = pS[:, 64:112]
    pat = pS[0:16, 128:256]
    pL = pS[:, 256:288]
    pM = pC[:, 0:64]
    pB2 = pC[:, 64:128]
    pU = pE[:, 0:64]
    pOI = pD[:, 0:128]
    pOJ = pD[:, 128:256]

    p.wait("tensor", "cld")
    p.wait("tensor", "cldg")
    p.wait("tensor", "cld_g")
    p.wait("tensor", "d_nea")

    done_xf = [0, 0]
    done_p48_v = 0
    done_p48_s = 0
    done_pv = [0, 0]
    done_pc = 0
    done_pM = 0
    done_spj = [0, 0]

    for b in range(S2_NB):
        for tt in range(16):
            pr = tt % 2
            p.wait("tensor", "d_spev")
            p.wait("sync", "d_mmv", done_xf[pr])
            ldv = p.dma("sync", Xf[pr].rearrange("q (ki t) -> q ki t", ki=8),
                        xn_fm[b].rearrange("(ki q) t -> q ki t", q=128)[:, :, tt * 128:(tt + 1) * 128],
                        f"d_xf{pr}")
            p.wait("tensor", f"d_xf{pr}", ldv)
            p.wait("tensor", "d_v", done_p48_v)
            p.wait("tensor", "d_gb", done_p48_s)
            for ki in range(8):
                p.op("tensor", "matmul", p48, Xf[pr][:, ki * 128:(ki + 1) * 128],
                     wgba[:, ki * 48:(ki + 1) * 48], start=(ki == 0), stop=(ki == 7),
                     sem="d_mm48" if ki == 7 else None)
            p.wait("scalar", "d_mm48")
            p.op("scalar", "activation", gate_b[:, tt * 16:(tt + 1) * 16], p48[:, 0:16],
                 AF.Sigmoid)
            p.op("scalar", "activation", beta_b[:, tt * 16:(tt + 1) * 16], p48[:, 16:32],
                 AF.Sigmoid, sem="d_gb")
            done_p48_s = p.cnt["d_gb"]
            p.wait("vector", "d_mm48")
            p.op("vector", "tensor_tensor", tmp16[:, 0:16], p48[:, 32:48], dtb, op=ALU.add,
                 sem="d_v")
            done_p48_v = p.cnt["d_v"]
            p.wait("vector", "d_v")
            p.op("vector", "tensor_scalar_mul", tmp16[:, 16:32], tmp16[:, 0:16], -1.0, sem="d_v")
            p.wait("vector", "d_v")
            p.op("vector", "tensor_tensor", tmp16[:, 32:48], tmp16[:, 0:16], tmp16[:, 16:32],
                 op=ALU.min, sem="d_v")
            p.wait("scalar", "d_v")
            p.op("scalar", "activation", tmp16[:, 48:64], tmp16[:, 32:48], AF.Exp, sem="d_s")
            p.wait("scalar", "d_s")
            p.op("scalar", "activation", tmp16[:, 64:80], tmp16[:, 48:64], AF.Ln, bias=1.0,
                 sem="d_s")
            p.op("vector", "tensor_single_scalar", tmp16[:, 80:96], tmp16[:, 0:16], 0.0,
                 ALU.max, sem="d_v")
            p.wait("vector", "d_s")
            p.wait("vector", "d_v")
            p.op("vector", "tensor_tensor", tmp16[:, 96:112], tmp16[:, 80:96], tmp16[:, 64:80],
                 op=ALU.add, sem="d_v")
            p.wait("vector", "d_v")
            p.op("vector", "tensor_tensor", decr_b[:, tt * 16:(tt + 1) * 16], tmp16[:, 96:112],
                 nea, op=ALU.mult, sem="d_dec")

            for jh in range(2):
                pv = [pA, pB][jh]
                p.wait("tensor", "d_vev", done_pv[jh])
                for ki in range(8):
                    p.op("tensor", "matmul", pv, Xf[pr][:, ki * 128:(ki + 1) * 128],
                         wwT[:, ki * 1024 + jh * 512: ki * 1024 + jh * 512 + 512],
                         start=(ki == 0), stop=(ki == 7), sem="d_mmv" if ki == 7 else None)
                done_xf[pr] = p.cnt["d_mmv"]
                p.wait("scalar", "d_mmv")
                p.wait("scalar", "d_gb", done_p48_s)
                for h8 in range(8):
                    h = jh * 8 + h8
                    p.op("scalar", "activation",
                         vb_b[:, tt * 1024 + h * 64: tt * 1024 + h * 64 + 64],
                         pv[:, h8 * 64:(h8 + 1) * 64], AF.Identity,
                         scale=beta_b[:, tt * 16 + h: tt * 16 + h + 1],
                         sem="d_vev" if h8 == 7 else None)
                done_pv[jh] = p.cnt["d_vev"]

        for tt in range(16):
            cs = tt * 16
            p.wait("tensor", "d_dec")
            p.wait("tensor", "d_gb")
            p.wait("tensor", "d_vev")
            p.wait("tensor", "d_aev", done_pc)
            rhs = decr_b[:, cs:cs + 16]
            p.op("tensor", "matmul", pc[:, 0:16], ut2, rhs, start=True, stop=True)
            p.op("tensor", "matmul", pc[:, 16:32], sl2, rhs, start=True, stop=True)
            p.op("tensor", "matmul", pc[:, 32:48], al2, rhs, start=True, stop=True, sem="d_mmc")
            p.wait("scalar", "d_mmc")
            p.wait("scalar", "d_eag")
            p.op("scalar", "copy", a_b[:, cs:cs + 16], pc[:, 0:16])
            p.op("scalar", "activation", dwe_b[:, cs:cs + 16], pc[:, 16:32], AF.Exp)
            p.op("scalar", "activation", lam_b[:, cs:cs + 16], pc[:, 32:48], AF.Exp)
            p.op("scalar", "activation", tmp16[:, 96:112], pc[:, 0:16], AF.Exp, sem="d_s2")
            p.wait("vector", "d_s2")
            p.op("vector", "tensor_scalar_mul", na_b[:, cs:cs + 16], a_b[:, cs:cs + 16], -1.0)
            p.op("vector", "tensor_tensor", eag_b[:, cs:cs + 16], tmp16[:, 96:112],
                 gate_b[:, cs:cs + 16], op=ALU.mult, sem="d_eag")
            p.wait("tensor", "d_s2")
            p.op("tensor", "transpose", pat, a_b[:, cs:cs + 16], identf, sem="d_tra")
            p.wait("scalar", "d_tra")
            p.wait("scalar", "d_ast")
            p.op("scalar", "copy", aev, pat, sem="d_aev")
            done_pc = p.cnt["d_aev"]
            p.wait("sync", "d_aev")
            p.dma("sync", a_fm_d[b][:, tt * 128:(tt + 1) * 128], aev, "d_ast")
        p.wait("gpsimd", "d_s2")
        for par in range(2):
            p.dma("gpsimd",
                  lam_fm_d[b].rearrange("h (tt par) -> par tt h", par=2)[par:par + 1],
                  lam_b[par * 64:par * 64 + 1, :].rearrange("one (tt h) -> one tt h", h=16),
                  "d_lst", allow_slow_non_contiguous=True)

        if DEV_DUMP:
            p.wait("gpsimd", "d_eag")
            p.wait("gpsimd", "d_dec")
            p.wait("gpsimd", "d_vev")
            for nm, buf in (("dbg_dec", decr_b), ("dbg_a", a_b), ("dbg_dwe", dwe_b),
                            ("dbg_lam", lam_b), ("dbg_eag", eag_b), ("dbg_na", na_b),
                            ("dbg_gate", gate_b), ("dbg_beta", beta_b)):
                dd = ctx.d(nm, (BPC, 128, 256), F32, "s2", ("zz",))
                p.dma("gpsimd", dd[b], buf, "d_dbg")
            dv = ctx.d("dbg_vb", (BPC, 128, 16 * 1024), BF16, "s2", ("zz",))
            p.dma("gpsimd", dv[b], vb_b, "d_dbg")

        p.wait("sync", "d_ast")
        p.wait("sync", "d_lst")
        for hp in range(S2_HP if S2_DO_P3 else 0):
            h0g = hp * 2
            p.op("vector", "memset", xnK[:, 0:1], 0.0, sem="d_mz")
            p.wait("sync", "d_mz")
            p.wait("sync", "d_mmO")
            p.wait("sync", "d_mmU")
            l1 = p.dma("sync", xnK[:, 1:2049], xn_fm[b][hp * 128:(hp + 1) * 128, :], "d_hp")
            l2 = p.dma("sync", xnT.rearrange("q (tt j) -> q tt j", tt=16),
                       xn_sh[b].rearrange("(tt q) j -> q tt j", q=128)[:, :, hp * 128:(hp + 1) * 128],
                       "d_hp")
            l3 = p.dma("sync", af[0], a_fm_d[b][h0g:h0g + 1, :], "d_hp")
            l4 = p.dma("sync", af[1], a_fm_d[b][h0g + 1:h0g + 2, :], "d_hp")
            l5 = p.dma("sync", lamf, lam_fm_d[b][h0g:h0g + 2, :], "d_hp")
            p.wait("tensor", "d_hp", l5)
            p.wait("tensor", "d_aev")
            p.wait("scalar", "d_hp", l5)
            p.wait("scalar", "d_eag")
            p.op("tensor", "matmul", pL, halfsel, lamf, start=True, stop=True, sem="d_mmL")
            p.wait("vector", "d_mmL")
            p.op("vector", "tensor_copy", lam2, pL, sem="d_lam2")

            p.wait("vector", "d_mmO")
            for cp in range(16 if S2_PA else 0):
                for hh in range(2):
                    jslot = (cp * 2 + hh) * 64
                    p.wait("tensor", "d_Jev", done_pM)
                    p.wait("scalar", "d_Jev", done_pM)
                    for par in range(2):
                        i = cp * 2 + par
                        p.op("tensor", "matmul", pM[par * 64:(par + 1) * 64, :],
                             xnK[hh * 64:(hh + 1) * 64, i * 64: i * 64 + 64],
                             xnK[hh * 64:(hh + 1) * 64, 1 + i * 64: 1 + i * 64 + 64],
                             start=True, stop=True)
                        p.op("tensor", "matmul", pB2[par * 64:(par + 1) * 64, :],
                             ones_row[0:1, 0:64],
                             af[hh][0:1, i * 64: i * 64 + 64], start=True, stop=False)
                        p.op("tensor", "matmul", pB2[par * 64:(par + 1) * 64, :],
                             maskPenT, ident64, start=False, stop=True,
                             sem="d_mmJ" if par == 1 else None)
                    p.wait("scalar", "d_mmJ")
                    p.op("scalar", "activation", Esb, pB2, AF.Exp,
                         bias=na_b[:, cp * 16 + h0g + hh: cp * 16 + h0g + hh + 1], sem="d_E")
                    p.wait("vector", "d_E")
                    p.op("vector", "tensor_tensor", J_all[:, jslot:jslot + 64], Esb, pM,
                         op=ALU.mult, sem="d_Jev")
                    done_pM = p.cnt["d_Jev"]

            p.wait("vector", "d_ogst")
            p.op("vector", "memset", og2_hp[:, 0:128], 0.0, sem="d_og2z")
            p.op("vector", "memset", S2[1], 0.0, sem="d_v")
            p.wait("vector", "d_v")
            jv = p.cnt["d_Jev"]
            done_wdw = p.cnt.get("d_mmU", 0)
            done_pU = 0
            done_sbf = 0
            done_sbf_prev = 0
            done_mmO_prev = p.cnt.get("d_mmO", 0)
            done_pO = p.cnt.get("d_oev", 0)
            for i in range(NCHUNK if S2_PB else 0):
                par, cp = i % 2, i // 2
                cur, nxt = S2[(i + 1) % 2], S2[i % 2]
                sbf_cur = S2bf[i % 2]
                sbf_prev = S2bf[(i + 1) % 2]
                p.wait("scalar", "d_mmU", done_wdw)
                for hh in range(2):
                    h = h0g + hh
                    p.op("scalar", "activation",
                         wdw2[par * 64:(par + 1) * 64, hh * 64:(hh + 1) * 64],
                         xnT[par * 64:(par + 1) * 64,
                             cp * 128 + hh * 64: cp * 128 + hh * 64 + 64],
                         AF.Identity,
                         scale=dwe_b[par * 64:(par + 1) * 64, cp * 16 + h: cp * 16 + h + 1],
                         sem="d_wdw" if hh == 1 else None)
                p.wait("tensor", "d_wdw")
                p.wait("tensor", "d_Supd", done_pU)
                for hh in range(2 if S2_PB_U else 0):
                    h = h0g + hh
                    p.op("tensor", "matmul", pU[hh * 64:(hh + 1) * 64, :],
                         wdw2[par * 64:(par + 1) * 64, hh * 64:(hh + 1) * 64],
                         vb_b[par * 64:(par + 1) * 64,
                              cp * 1024 + h * 64: cp * 1024 + h * 64 + 64],
                         start=True, stop=True, sem="d_mmU" if hh == 1 else None)
                done_wdw = p.cnt["d_mmU"]
                p.wait("tensor", "d_oev", done_pO)
                for hh in range(2 if S2_PB_O else 0):
                    if i > 0 and not BISECT_OMM:
                        p.wait("tensor", "d_Sbf", done_sbf_prev)
                        p.op("tensor", "matmul",
                             pOI[par * 64:(par + 1) * 64, hh * 64:(hh + 1) * 64],
                             xnK[hh * 64:(hh + 1) * 64, 1 + i * 64: 1 + i * 64 + 64],
                             sbf_prev[hh * 64:(hh + 1) * 64, :], start=True, stop=True)
                    p.wait("tensor", "d_Jev", jv)
                    p.op("tensor", "matmul",
                         pOJ[par * 64:(par + 1) * 64, hh * 64:(hh + 1) * 64],
                         J_all[par * 64:(par + 1) * 64,
                               (cp * 2 + hh) * 64:(cp * 2 + hh) * 64 + 64],
                         vb_b[par * 64:(par + 1) * 64,
                              cp * 1024 + (h0g + hh) * 64: cp * 1024 + (h0g + hh) * 64 + 64],
                         start=True, stop=True, sem="d_mmO" if hh == 1 else None)
                done_mmO = p.cnt["d_mmO"]
                if not S2_PB_S:
                    continue
                p.wait("vector", "d_mmU")
                p.wait("vector", "d_Sbf", done_sbf)
                p.wait("vector", "d_lam2")
                p.op("vector", "scalar_tensor_tensor", nxt, cur, lam2[:, i:i + 1], pU,
                     op0=ALU.mult, op1=ALU.add, sem="d_Supd")
                done_pU = p.cnt["d_Supd"]
                p.wait("scalar", "d_Supd")
                p.wait("scalar", "d_mmO", done_mmO_prev)
                p.op("scalar", "copy", sbf_cur, nxt, sem="d_Sbf")
                done_sbf_prev = done_sbf
                done_sbf = p.cnt["d_Sbf"]
                done_mmO_prev = done_mmO
                p.wait("scalar", "d_mmO")
                p.wait("scalar", "d_oev", done_pO)
                for hh in range(2 if (S2_PB_O and S2_PB_EV) else 0):
                    h = h0g + hh
                    gcol = gate_b[par * 64:(par + 1) * 64, cp * 16 + h: cp * 16 + h + 1]
                    ecol = eag_b[par * 64:(par + 1) * 64, cp * 16 + h: cp * 16 + h + 1]
                    dst = og_hp[par * 64:(par + 1) * 64,
                                cp * 128 + hh * 64: cp * 128 + hh * 64 + 64]
                    dst2 = og2_hp[par * 64:(par + 1) * 64,
                                  cp * 128 + hh * 64: cp * 128 + hh * 64 + 64]
                    p.op("scalar", "activation", dst,
                         pOJ[par * 64:(par + 1) * 64, hh * 64:(hh + 1) * 64],
                         AF.Identity, scale=gcol,
                         sem=None if i > 0 else ("d_oev" if hh == 1 else None))
                    if i > 0:
                        p.op("scalar", "activation", dst2,
                             pOI[par * 64:(par + 1) * 64, hh * 64:(hh + 1) * 64],
                             AF.Identity, scale=ecol,
                             sem="d_oev" if hh == 1 else None)
                done_pO = p.cnt["d_oev"]

            if DEV_DUMP and b == 0:
                dj = ctx.d("dbg_J", (8, 128, 32 * 64), BF16, "s2", ("zz",))
                p.wait("gpsimd", "d_Jev")
                p.dma("gpsimd", dj[hp], J_all, "d_dbg")
            if not S2_OG:
                continue
            p.wait("gpsimd", "d_oev")
            v1 = p.dma("gpsimd",
                  og_d[b].rearrange("(tt q) j -> q tt j", q=128)[:, :, hp * 128:(hp + 1) * 128],
                  og_hp.rearrange("q (tt c) -> q tt c", tt=16), "d_ogst")
            p.wait("gpsimd", "d_ogst", v1)
            p.wait("gpsimd", "d_og2z")
            p.dma("gpsimd",
                  og_d[b].rearrange("(tt q) j -> q tt j", q=128)[:, :, hp * 128:(hp + 1) * 128],
                  og2_hp.rearrange("q (tt c) -> q tt c", tt=16), "d_ogst",
                  accum_op=ALU.add)

        if not S2_DO_S2B:
            continue
        p.wait("sync", "d_ogst")
        p.wait("sync", "d_mmS")
        for ki in range(8):
            p.dmat("sync", ogf[:, ki * T:(ki + 1) * T], og_d[b][:, ki * 128:(ki + 1) * 128],
                   "d_ogl")
        lv = p.cnt["d_ogl"]
        for ts_ in range(4):
            for mj in range(8):
                po = [pA, pB][mj % 2]
                spr = spj[mj % 2]
                p.wait("tensor", f"d_spst{mj % 2}", done_spj[mj % 2])
                p.wait("tensor", "d_ogl", lv)
                p.wait("tensor", "d_vev")
                for ki in range(8):
                    p.op("tensor", "matmul", po,
                         woT[:, ki * 1024 + mj * 128: ki * 1024 + mj * 128 + 128],
                         ogf[:, ki * T + ts_ * 512: ki * T + ts_ * 512 + 512],
                         start=(ki == 0), stop=(ki == 7), sem="d_mmS" if ki == 7 else None)
                p.wait("vector", "d_mmS")
                p.op("vector", "tensor_copy", spr, po, sem="d_spev")
                p.wait("gpsimd", "d_spev")
                p.dma("gpsimd", sproj_d[b][mj * 128:(mj + 1) * 128, ts_ * 512:(ts_ + 1) * 512],
                      spr, f"d_spst{mj % 2}")
                done_spj[mj % 2] = p.cnt[f"d_spst{mj % 2}"]


def stage3(ctx):
    p = ctx.p
    h_fm = ctx.d("h_fm_bf", (BPC, D, T), BF16, "s1", ("s3",))
    acc0 = ctx.d("acc0_fm", (BPC, D, T), F32, "s3", ("s3",))
    g0_d = ctx.d("g0_bf", (BPC, D, T), BF16, "s3", ("s4",))
    gates_d = ctx.d("gates0", (8, NT), F32, "s3", ("s3",))
    sw1_d = ctx.d("sw1", (2048, D), F32, None, ())
    sw2_d = ctx.d("sw2", (D, 2048), F32, None, ())
    ew1_d = ctx.d("ew1", (8, 2048, D), F32, None, ())
    ew2_d = ctx.d("ew2", (8, D, 2048), F32, None, ())
    rt_d = ctx.d("router_T", (D, 8), F32, None, ())

    identb = load_const(ctx, "ident_bf", (128, 128), BF16, sem="cld3")
    identf = load_const(ctx, "ident_f32", (128, 128), sem="cld3")
    ones_row = load_const(ctx, "ones_row", (1, 128), sem="cld3")

    rt = ctx.sbuf("s3_rt", (128, 8 * 8), BF16)
    p.dma("gpsimd", rt.rearrange("q (ki c) -> q ki c", ki=8),
          rt_d.rearrange("(ki q) c -> q ki c", q=128), "cld3g")

    pS = ctx.psum("pS", (128, 512), F32)
    pA = ctx.psum("pA", (128, 512), F32)
    pB = ctx.psum("pB", (128, 512), F32)
    pC = ctx.psum("pC", (128, 512), F32)
    pT = ctx.psum("pT2", (128, 512), BF16)
    pT3 = ctx.psum("pT3", (128, 512), BF16)

    hbuf = [ctx.sbuf(f"s3_h{i}", (128, 8 * 512), BF16) for i in range(2)]
    h1b = ctx.sbuf("s3_h1big", (128, 16 * 512), BF16)
    wqA = ctx.sbuf("s3_wqA", (128, 128 * 128), BF16)
    wqB = ctx.sbuf("s3_wqB", (128, 128 * 128), BF16)
    wnat = [ctx.sbuf(f"s3_wn{i}", (128, 2048), F32) for i in range(2)]
    qst = ctx.sbuf("s3_qst", (128, 8), F32)
    qt1 = ctx.sbuf("s3_qt1", (128, 2048), F32)
    qt2 = ctx.sbuf("s3_qt2", (128, 2048), F32)
    qt3 = ctx.sbuf("s3_qt3", (128, 2048), F32)
    qr = ctx.sbuf("s3_qr", (128, 2048), BF16)
    lg = ctx.sbuf("s3_lg", (8, 512), F32)
    lt = ctx.sbuf("s3_lt", (128, 8), F32)
    rst = ctx.sbuf("s3_rst", (128, 8), F32)
    exr = ctx.sbuf("s3_exr", (128, 8), F32)
    gtm = ctx.sbuf("s3_gtm", (128, 8), F32)
    gtm0 = ctx.sbuf("s3_gtm0", (128, 8), F32)
    grow = ctx.sbuf("s3_grow", (1, 512), F32)
    gbc = ctx.sbuf("s3_gbc", (128, 512), F32)
    ev = [ctx.sbuf(f"s3_ev{i}", (128, 512), F32) for i in range(2)]
    gl = [ctx.sbuf(f"s3_gl{i}", (128, 512), F32) for i in range(2)]
    gq = [ctx.sbuf(f"s3_gq{i}", (128, 512), BF16) for i in range(2)]

    p.wait("tensor", "cld3")
    p.wait("tensor", "cld3g")
    p.wait("tensor", "cld3_g")

    dn = {"hb": [0, 0], "ff": [0, 0], "ev": [0, 0], "evd": [0, 0],
          "acc_prev": 0, "gbc": 0, "gbc_war": 0, "h1": 0, "wn": [0, 0], "accb": [], "gmm": 0,
          "qr": 0, "plg": 0, "ptr": 0, "gst": 0}

    for gtt in range(16 if S3_ROUTER else 0):
        b, ts_ = divmod(gtt, 4)
        pr = gtt % 2
        p.wait("sync", "f_mm", dn["hb"][pr])
        ldv = p.dma("sync", hbuf[pr].rearrange("q (ki t) -> q ki t", ki=8),
                    h_fm[b].rearrange("(ki q) t -> q ki t", q=128)[:, :, ts_ * 512:(ts_ + 1) * 512],
                    f"f_hl{pr}")
        p.wait("tensor", f"f_hl{pr}", ldv)
        p.wait("tensor", "f_lg", dn["plg"])
        p.wait("tensor", "f_ltc", dn["ptr"])
        for ki in range(8):
            p.op("tensor", "matmul", pS[0:8, 0:512], rt[:, ki * 8:(ki + 1) * 8],
                 hbuf[pr][:, ki * 512:(ki + 1) * 512], start=(ki == 0), stop=(ki == 7),
                 sem="f_mm" if ki == 7 else None)
        dn["hb"][pr] = p.cnt["f_mm"]
        p.wait("scalar", "f_mm")
        p.op("scalar", "copy", lg, pS[0:8, 0:512], sem="f_lg")
        dn["plg"] = p.cnt["f_lg"]
        for j in range(4):
            p.wait("tensor", "f_lg", dn["plg"])
            p.wait("tensor", "f_ltc", dn["ptr"])
            p.op("tensor", "transpose", pS[0:128, 0:8], lg[:, j * 128:(j + 1) * 128],
                 identf[0:8, 0:8], sem="f_tr")
            p.wait("scalar", "f_tr")
            p.wait("scalar", "f_gdone")
            p.op("scalar", "copy", lt, pS[0:128, 0:8], sem="f_ltc")
            dn["ptr"] = p.cnt["f_ltc"]
            p.wait("vector", "f_ltc")
            p.op("vector", "reduce_max", rst[:, 0:1], lt, AX.X, sem="f_v")
            p.wait("vector", "f_v")
            p.op("vector", "tensor_scalar", gtm0, lt, rst[:, 0:1], None, op0=ALU.is_ge)
            p.op("vector", "tensor_scalar_mul", rst[:, 1:2], rst[:, 0:1], -1.0, sem="f_v")
            p.wait("scalar", "f_v")
            p.op("scalar", "activation", exr, lt, AF.Exp,
                 bias=rst[:, 1:2], accum_out=rst[:, 3:4], sem="f_s")
            p.wait("vector", "f_s")
            p.op("vector", "reciprocal", rst[:, 4:5], rst[:, 3:4], sem="f_v")
            p.wait("vector", "f_v")
            p.wait("vector", "f_gst", dn["gst"])
            p.op("vector", "tensor_scalar", gtm, gtm0, rst[:, 4:5], None, op0=ALU.mult,
                 sem="f_gdone")
            p.wait("gpsimd", "f_gdone")
            tr0 = gtt * 512 + j * 128
            p.dma("gpsimd", gates_d[:, tr0:tr0 + 128].rearrange("e t -> t e"), gtm,
                  "f_gst", allow_slow_non_contiguous=True)
            dn["gst"] = p.cnt["f_gst"]

    def quant_matrix(w_d, RR, CC, dest, slot_of):
        for fr in range(RR // 128):
            pr = fr % 2
            wn = wnat[pr]
            p.wait("sync", "q_done", dn["wn"][pr])
            lv = p.dma("sync", wn[:, 0:CC], w_d[fr * 128:(fr + 1) * 128, :], f"q_ld{pr}")
            if S3_QL < 2:
                continue
            p.wait("vector", f"q_ld{pr}", lv)
            p.op("vector", "reduce_sum", qst[:, 0:1], wn[:, 0:CC], AX.X,
                 apply_absolute_value=True, sem="q_v")
            p.wait("vector", "q_v")
            p.op("vector", "tensor_scalar_mul", qst[:, 1:2], qst[:, 0:1], 1.0 / CC, sem="q_v")
            p.wait("vector", "q_v")
            p.op("vector", "tensor_scalar_max", qst[:, 2:3], qst[:, 1:2], 1e-5, sem="q_v")
            p.wait("vector", "q_v")
            p.op("vector", "tensor_scalar_mul", qst[:, 3:4], qst[:, 2:3], 0.5)
            p.op("vector", "tensor_scalar_mul", qst[:, 4:5], qst[:, 2:3], -0.5, sem="q_v")
            p.wait("vector", "q_v")
            p.op("vector", "tensor_scalar", qt1[:, 0:CC], wn[:, 0:CC], qst[:, 3:4], None,
                 op0=ALU.is_gt)
            p.op("vector", "tensor_scalar", qt2[:, 0:CC], wn[:, 0:CC], qst[:, 4:5], None,
                 op0=ALU.is_lt, sem="q_v")
            p.wait("vector", "q_v")
            p.op("vector", "tensor_tensor", qt3[:, 0:CC], qt1[:, 0:CC], qt2[:, 0:CC],
                 op=ALU.subtract, sem="q_v")
            p.wait("vector", "q_v")
            p.wait("vector", "q_tr", dn["qr"])
            p.op("vector", "tensor_scalar", qr[:, 0:CC], qt3[:, 0:CC], qst[:, 2:3], None,
                 op0=ALU.mult, sem="q_done")
            dn["wn"][pr] = p.cnt["q_done"]
            if S3_QL < 3:
                continue
            p.wait("tensor", "q_done")
            for sub in range(CC // 128):
                p.wait("tensor", "q_ev", p.cnt.get("q_ev", 0))
                k = p.cnt.get("q_ev", 0)
                ptk = [pT, pT3][k % 2][:, ((k // 2) % 2) * 128:((k // 2) % 2) * 128 + 128]
                p.op("tensor", "transpose", ptk,
                     qr[:, sub * 128:(sub + 1) * 128], identb, sem="q_tr")
                p.wait("vector", "q_tr")
                slot = slot_of(fr, sub)
                p.op("vector", "tensor_copy", dest[:, slot * 128:(slot + 1) * 128],
                     ptk, sem="q_ev")
            dn["qr"] = p.cnt["q_tr"]
        p.wait("tensor", "q_ev")

    def ffn_pass(e):
        for gtt in range(16):
            b, ts_ = divmod(gtt, 4)
            pr = gtt % 2
            p.wait("sync", "f_mm", dn["hb"][pr])
            ldv = p.dma("sync", hbuf[pr].rearrange("q (ki t) -> q ki t", ki=8),
                        h_fm[b].rearrange("(ki q) t -> q ki t", q=128)
                        [:, :, ts_ * 512:(ts_ + 1) * 512], f"f_hl{pr}")
            if e >= 0:
                p.wait("sync", "f_mmg", dn["gmm"])
                glv = p.dma("sync", grow, gates_d[e:e + 1, gtt * 512:(gtt + 1) * 512], "f_gl")
                p.wait("tensor", "f_gl", glv)
                p.wait("tensor", "f_gbc", dn["gbc"])
                p.op("tensor", "matmul", pC[:, 0:512], ones_row[0:1, 0:128],
                     grow, start=True, stop=True, sem="f_mmg")
                dn["gmm"] = p.cnt["f_mmg"]
                p.wait("scalar", "f_mmg")
                p.wait("scalar", "f_ev", dn["gbc_war"])
                p.op("scalar", "copy", gbc, pC[:, 0:512], sem="f_gbc")
                dn["gbc"] = p.cnt["f_gbc"]
            p.wait("tensor", f"f_hl{pr}", ldv)
            p.wait("scalar", "f_mm2", dn["h1"])
            for fj in range(16):
                pp = [pA, pB][fj % 2]
                p.wait("tensor", "f_h1ev", dn["ff"][fj % 2])
                p.wait("tensor", "f_ev", dn["ev"][fj % 2])
                for ki in range(8):
                    p.op("tensor", "matmul", pp,
                         wqA[:, (ki * 16 + fj) * 128:(ki * 16 + fj) * 128 + 128],
                         hbuf[pr][:, ki * 512:(ki + 1) * 512], start=(ki == 0), stop=(ki == 7),
                         sem="f_mm" if ki == 7 else None)
                p.wait("scalar", "f_mm")
                p.op("scalar", "activation", h1b[:, fj * 512:(fj + 1) * 512], pp, AF.Silu,
                     sem="f_h1ev")
                dn["ff"][fj % 2] = p.cnt["f_h1ev"]
            dn["hb"][pr] = p.cnt["f_mm"]
            p.wait("tensor", "f_h1ev")
            for mj in range(8):
                pp = [pA, pB][mj % 2]
                p.wait("tensor", "f_ev", dn["ev"][mj % 2])
                for fk in range(16):
                    p.op("tensor", "matmul", pp,
                         wqB[:, (fk * 8 + mj) * 128:(fk * 8 + mj) * 128 + 128],
                         h1b[:, fk * 512:(fk + 1) * 512], start=(fk == 0), stop=(fk == 15),
                         sem="f_mm2" if fk == 15 else None)
                evb = ev[mj % 2]
                p.wait("scalar", "f_mm2")
                p.wait("scalar", f"f_acst{mj % 2}", dn["evd"][mj % 2])
                if e < 0:
                    p.op("scalar", "copy", evb, pp, sem="f_ev")
                else:
                    glb = gl[mj % 2]
                    p.op("scalar", "copy", glb, pp, sem="f_ev0")
                    p.wait("vector", "f_ev0")
                    p.wait("vector", "f_gbc", dn["gbc"])
                    p.wait("vector", f"f_acst{mj % 2}", dn["evd"][mj % 2])
                    p.op("vector", "tensor_tensor", evb, glb, gbc, op=ALU.mult, sem="f_ev")
                dn["ev"][mj % 2] = p.cnt["f_ev"]
                p.wait("gpsimd", "f_ev")
                if e >= 0 and dn["accb"]:
                    for nm, vv in dn["accb"]:
                        p.wait("gpsimd", nm, vv)
                    dn["accb"] = []
                p.dma("gpsimd", acc0[b][mj * 128:(mj + 1) * 128, ts_ * 512:(ts_ + 1) * 512],
                      evb, f"f_acst{mj % 2}", **({} if e < 0 else {"accum_op": ALU.add}))
                dn["evd"][mj % 2] = p.cnt[f"f_acst{mj % 2}"]
            dn["h1"] = p.cnt["f_mm2"]
            dn["gbc_war"] = p.cnt["f_ev"]
        dn["accb"] = [("f_acst0", p.cnt.get("f_acst0", 0)), ("f_acst1", p.cnt.get("f_acst1", 0))]

    quant_matrix(sw1_d, 2048, D, wqA, lambda fr, sub: sub * 16 + fr)
    quant_matrix(sw2_d, D, 2048, wqB, lambda fr, sub: sub * 8 + fr)
    p.wait("sync", "f_gst")
    if S3_SHARED:
        ffn_pass(-1)
    for e in range(S3_NEXP):
        p.wait("sync", "f_mm")
        p.wait("sync", "f_mm2")
        quant_matrix(ew1_d[e], 2048, D, wqA, lambda fr, sub: sub * 16 + fr)
        quant_matrix(ew2_d[e], D, 2048, wqB, lambda fr, sub: sub * 8 + fr)
        ffn_pass(e)

    p.wait("sync", "f_acst0")
    p.wait("sync", "f_acst1")
    if not S3_G0:
        return
    dng = [0, 0]
    dng2 = [0, 0]
    for b in range(BPC):
        for pt_ in range(8):
            for ts_ in range(4):
                pr = ts_ % 2
                p.wait("sync", "g_sil", dng[pr])
                lv = p.dma("sync", gl[pr], acc0[b][pt_ * 128:(pt_ + 1) * 128,
                                                   ts_ * 512:(ts_ + 1) * 512], f"g_ld{pr}")
                p.wait("scalar", f"g_ld{pr}", lv)
                p.wait("scalar", f"g_st{pr}", dng2[pr])
                p.op("scalar", "activation", gq[pr], gl[pr], AF.Silu, sem="g_sil")
                dng[pr] = p.cnt["g_sil"]
                p.wait("gpsimd", "g_sil")
                p.dma("gpsimd", g0_d[b][pt_ * 128:(pt_ + 1) * 128, ts_ * 512:(ts_ + 1) * 512],
                      gq[pr], f"g_st{pr}")
                dng2[pr] = p.cnt[f"g_st{pr}"]


def stage4(ctx):
    p = ctx.p
    x_tm = ctx.d("x_tm", (BPC, T, D), F32, None, ())
    xn_f = ctx.d("xn_tm_f32", (BPC, T, D), F32, "s1", ("s4",))
    sproj_d = ctx.d("sproj_fm_bf", (BPC, D, T), BF16, "s2", ("s4",))
    g0_d = ctx.d("g0_bf", (BPC, D, T), BF16, "s3", ("s4",))
    x1_d = ctx.d("x1_tm", (BPC, T, D), F32, "s4", ("s8",))
    xn2b_d = ctx.d("xn2_tm_bf", (BPC, T, D), BF16, "s4", ("s4",))
    xn2fm_d = ctx.d("xn2_fm_bf", (BPC, D, T), BF16, "s4", ("s5", "s7"))
    wsel_d = ctx.d("wsel8", (8, NT), F32, "s4", ("s7",))

    NB = 3
    xt = [ctx.sbuf(f"s4_x{i}", (128, D), F32) for i in range(NB)]
    xnt = [ctx.sbuf(f"s4_xn{i}", (128, D), F32) for i in range(NB)]
    spt = [ctx.sbuf(f"s4_sp{i}", (128, D), BF16) for i in range(NB)]
    g0t = [ctx.sbuf(f"s4_g0{i}", (128, D), BF16) for i in range(NB)]
    sb_ = [ctx.sbuf(f"s4_s{i}", (128, D), F32) for i in range(NB)]
    sgt = [ctx.sbuf(f"s4_sg{i}", (128, D), F32) for i in range(NB)]
    x1t = [ctx.sbuf(f"s4_x1{i}", (128, D), F32) for i in range(NB)]
    bns = [ctx.sbuf(f"s4_bn{i}", (128, 12), F32) for i in range(NB)]
    sts = [ctx.sbuf(f"s4_st{i}", (128, 16), F32) for i in range(NB)]
    xn2t = [ctx.sbuf(f"s4_x2n{i}", (128, D), F32) for i in range(NB)]
    xn2b = [ctx.sbuf(f"s4_x2b{i}", (128, D), BF16) for i in range(NB)]
    s01 = ctx.sbuf("s4_s01", (128, 64), F32)
    hshA = ctx.sbuf("s4_hA", (128, 64), F32)
    hshB = ctx.sbuf("s4_hB", (128, 64), F32)
    hshC = ctx.sbuf("s4_hC", (128, 64), F32)
    hshD = ctx.sbuf("s4_hD", (128, 64), F32)
    wrow = ctx.sbuf("s4_wrow", (128, 64), F32)
    wrow2 = ctx.sbuf("s4_wrow2", (128, 64), F32)

    dx = [0] * NB
    dxn = [0] * NB
    dsp = [0] * NB
    dg0 = [0] * NB
    dst_ = [0] * NB
    dx1 = [0] * NB
    dx2b = [0] * NB
    ds01 = 0
    EPS = 1e-5

    for it in range(NT // 128):
        b, r = divmod(it, T // 128)
        pr = it % NB
        t0 = r * 128
        if it >= NB:
            p.wait("sync", "c_v", dx[pr])
        lx = p.dma("sync", xt[pr], x_tm[b, t0:t0 + 128, :], f"c_lx{pr}")
        ln = p.dma("sync", xnt[pr], xn_f[b, t0:t0 + 128, :], f"c_ln{pr}")
        lsp = p.dmat("sync", spt[pr], sproj_d[b][:, t0:t0 + 128], f"c_lsp{pr}")
        lg0 = p.dmat("sync", g0t[pr], g0_d[b][:, t0:t0 + 128], f"c_lg0{pr}")
        p.wait("vector", f"c_ln{pr}", ln)
        p.wait("vector", f"c_lsp{pr}", lsp)
        p.wait("vector", f"c_lx{pr}", lx)
        p.wait("vector", f"c_lg0{pr}", lg0)
        if it >= NB:
            p.wait("vector", f"c_sxb{pr}", dx2b[pr])
            p.wait("vector", f"c_sx1{pr}", dx1[pr])
        p.op("vector", "tensor_tensor", sb_[pr], xnt[pr], spt[pr], op=ALU.add, sem="c_v")
        p.wait("vector", "c_v")
        p.op("vector", "tensor_tensor", sgt[pr], sb_[pr], g0t[pr], op=ALU.mult, sem="c_v")
        p.wait("vector", "c_v")
        p.op("vector", "tensor_tensor", x1t[pr], xt[pr], sgt[pr], op=ALU.add, sem="c_v")
        dx[pr] = p.cnt["c_v"]
        p.wait("sync", "c_v")
        p.dma("sync", x1_d[b, t0:t0 + 128, :], x1t[pr], f"c_sx1{pr}")
        dx1[pr] = p.cnt[f"c_sx1{pr}"]

        bn, st = bns[pr], sts[pr]
        if it >= NB:
            p.wait("vector", "c_hx", dst_[pr])
        p.wait("vector", "c_v")
        p.op("vector", "bn_stats", bn[:, 0:6], x1t[pr][:, 0:512])
        p.op("vector", "bn_stats", bn[:, 6:12], x1t[pr][:, 512:1024], sem="c_v")
        p.wait("vector", "c_v")
        p.op("vector", "bn_aggr", st[:, 0:2], bn, sem="c_v")
        p.wait("vector", "c_v")
        p.op("vector", "tensor_scalar_add", st[:, 2:3], st[:, 1:2], EPS, sem="c_v")
        p.wait("scalar", "c_v")
        p.op("scalar", "activation", st[:, 3:4], st[:, 2:3], AF.Sqrt, sem="c_s")
        p.wait("vector", "c_s")
        p.op("vector", "reciprocal", st[:, 4:5], st[:, 3:4], sem="c_v")
        p.wait("vector", "c_v")
        p.op("vector", "tensor_scalar", st[:, 5:6], st[:, 0:1], st[:, 4:5], -1.0,
             op0=ALU.mult, op1=ALU.mult, sem="c_v")
        p.wait("scalar", "c_v")
        p.op("scalar", "activation", xn2t[pr], x1t[pr], AF.Identity,
             scale=st[:, 4:5], bias=st[:, 5:6], sem="c_hx")
        dst_[pr] = p.cnt["c_hx"]
        p.wait("vector", "c_hx")
        p.op("vector", "tensor_tensor", s01[:, it % 64:it % 64 + 1] if False else s01[:, it:it + 1] if it < 64 else None,
             xn2t[pr][:, 0:1], xn2t[pr][:, 1:2], op=ALU.add, sem="c_s01")
        p.op("vector", "tensor_copy", xn2b[pr], xn2t[pr], sem="s4xb")
        p.wait("gpsimd", "s4xb")
        p.dma("gpsimd", xn2b_d[b, t0:t0 + 128, :], xn2b[pr], f"c_sxb{pr}")
        dx2b[pr] = p.cnt[f"c_sxb{pr}"]

    p.wait("vector", "c_s01")
    def fence(name="h_v"):
        p.wait("vector", name)
    p.op("vector", "tensor_scalar_mul", hshA, s01, 0.125, sem="h_v"); fence()
    p.op("vector", "tensor_scalar", hshB, hshA, 1.0, None, op0=ALU.is_ge, sem="h_v"); fence()
    for k in range(2, 8):
        p.op("vector", "tensor_scalar", hshC, hshA, float(k), None, op0=ALU.is_ge, sem="h_v"); fence()
        p.op("vector", "tensor_tensor", hshD, hshB, hshC, op=ALU.add, sem="h_v"); fence()
        hshB, hshD = hshD, hshB
    for k in range(0, 8):
        p.op("vector", "tensor_scalar", hshC, hshA, float(-k), None, op0=ALU.is_lt, sem="h_v"); fence()
        p.op("vector", "tensor_tensor", hshD, hshB, hshC, op=ALU.subtract, sem="h_v"); fence()
        hshB, hshD = hshD, hshB
    p.op("vector", "scalar_tensor_tensor", hshA, hshB, -8.0, s01,
         op0=ALU.mult, op1=ALU.add, sem="h_v"); fence()
    p.op("vector", "tensor_scalar", hshB, hshA, 1.0, None, op0=ALU.is_ge, sem="h_v"); fence()
    for k in range(2, 8):
        p.op("vector", "tensor_scalar", hshC, hshA, float(k), None, op0=ALU.is_ge, sem="h_v"); fence()
        p.op("vector", "tensor_tensor", hshD, hshB, hshC, op=ALU.add, sem="h_v"); fence()
        hshB, hshD = hshD, hshB
    p.op("vector", "tensor_scalar", hshC, s01, 0.0, None, op0=ALU.is_lt, sem="h_v"); fence()
    p.op("vector", "tensor_tensor", hshA, hshB, hshC, op=ALU.add, sem="h_v"); fence()
    p.op("vector", "tensor_scalar", hshC, hshA, 8.0, None, op0=ALU.is_ge, sem="h_v"); fence()
    p.op("vector", "scalar_tensor_tensor", hshB, hshC, -8.0, hshA,
         op0=ALU.mult, op1=ALU.add, sem="h_v"); fence()
    dws = 0
    for e in range(8):
        p.op("vector", "tensor_scalar", hshC, hshB, float(e), None, op0=ALU.is_equal, sem="h_v"); fence()
        p.op("vector", "tensor_scalar", hshD, hshB, float((e + 7) % 8), None,
             op0=ALU.is_equal, sem="h_v"); fence()
        p.op("vector", "tensor_tensor", wrow, hshC, hshD, op=ALU.add, sem="h_v"); fence()
        p.wait("vector", "h_ws", dws)
        p.op("vector", "tensor_scalar_mul", wrow2, wrow, 0.5, sem="h_v"); fence()
        p.wait("gpsimd", "h_v")
        p.dma("gpsimd", wsel_d[e].rearrange("(tt q) -> q tt", q=128), wrow2, "h_ws",
              allow_slow_non_contiguous=True)
        dws = p.cnt["h_ws"]

    trb = [ctx.sbuf(f"s4_tr{i}", (128, T), BF16) for i in range(2)]
    done_tr = [0, 0]
    for nm in [f"c_sxb{i}" for i in range(NB)]:
        p.wait("sync", nm)
    i = 0
    for b in range(BPC):
        for ki in range(8):
            pr = i % 2
            if i >= 2:
                p.wait("sync", f"c_trs{pr}", done_tr[pr])
            ldv = p.dmat("sync", trb[pr], xn2b_d[b, :, ki * 128:(ki + 1) * 128], f"c_trl{pr}")
            p.wait("scalar", f"c_trl{pr}", ldv)
            p.dma("scalar", xn2fm_d[b, ki * 128:(ki + 1) * 128, :], trb[pr], f"c_trs{pr}")
            done_tr[pr] = p.cnt[f"c_trs{pr}"]
            i += 1


def stage5(ctx):
    p = ctx.p
    xn2fm_d = ctx.d("xn2_fm_bf", (BPC, D, T), BF16, "s4", ("s5", "s7"))
    q_d = ctx.d("q_fm", (BPC, D, T), BF16, "s5", ("s6",))
    k_d = ctx.d("k_fm", (BPC, 256, T), BF16, "s5", ("s6",))
    v_d = ctx.d("v1_tm", (BPC, T, 256), BF16, "s5", ("s6",))
    wq_d = ctx.d("wq_T", (D, D), F32, None, ())
    wk_d = ctx.d("wk_T", (D, 256), F32, None, ())
    wv_d = ctx.d("wv_T", (D, 256), F32, None, ())

    wq = ctx.sbuf("s5_wq", (128, 8 * 1024), BF16)
    wk = ctx.sbuf("s5_wk", (128, 8 * 256), BF16)
    wv = ctx.sbuf("s5_wv", (128, 8 * 256), BF16)
    p.dma("gpsimd", wq.rearrange("q (ki c) -> q ki c", ki=8),
          wq_d.rearrange("(ki q) c -> q ki c", q=128), "a_wl")
    p.dma("gpsimd", wk.rearrange("q (ki c) -> q ki c", ki=8),
          wk_d.rearrange("(ki q) c -> q ki c", q=128), "a_wl")
    p.dma("gpsimd", wv.rearrange("q (ki c) -> q ki c", ki=8),
          wv_d.rearrange("(ki q) c -> q ki c", q=128), "a_wl")

    pA = ctx.psum("pA", (128, 512), F32)
    pB = ctx.psum("pB", (128, 512), F32)
    hbuf = [ctx.sbuf(f"s3_h{i}", (128, 8 * 512), BF16) for i in range(2)]
    evq = [ctx.sbuf(f"s5_ev{i}", (128, 512), BF16) for i in range(2)]

    p.wait("tensor", "a_wl")
    dhb = [0, 0]
    dev = [0, 0]
    devd = [0, 0]
    for gtt in range(16):
        b, ts_ = divmod(gtt, 4)
        pr = gtt % 2
        p.wait("sync", "a_mm", dhb[pr])
        ldv = p.dma("sync", hbuf[pr].rearrange("q (ki t) -> q ki t", ki=8),
                    xn2fm_d[b].rearrange("(ki q) t -> q ki t", q=128)
                    [:, :, ts_ * 512:(ts_ + 1) * 512], f"a_hl{pr}")
        p.wait("tensor", f"a_hl{pr}", ldv)
        jobs = []
        for mj in range(8):
            jobs.append(("q", mj, q_d[b][mj * 128:(mj + 1) * 128, ts_ * 512:(ts_ + 1) * 512],
                         [wq[:, ki * 1024 + mj * 128: ki * 1024 + mj * 128 + 128] for ki in range(8)]))
        for mj in range(2):
            jobs.append(("k", mj, k_d[b][mj * 128:(mj + 1) * 128, ts_ * 512:(ts_ + 1) * 512],
                         [wk[:, ki * 256 + mj * 128: ki * 256 + mj * 128 + 128] for ki in range(8)]))
        for ji, (kind, mj, dram_out, wblocks) in enumerate(jobs):
            pp = [pA, pB][ji % 2]
            p.wait("tensor", "a_ev", dev[ji % 2])
            for ki in range(8):
                p.op("tensor", "matmul", pp, wblocks[ki],
                     hbuf[pr][:, ki * 512:(ki + 1) * 512], start=(ki == 0), stop=(ki == 7),
                     sem="a_mm" if ki == 7 else None)
            p.wait("scalar", "a_mm")
            p.wait("scalar", f"a_st{ji % 2}", devd[ji % 2])
            p.op("scalar", "copy", evq[ji % 2], pp, sem="a_ev")
            dev[ji % 2] = p.cnt["a_ev"]
            p.wait("gpsimd", "a_ev")
            p.dma("gpsimd", dram_out, evq[ji % 2], f"a_st{ji % 2}")
            devd[ji % 2] = p.cnt[f"a_st{ji % 2}"]
        for sub in range(4):
            ji = 10 + sub
            pp = [pA, pB][ji % 2]
            p.wait("tensor", "a_ev", dev[ji % 2])
            for ki in range(8):
                p.op("tensor", "matmul", pp[:, 0:256],
                     hbuf[pr][:, ki * 512 + sub * 128: ki * 512 + sub * 128 + 128],
                     wv[:, ki * 256:(ki + 1) * 256], start=(ki == 0), stop=(ki == 7),
                     sem="a_mm" if ki == 7 else None)
            p.wait("scalar", "a_mm")
            p.wait("scalar", f"a_st{ji % 2}", devd[ji % 2])
            p.op("scalar", "copy", evq[ji % 2][:, 0:256], pp[:, 0:256], sem="a_ev")
            dev[ji % 2] = p.cnt["a_ev"]
            p.wait("gpsimd", "a_ev")
            p.dma("gpsimd", v_d[b][ts_ * 512 + sub * 128: ts_ * 512 + sub * 128 + 128, :],
                  evq[ji % 2][:, 0:256], f"a_st{ji % 2}")
            devd[ji % 2] = p.cnt[f"a_st{ji % 2}"]
        dhb[pr] = p.cnt["a_mm"]


def stage6(ctx):
    p = ctx.p
    q_d = ctx.d("q_fm", (BPC, D, T), BF16, "s5", ("s6",))
    k_d = ctx.d("k_fm", (BPC, 256, T), BF16, "s5", ("s6",))
    v_d = ctx.d("v1_tm", (BPC, T, 256), BF16, "s5", ("s6",))
    o_d = ctx.d("o1_fm", (BPC, D, T), BF16, "s6", ("s6",))
    s2_d = ctx.d("s2_fm_bf", (BPC, D, T), BF16, "s6", ("s8",))
    am_d = ctx.d("amask", (4, 128, 512), F32, None, ())
    wo_d = ctx.d("wo_T", (D, D), F32, None, ())
    identb = load_const(ctx, "ident_bf", (128, 128), BF16, sem="at_c")

    amask = ctx.sbuf("s6_am", (128, 4 * 512), F32)
    p.dma("sync", amask.rearrange("q (m c) -> q m c", m=4),
          am_d.rearrange("m q c -> q m c"), "at_c")
    wo = ctx.sbuf("s5_wq", (128, 8 * 1024), BF16)
    p.dma("gpsimd", wo.rearrange("q (ki c) -> q ki c", ki=8),
          wo_d.rearrange("(ki q) c -> q ki c", q=128), "at_cg")

    pA = ctx.psum("pA", (128, 512), F32)
    pB = ctx.psum("pB", (128, 512), F32)
    pT = ctx.psum("pT2", (128, 512), BF16)
    pT3 = ctx.psum("pT3", (128, 512), BF16)
    pD = ctx.psum("pD", (128, 512), F32)

    qh = ctx.sbuf("s6_qh", (64, T), BF16)
    kg = ctx.sbuf("s6_kg", (64, T), BF16)
    vb = ctx.sbuf("s6_vb", (128, 16 * 64), BF16)
    Ssb = ctx.sbuf("s6_S", (128, T), F32)
    Psb = ctx.sbuf("s6_P", (128, T), BF16)
    P2sb = ctx.sbuf("s6_P2", (128, T), BF16)
    Pt = [ctx.sbuf(f"s6_Pt{i}", (128, 128), BF16) for i in range(2)]
    ob = ctx.sbuf("s6_ob", (64, T), BF16)
    sst = ctx.sbuf("s6_sst", (128, 8), F32)
    hbuf = [ctx.sbuf(f"s3_h{i}", (128, 8 * 512), BF16) for i in range(2)]
    evq = [ctx.sbuf(f"s5_ev{i}", (128, 512), BF16) for i in range(2)]

    p.wait("tensor", "at_c")
    p.wait("tensor", "at_cg")
    dsc = [0, 0]
    dptev = [0, 0]
    dob = 0
    dqh = 0
    for b in range(BPC):
        for h in range(16):
            g = h // 4
            p.wait("sync", "at_mm", dqh)
            p.wait("sync", "at_obst", p.cnt.get("at_obst", 0))
            l1 = p.dma("sync", qh, q_d[b][h * 64:(h + 1) * 64, :], "at_l")
            l2 = p.dma("sync", kg, k_d[b][g * 64:(g + 1) * 64, :], "at_l")
            l3 = p.dma("sync", vb.rearrange("q (tt e) -> q tt e", tt=16),
                       v_d[b].rearrange("(tt q) j -> q tt j", q=128)
                       [:, :, g * 64:(g + 1) * 64], "at_l")
            p.wait("tensor", "at_l", l3)
            for qt in range(16):
                nk5 = qt // 4 + 1
                L = nk5 * 512
                for kt in range(nk5):
                    pp = [pA, pB][kt % 2]
                    p.wait("tensor", "at_sev", dsc[kt % 2])
                    p.op("tensor", "matmul", pp, qh[:, qt * 128:(qt + 1) * 128],
                         kg[:, kt * 512:(kt + 1) * 512], start=True, stop=True, sem="at_mm")
                    p.wait("vector", "at_mm")
                    if kt == qt // 4:
                        p.op("vector", "scalar_tensor_tensor", Ssb[:, kt * 512:(kt + 1) * 512],
                             pp, 0.125, amask[:, (qt % 4) * 512:(qt % 4) * 512 + 512],
                             op0=ALU.mult, op1=ALU.add, sem="at_sev")
                    else:
                        p.op("vector", "tensor_scalar_mul", Ssb[:, kt * 512:(kt + 1) * 512],
                             pp, 0.125, sem="at_sev")
                    dsc[kt % 2] = p.cnt["at_sev"]
                p.wait("vector", "at_sev")
                p.op("vector", "reduce_max", sst[:, 0:1], Ssb[:, 0:L], AX.X,
                     negate=True, sem="at_v")
                p.wait("scalar", "at_v")
                p.op("scalar", "activation", Psb[:, 0:L], Ssb[:, 0:L], AF.Exp,
                     bias=sst[:, 0:1], accum_out=sst[:, 1:2], sem="at_s")
                p.wait("vector", "at_s")
                p.op("vector", "reciprocal", sst[:, 2:3], sst[:, 1:2], sem="at_v")
                p.wait("vector", "at_v")
                p.op("vector", "tensor_scalar", P2sb[:, 0:L], Psb[:, 0:L], sst[:, 2:3],
                     None, op0=ALU.mult, sem="at_p2")
                nkt = qt + 1
                p.wait("tensor", "at_p2")
                for ktt in range(nkt):
                    pr2 = ktt % 2
                    ptk = [pT, pT3][pr2][:, 0:128]
                    p.wait("tensor", "at_ptev", p.cnt.get("at_ptev", 0))
                    p.op("tensor", "transpose", ptk, P2sb[:, ktt * 128:(ktt + 1) * 128],
                         identb, sem="at_tr")
                    p.wait("scalar", "at_tr")
                    p.op("scalar", "copy", Pt[pr2], ptk, sem="at_ptev")
                    p.wait("tensor", "at_ptev")
                    p.op("tensor", "matmul", pD[0:64, 0:128], vb[:, ktt * 64:(ktt + 1) * 64],
                         Pt[pr2], start=(ktt == 0), stop=(ktt == nkt - 1),
                         sem="at_mmo" if ktt == nkt - 1 else None)
                p.wait("scalar", "at_mmo")
                p.op("scalar", "copy", ob[:, qt * 128:(qt + 1) * 128], pD[0:64, 0:128],
                     sem="at_oev")
            dqh = p.cnt["at_mm"]
            p.wait("gpsimd", "at_oev")
            p.dma("gpsimd", o_d[b][h * 64:(h + 1) * 64, :], ob, "at_obst")

    p.wait("sync", "at_obst")
    dhb = [0, 0]
    dev = [0, 0]
    devd = [0, 0]
    for gtt in range(16):
        b, ts_ = divmod(gtt, 4)
        pr = gtt % 2
        p.wait("sync", "at_mm", dhb[pr])
        ldv = p.dma("sync", hbuf[pr].rearrange("q (ki t) -> q ki t", ki=8),
                    o_d[b].rearrange("(ki q) t -> q ki t", q=128)
                    [:, :, ts_ * 512:(ts_ + 1) * 512], f"at_ol{pr}")
        p.wait("tensor", f"at_ol{pr}", ldv)
        for mj in range(8):
            pp = [pA, pB][mj % 2]
            p.wait("tensor", "at_ev2", dev[mj % 2])
            p.wait("tensor", "at_sev")
            for ki in range(8):
                p.op("tensor", "matmul", pp, wo[:, ki * 1024 + mj * 128: ki * 1024 + mj * 128 + 128],
                     hbuf[pr][:, ki * 512:(ki + 1) * 512], start=(ki == 0), stop=(ki == 7),
                     sem="at_mm" if ki == 7 else None)
            p.wait("scalar", "at_mm")
            p.wait("scalar", f"at_s2st{mj % 2}", devd[mj % 2])
            p.op("scalar", "copy", evq[mj % 2], pp, sem="at_ev2")
            dev[mj % 2] = p.cnt["at_ev2"]
            p.wait("gpsimd", "at_ev2")
            p.dma("gpsimd", s2_d[b][mj * 128:(mj + 1) * 128, ts_ * 512:(ts_ + 1) * 512],
                  evq[mj % 2], f"at_s2st{mj % 2}")
            devd[mj % 2] = p.cnt[f"at_s2st{mj % 2}"]
        dhb[pr] = p.cnt["at_mm"]


def stage7(ctx):
    p = ctx.p
    xn2fm_d = ctx.d("xn2_fm_bf", (BPC, D, T), BF16, "s4", ("s5", "s7"))
    acc1 = ctx.d("acc1_fm", (BPC, D, T), F32, "s7", ("s7",))
    g1_d = ctx.d("g1_bf", (BPC, D, T), BF16, "s7", ("s8",))
    wsel_d = ctx.d("wsel8", (8, NT), F32, "s4", ("s7",))
    W1_d = ctx.d("W1", (8, D, 4096), F32, None, ())
    W2_d = ctx.d("W2", (8, 4096, D), F32, None, ())
    b1_d = ctx.d("b1c", (8, 128, 32), F32, None, ())
    b2_d = ctx.d("b2c", (8, 128, 8), F32, None, ())
    ones_row = load_const(ctx, "ones_row", (1, 128), sem="l_c")

    pA = ctx.psum("pA", (128, 512), F32)
    pB = ctx.psum("pB", (128, 512), F32)
    pC = ctx.psum("pC", (128, 512), F32)
    hbuf = [ctx.sbuf(f"s3_h{i}", (128, 8 * 512), BF16) for i in range(2)]
    h1b = ctx.sbuf("s3_h1big", (128, 16 * 512), BF16)
    wqA = ctx.sbuf("s3_wqA", (128, 128 * 128), BF16)
    wqB = ctx.sbuf("s3_wqB", (128, 128 * 128), BF16)
    b1sb = ctx.sbuf("s7_b1", (128, 32), F32)
    b2sb = ctx.sbuf("s7_b2", (128, 8), F32)
    grow = ctx.sbuf("s3_grow", (1, 512), F32)
    gbc = ctx.sbuf("s3_gbc", (128, 512), F32)
    ev = [ctx.sbuf(f"s3_ev{i}", (128, 512), F32) for i in range(2)]
    gl = [ctx.sbuf(f"s3_gl{i}", (128, 512), F32) for i in range(2)]
    gq = [ctx.sbuf(f"s3_gq{i}", (128, 512), BF16) for i in range(2)]

    dn = {"hb": [0, 0], "ff": [0, 0], "ev": [0, 0], "evd": [0, 0],
          "gbc": 0, "gbc_war": 0, "h1": 0, "gmm": 0, "accb": [], "wl": 0, "bl": 0}

    p.wait("tensor", "l_c")
    for e in range(8):
        for ha in range(2):
            p.wait("gpsimd", "l_mm", dn["wl"])
            p.wait("gpsimd", "l_mm2", dn["wl2"] if "wl2" in dn else 0)
            w1v = p.dma("gpsimd", wqA.rearrange("q (ki f) -> q ki f", ki=8),
                        W1_d[e].rearrange("(ki q) f -> q ki f", q=128)
                        [:, :, ha * 2048:(ha + 1) * 2048], "l_w1")
            w2v = p.dma("gpsimd", wqB.rearrange("q (fk m) -> q fk m", fk=16),
                        W2_d[e][ha * 2048:(ha + 1) * 2048, :]
                        .rearrange("(fk q) m -> q fk m", q=128), "l_w2")
            if ha == 0:
                p.wait("sync", "l_h1ev", dn["bl"])
                p.wait("sync", "l_ev0", dn["bl2"] if "bl2" in dn else 0)
                b1v = p.dma("sync", b1sb, b1_d[e], "l_b1")
                b2v = p.dma("sync", b2sb, b2_d[e], "l_b1")
            p.wait("tensor", "l_w1", w1v)
            p.wait("tensor", "l_w2", w2v)
            p.wait("scalar", "l_b1")
            for gtt in range(16):
                b, ts_ = divmod(gtt, 4)
                pr = gtt % 2
                p.wait("sync", "l_mm", dn["hb"][pr])
                ldv = p.dma("sync", hbuf[pr].rearrange("q (ki t) -> q ki t", ki=8),
                            xn2fm_d[b].rearrange("(ki q) t -> q ki t", q=128)
                            [:, :, ts_ * 512:(ts_ + 1) * 512], f"l_hl{pr}")
                if True:
                    p.wait("sync", "l_mmg", dn["gmm"])
                    glv = p.dma("sync", grow, wsel_d[e:e + 1, gtt * 512:(gtt + 1) * 512],
                                "l_gl")
                    p.wait("tensor", "l_gl", glv)
                    p.wait("tensor", "l_gbc", dn["gbc"])
                    p.op("tensor", "matmul", pC[:, 0:512], ones_row[0:1, 0:128],
                         grow, start=True, stop=True, sem="l_mmg")
                    dn["gmm"] = p.cnt["l_mmg"]
                    p.wait("scalar", "l_mmg")
                    p.wait("scalar", "l_ev", dn["gbc_war"])
                    p.op("scalar", "copy", gbc, pC[:, 0:512], sem="l_gbc")
                    dn["gbc"] = p.cnt["l_gbc"]
                p.wait("tensor", f"l_hl{pr}", ldv)
                p.wait("scalar", "l_mm2", dn["h1"])
                for fj in range(16):
                    pp = [pA, pB][fj % 2]
                    p.wait("tensor", "l_h1ev", dn["ff"][fj % 2])
                    p.wait("tensor", "l_ev", dn["ev"][fj % 2])
                    for ki in range(8):
                        p.op("tensor", "matmul", pp,
                             wqA[:, (ki * 16 + fj) * 128:(ki * 16 + fj) * 128 + 128],
                             hbuf[pr][:, ki * 512:(ki + 1) * 512],
                             start=(ki == 0), stop=(ki == 7),
                             sem="l_mm" if ki == 7 else None)
                    p.wait("scalar", "l_mm")
                    p.op("scalar", "activation", h1b[:, fj * 512:(fj + 1) * 512], pp,
                         AF.Relu, bias=b1sb[:, ha * 16 + fj: ha * 16 + fj + 1],
                         sem="l_h1ev")
                    dn["ff"][fj % 2] = p.cnt["l_h1ev"]
                dn["hb"][pr] = p.cnt["l_mm"]
                p.wait("tensor", "l_h1ev")
                for mj in range(8):
                    pp = [pA, pB][mj % 2]
                    p.wait("tensor", "l_ev", dn["ev"][mj % 2])
                    for fk in range(16):
                        p.op("tensor", "matmul", pp,
                             wqB[:, (fk * 8 + mj) * 128:(fk * 8 + mj) * 128 + 128],
                             h1b[:, fk * 512:(fk + 1) * 512],
                             start=(fk == 0), stop=(fk == 15),
                             sem="l_mm2" if fk == 15 else None)
                    evb = ev[mj % 2]
                    glb = gl[mj % 2]
                    p.wait("scalar", "l_mm2")
                    p.wait("scalar", f"l_acst{mj % 2}", dn["evd"][mj % 2])
                    if ha == 1:
                        p.op("scalar", "activation", glb, pp, AF.Identity,
                             bias=b2sb[:, mj:mj + 1], sem="l_ev0")
                    else:
                        p.op("scalar", "copy", glb, pp, sem="l_ev0")
                    p.wait("vector", "l_ev0")
                    p.wait("vector", "l_gbc", dn["gbc"])
                    p.wait("vector", f"l_acst{mj % 2}", dn["evd"][mj % 2])
                    p.op("vector", "tensor_tensor", evb, glb, gbc, op=ALU.mult, sem="l_ev")
                    dn["ev"][mj % 2] = p.cnt["l_ev"]
                    p.wait("gpsimd", "l_ev")
                    if dn["accb"]:
                        for nm, vv in dn["accb"]:
                            p.wait("gpsimd", nm, vv)
                        dn["accb"] = []
                    first = (e == 0 and ha == 0)
                    p.dma("gpsimd", acc1[b][mj * 128:(mj + 1) * 128, ts_ * 512:(ts_ + 1) * 512],
                          evb, f"l_acst{mj % 2}",
                          **({} if first else {"accum_op": ALU.add}))
                    dn["evd"][mj % 2] = p.cnt[f"l_acst{mj % 2}"]
                dn["h1"] = p.cnt["l_mm2"]
                dn["gbc_war"] = p.cnt["l_ev"]
            dn["wl"] = p.cnt["l_mm"]
            dn["wl2"] = p.cnt["l_mm2"]
            if ha == 1:
                dn["bl"] = p.cnt["l_h1ev"]
                dn["bl2"] = p.cnt["l_ev0"]
            dn["accb"] = [("l_acst0", p.cnt.get("l_acst0", 0)),
                          ("l_acst1", p.cnt.get("l_acst1", 0))]

    p.wait("sync", "l_acst0")
    p.wait("sync", "l_acst1")
    dng = [0, 0]
    dng2 = [0, 0]
    for b in range(BPC):
        for pt_ in range(8):
            for ts_ in range(4):
                pr = ts_ % 2
                p.wait("sync", "l_sil", dng[pr])
                lv = p.dma("sync", gl[pr], acc1[b][pt_ * 128:(pt_ + 1) * 128,
                                                   ts_ * 512:(ts_ + 1) * 512], f"l_gld{pr}")
                p.wait("scalar", f"l_gld{pr}", lv)
                p.wait("scalar", f"l_gst{pr}", dng2[pr])
                p.op("scalar", "activation", gq[pr], gl[pr], AF.Silu, sem="l_sil")
                dng[pr] = p.cnt["l_sil"]
                p.wait("gpsimd", "l_sil")
                p.dma("gpsimd", g1_d[b][pt_ * 128:(pt_ + 1) * 128, ts_ * 512:(ts_ + 1) * 512],
                      gq[pr], f"l_gst{pr}")
                dng2[pr] = p.cnt[f"l_gst{pr}"]


def stage8(ctx):
    p = ctx.p
    x1_d = ctx.d("x1_tm", (BPC, T, D), F32, "s4", ("s8",))
    s2_d = ctx.d("s2_fm_bf", (BPC, D, T), BF16, "s6", ("s8",))
    g1_d = ctx.d("g1_bf", (BPC, D, T), BF16, "s7", ("s8",))
    out_d = ctx.d("out_tm", (BPC, T, D), F32, "s8", ())

    NB = 3
    x1t = [ctx.sbuf(f"s8_x1{i}", (128, D), F32) for i in range(NB)]
    s2t = [ctx.sbuf(f"s8_s2{i}", (128, D), BF16) for i in range(NB)]
    g1t = [ctx.sbuf(f"s8_g1{i}", (128, D), BF16) for i in range(NB)]
    sgt = [ctx.sbuf(f"s8_sg{i}", (128, D), F32) for i in range(NB)]
    x2t = [ctx.sbuf(f"s8_x2{i}", (128, D), F32) for i in range(NB)]

    dv = [0] * NB
    dst8 = [0] * NB
    for it in range(NT // 128):
        b, r = divmod(it, T // 128)
        pr = it % NB
        t0 = r * 128
        if it >= NB:
            p.wait("sync", "o_v", dv[pr])
        lx = p.dma("sync", x1t[pr], x1_d[b, t0:t0 + 128, :], f"o_lx{pr}")
        ls = p.dmat("sync", s2t[pr], s2_d[b][:, t0:t0 + 128], f"o_ls{pr}")
        lg = p.dmat("sync", g1t[pr], g1_d[b][:, t0:t0 + 128], f"o_lg{pr}")
        p.wait("vector", f"o_lx{pr}", lx)
        p.wait("vector", f"o_ls{pr}", ls)
        p.wait("vector", f"o_lg{pr}", lg)
        if it >= NB:
            p.wait("vector", f"o_st{pr}", dst8[pr])
        p.op("vector", "tensor_tensor", sgt[pr], s2t[pr], g1t[pr], op=ALU.mult, sem="o_v")
        p.wait("vector", "o_v")
        p.op("vector", "tensor_tensor", x2t[pr], x1t[pr], sgt[pr], op=ALU.add, sem="o_v")
        dv[pr] = p.cnt["o_v"]
        p.wait("sync", "o_v")
        p.dma("sync", out_d[b, t0:t0 + 128, :], x2t[pr], f"o_st{pr}")
        dst8[pr] = p.cnt[f"o_st{pr}"]


STAGE_FNS = {"s1": stage1, "s2": stage2, "s3": stage3, "s4": stage4,
             "s5": stage5, "s6": stage6, "s7": stage7, "s8": stage8}


def build_launch(stages, core_input_names):
    ctx = Ctx(stages, core_input_names)
    with ctx.stack:
        dram_common(ctx)
        first = True
        for s in ALL_STAGES:
            if s in ctx.stages:
                if not first:
                    ctx.p.barrier()
                ctx.new_stage()
                STAGE_FNS[s](ctx)
                first = False
        check_deadlock(ctx.p)
        ctx.finalize()
    return ctx


LAUNCH_PLAN = [tuple(ALL_STAGES)]


def kernel(**inputs):
    in_maps = host_prepare(inputs)
    core_input_names = set(in_maps[0].keys())
    state = [dict(m) for m in in_maps]
    for stages in LAUNCH_PLAN:
        ctx = build_launch(stages, core_input_names)
        launch_maps = [{k: st[k] for k in ctx.ext_in} for st in state]
        res = run_bass_kernel_spmd(ctx.nc, launch_maps, core_ids=list(range(NCORES)))
        for c in range(NCORES):
            for name in ctx.ext_out:
                state[c][name] = np.asarray(res.results[c][name])
    out = np.concatenate([state[c]["out_tm"] for c in range(NCORES)], axis=0)
    return out.astype(np.float32)


# version 03
# speedup vs baseline: 1.0563x; 1.0257x over previous
import contextlib
import math
import numpy as np
import ml_dtypes
import concourse.bass as bass
import concourse.mybir as mybir
from concourse.bass_utils import run_bass_kernel_spmd

F32 = mybir.dt.float32
BF16 = mybir.dt.bfloat16
AF = mybir.ActivationFunctionType
ALU = mybir.AluOpType
AX = mybir.AxisListType
BF = ml_dtypes.bfloat16

NCORES = 8
BPC = 4
T = 2048
D = 1024
NT = BPC * T
H, HD, CK = 16, 64, 64
NCHUNK = T // CK
NEG = -1.0e30

ENGINES = ("sync", "scalar", "vector", "tensor", "gpsimd")

ALL_STAGES = ("s1", "s2", "s3", "s4", "s5", "s6", "s7", "s8")
DEBUG_NAMES = set()
DEV_DUMP = False
BISECT_EVAC = False
BISECT_OMM = False
BISECT_POJ = False
BISECT_TS = False
BISECT_NOACT = False
S2_NB = 4
S2_HP = 8
S2_DO_P3 = True
S2_PA = True
S2_PB = True
S2_OG = True
S2_PB_S = True
S2_PB_O = True
S2_PB_U = True
S2_PB_EV = True
S3_ROUTER = True
S3_SHARED = True
S3_NEXP = 8
S3_G0 = True
S3_QL = 3
S2_DO_S2B = True


class P:

    def __init__(self, nc, stack):
        self.nc = nc
        self.stack = stack
        self.plan = {e: [] for e in ENGINES}
        self.meta = {e: [] for e in ENGINES}
        self.handles = []
        self.hcnt = []
        self.epoch = 0
        self.name2h = {}
        self.pool_ptr = 0
        self.sem_queues = {}
        self.dma_handles = {}
        self._barn = 0

    def _hid(self, name):
        key = (self.epoch, name)
        if key not in self.name2h:
            if self.pool_ptr >= len(self.handles):
                h = self.stack.enter_context(self.nc.semaphore(f"s{len(self.handles)}"))
                self.handles.append(h)
                self.hcnt.append(0)
            self.name2h[key] = self.pool_ptr
            self.pool_ptr += 1
        return self.name2h[key]

    class _CntView:
        def __init__(self, p):
            self.p = p

        def __getitem__(self, name):
            return self.p.hcnt[self.p._hid(name)]

        def get(self, name, default=0):
            return self.p.hcnt[self.p._hid(name)]

        def __setitem__(self, name, v):
            self.p.hcnt[self.p._hid(name)] = v

        def __contains__(self, name):
            return (self.p.epoch, name) in self.p.name2h

    @property
    def cnt(self):
        return P._CntView(self)

    def sem(self, name):
        return self.handles[self._hid(name)]

    def wait(self, eng, name, val=None):
        hid = self._hid(name)
        h = self.handles[hid]
        v = self.hcnt[hid] if val is None else val
        self.plan[eng].append(lambda e, h=h, v=v: e.wait_ge(h, v))
        self.meta[eng].append(("wait", hid, v))
        return v

    def dma(self, q, out, in_, sem, **kw):
        hid = self._hid(sem)
        h = self.handles[hid]
        self.hcnt[hid] += 16
        qc = "sw" if q == "gpsimd" else "hw"
        key = (self.epoch, sem)
        self.sem_queues.setdefault(key, set()).add(q)
        assert all(("sw" if x == "gpsimd" else "hw") == qc for x in self.sem_queues[key]), \
            f"sem {sem} mixes SWDGE and HWDGE queues"
        self.dma_handles.setdefault(hid, set()).add(q)
        def f(e, out=out, in_=in_, h=h, kw=kw):
            e.dma_start(out=out, in_=in_, **kw).then_inc(h, 16)
        self.plan[q].append(f)
        self.meta[q].append(("inc", hid, 16))
        return self.hcnt[hid]

    def dmat(self, q, out, in_, sem):
        hid = self._hid(sem)
        h = self.handles[hid]
        self.hcnt[hid] += 16
        key = (self.epoch, sem)
        self.sem_queues.setdefault(key, set()).add(q)
        self.dma_handles.setdefault(hid, set()).add(q)
        def f(e, out=out, in_=in_, h=h):
            e.dma_start_transpose(out, in_).then_inc(h, 16)
        self.plan[q].append(f)
        self.meta[q].append(("inc", hid, 16))
        return self.hcnt[hid]

    def op(self, eng, meth, *args, sem=None, **kw):
        h = None
        if sem is not None:
            hid = self._hid(sem)
            h = self.handles[hid]
            self.hcnt[hid] += 1
        def f(e, meth=meth, args=args, kw=kw, h=h):
            ins = getattr(e, meth)(*args, **kw)
            if h is not None:
                ins.then_inc(h, 1)
        self.plan[eng].append(f)
        if sem is not None:
            self.meta[eng].append(("inc", hid, 1))
            return self.hcnt[hid]
        self.meta[eng].append(("op", -1, 0))
        return None

    def barrier(self):
        for (ep, name), queues in list(self.sem_queues.items()):
            if ep != self.epoch:
                continue
            for q in queues:
                self.wait(q, name)
        self._barn += 1
        hid = self._hid(f"__bar{self._barn}")
        h = self.handles[hid]
        for e in ENGINES:
            self.hcnt[hid] += 1
            def f(en, h=h):
                en.sem_inc(h, 1)
            self.plan[e].append(f)
            self.meta[e].append(("inc", hid, 1))
        tgt = self.hcnt[hid]
        for e in ENGINES:
            self.plan[e].append(lambda en, h=h, v=tgt: en.wait_ge(h, v))
            self.meta[e].append(("wait", hid, tgt))
        self.epoch += 1
        self.pool_ptr = 0

    def final_epilogue(self):
        for hid, queues in self.dma_handles.items():
            h = self.handles[hid]
            v = self.hcnt[hid]
            for q in queues:
                self.plan[q].append(lambda e, h=h, v=v: e.wait_ge(h, v))
                self.meta[q].append(("wait", hid, v))


def check_deadlock(p):
    pos = {e: 0 for e in ENGINES}
    sems = {}
    progress = True
    while progress:
        progress = False
        for e in ENGINES:
            m = p.meta[e]
            while pos[e] < len(m):
                kind, a, b = m[pos[e]]
                if kind == "wait" and sems.get(a, 0) < b:
                    break
                if kind == "inc":
                    sems[a] = sems.get(a, 0) + b
                pos[e] += 1
                progress = True
    stuck = {e: pos[e] for e in ENGINES if pos[e] < len(p.meta[e])}
    if stuck:
        for e, i in stuck.items():
            kind, a, b = p.meta[e][i]
            print(f"DEADLOCK {e}@{i}/{len(p.meta[e])}: waiting h{a} >= {b} "
                  f"(have {sems.get(a, 0)})")
        raise RuntimeError("plan deadlock")


class Ctx:

    def __init__(self, stages, core_inputs_names):
        self.stages = set(stages)
        self.nc = bass.Bass("TRN2", target_bir_lowering=False)
        self.stack = contextlib.ExitStack()
        self.p = P(self.nc, self.stack)
        self.dram = {}
        self.sb = {}
        self.ps = {}
        self.ext_in = []
        self.ext_out = []
        self._input_names = set(core_inputs_names)
        self.fused = len(self.stages) > 1
        self.cursor = 20480

    def d(self, name, shape, dtype, producer, consumers):
        if name in self.dram:
            return self.dram[name]
        if producer is None:
            kind = "ExternalInput"
            self.ext_in.append(name)
        elif producer in self.stages:
            if (name == "out_tm" or name in DEBUG_NAMES
                    or any(c not in self.stages for c in consumers)):
                kind = "ExternalOutput"
                self.ext_out.append(name)
            else:
                kind = "Internal"
        else:
            kind = "ExternalInput"
            self.ext_in.append(name)
        t = self.nc.dram_tensor(name, list(shape), dtype, kind=kind)
        ap = t.ap()
        self.dram[name] = ap
        return ap

    def sbuf(self, name, shape, dtype):
        if name not in self.sb:
            if self.fused:
                nb = shape[1] * (2 if dtype == BF16 else 4)
                nb = (nb + 63) & ~63
                assert self.cursor + nb <= 196608 - 128, f"sbuf overflow at {name}"
                h = self.nc.alloc_sbuf_tensor_at(name, list(shape), dtype,
                                                 offset=self.cursor)
                self.cursor += nb
            else:
                h = self.nc.alloc_sbuf_tensor(name, list(shape), dtype)
            self.sb[name] = h[:]
        return self.sb[name]

    def new_stage(self):
        if self.fused:
            self.cursor = 20480
            self.sb = {}

    def psum(self, name, shape, dtype=F32):
        if name not in self.ps:
            h = self.nc.alloc_psum_tensor(name, list(shape), dtype)
            self.ps[name] = h[:]
        return self.ps[name]

    def finalize(self):
        nc, p = self.nc, self.p
        p.final_epilogue()
        with nc.Block() as blk:
            for eng in ENGINES:
                fns = p.plan[eng]
                if not fns:
                    continue
                getattr(blk, eng)(lambda e, fns=fns: [f(e) for f in fns])



def host_prepare(inputs):
    f32 = np.float32
    asc = np.ascontiguousarray

    shared = {}
    shared["w0_write_T"] = asc(inputs["L0_attn_W_write"].T)
    shared["w0_out_T"] = asc(inputs["L0_attn_W_out"].T)
    shared["w0_gba_T"] = asc(np.concatenate(
        [inputs["L0_attn_W_gate"].T, inputs["L0_attn_W_beta"].T,
         inputs["L0_attn_W_alpha"].T], axis=1))
    shared["A_log_bc"] = asc(np.broadcast_to(inputs["L0_attn_A_log"][None, :], (128, 16))).astype(f32)
    shared["dt_bias_bc"] = asc(np.broadcast_to(inputs["L0_attn_dt_bias"][None, :], (128, 16))).astype(f32)
    shared["sw1"] = asc(inputs["L0_ffn_shared_w1"])
    shared["sw2"] = asc(inputs["L0_ffn_shared_w2"])
    shared["ew1"] = asc(inputs["L0_ffn_expert_w1"])
    shared["ew2"] = asc(inputs["L0_ffn_expert_w2"])
    shared["router_T"] = asc(inputs["L0_ffn_router_w"].T)
    shared["gamma_bc"] = asc(np.broadcast_to(inputs["L0_ffn_ln_gamma"][None, :], (128, 1024))).astype(f32)
    shared["beta_bc"] = asc(np.broadcast_to(inputs["L0_ffn_ln_beta"][None, :], (128, 1024))).astype(f32)
    shared["wq_T"] = asc(inputs["L1_attn_Wq"].T)
    shared["wk_T"] = asc(inputs["L1_attn_Wk"].T)
    shared["wv_T"] = asc(inputs["L1_attn_Wv"].T)
    shared["wo_T"] = asc(inputs["L1_attn_Wo"].T)
    shared["W1"] = asc(inputs["L1_ffn_W1"])
    shared["W2"] = asc(inputs["L1_ffn_W2"])
    shared["b1c"] = asc(inputs["L1_ffn_b1"].reshape(8, 32, 128).transpose(0, 2, 1))
    shared["b2c"] = asc(inputs["L1_ffn_b2"].reshape(8, 8, 128).transpose(0, 2, 1))

    shared["ident_bf"] = np.eye(128, dtype=BF)
    shared["ident_f32"] = np.eye(128, dtype=f32)
    ut = np.triu(np.ones((CK, CK), f32))
    sl = np.tril(np.ones((CK, CK), f32), -1)
    al = np.ones((CK, CK), f32)
    z = np.zeros((CK, CK), f32)
    shared["ut2"] = asc(np.block([[ut, z], [z, ut]]))
    shared["sl2"] = asc(np.block([[sl, z], [z, sl]]))
    shared["al2"] = asc(np.block([[al, z], [z, al]]))
    shared["halfsel"] = asc(np.block([[np.ones((1, 64), f32), np.zeros((1, 64), f32)],
                                      [np.zeros((1, 64), f32), np.ones((1, 64), f32)]]))
    mT = np.triu(np.ones((CK, CK), f32), 1)
    shared["maskT2"] = asc(np.concatenate([mT, mT], axis=0))
    mp = np.where(np.triu(np.ones((CK, CK), f32), 1) > 0, 0.0, NEG).astype(f32)
    shared["maskPenT"] = asc(mp.T)
    shared["ident64"] = np.eye(64, dtype=f32)
    shared["ones_row"] = np.ones((1, 128), f32)
    amask = np.zeros((4, 128, 512), f32)
    for m in range(4):
        for pp in range(128):
            for c in range(512):
                if c > m * 128 + pp:
                    amask[m, pp, c] = NEG
    shared["amask"] = amask

    x = inputs["L0_attn_out"]
    in_maps = []
    for c in range(NCORES):
        m = dict(shared)
        m["x_tm"] = asc(x[c * BPC:(c + 1) * BPC])
        in_maps.append(m)
    return in_maps



def dram_common(ctx):
    pass


def load_const(ctx, name, shape, dtype=F32, sb_dtype=None, q="sync", sem="cld"):
    ap = ctx.d(name, shape, dtype, None, ())
    sbd = sb_dtype or dtype
    sb = ctx.sbuf("c_" + name, shape, sbd)
    if sbd != dtype:
        ctx.p.dma("gpsimd", sb, ap, sem + "_g")
    else:
        ctx.p.dma(q, sb, ap, sem)
    return sb


def stage1(ctx):
    p = ctx.p
    x_tm = ctx.d("x_tm", (BPC, T, D), F32, None, ())
    xn_f = ctx.d("xn_tm_f32", (BPC, T, D), F32, "s1", ("s4",))
    xn_b = ctx.d("xn_tm_bf", (BPC, T, D), BF16, "s1", ("s1",))
    h_b = ctx.d("h_tm_bf", (BPC, T, D), BF16, "s1", ("s1",))
    xn_sh = ctx.d("xn_tm_shift", (BPC, T, D), BF16, "s1", ("s2",))
    xn_fm = ctx.d("xn_fm_bf", (BPC, D, T), BF16, "s1", ("s2",))
    h_fm = ctx.d("h_fm_bf", (BPC, D, T), BF16, "s1", ("s3",))
    gamma = ctx.d("gamma_bc", (128, D), F32, None, ())
    beta = ctx.d("beta_bc", (128, D), F32, None, ())

    gm = ctx.sbuf("s1_gamma", (128, D), F32)
    bt = ctx.sbuf("s1_beta", (128, D), F32)
    p.dma("sync", gm, gamma, "s1cl")
    p.dma("sync", bt, beta, "s1cl")

    zrow = ctx.sbuf("s1_zrow", (1, D), BF16)
    p.op("vector", "memset", zrow, 0.0, sem="s1z")
    p.wait("gpsimd", "s1z")
    for b in range(BPC):
        p.dma("gpsimd", xn_sh[b, 0:1, :], zrow, "s1sh")

    NB = 3
    xt = [ctx.sbuf(f"s1_x{i}", (128, D), F32) for i in range(NB)]
    bns = [ctx.sbuf(f"s1_bn{i}", (128, 12), F32) for i in range(NB)]
    sts = [ctx.sbuf(f"s1_stats{i}", (128, 16), F32) for i in range(NB)]
    xnt = [ctx.sbuf(f"s1_xn{i}", (128, D), F32) for i in range(NB)]
    xnb = [ctx.sbuf(f"s1_xnb{i}", (128, D), BF16) for i in range(NB)]
    ht = [ctx.sbuf(f"s1_h{i}", (128, D), F32) for i in range(NB)]
    ht2 = [ctx.sbuf(f"s1_h2{i}", (128, D), F32) for i in range(NB)]
    hb = [ctx.sbuf(f"s1_hb{i}", (128, D), BF16) for i in range(NB)]

    done_x_s = [0] * NB
    done_x_v = [0] * NB
    done_xn = [0] * NB
    done_xn_c = [0] * NB
    done_xnb = [0] * NB
    done_hb = [0] * NB
    done_ht = [0] * NB
    done_st = [0] * NB
    done_sqv = [0] * NB

    EPS = 1e-5
    for it in range(NT // 128):
        b, r = divmod(it, T // 128)
        pr = it % NB
        xti, xni, xbi, hbi = xt[pr], xnt[pr], xnb[pr], hb[pr]
        bn, st = bns[pr], sts[pr]
        if it >= NB:
            p.wait("sync", "s1xn", done_x_s[pr])
            p.wait("sync", "s1v", done_x_v[pr])
        ldv = p.dma("sync", xti, x_tm[b, r * 128:(r + 1) * 128, :], f"s1ld{pr}")

        if it >= NB:
            p.wait("vector", "s1hx", done_st[pr])
        if it == 0:
            p.wait("vector", "s1cl")
        p.wait("vector", f"s1ld{pr}", ldv)
        p.op("vector", "bn_stats", bn[:, 0:6], xti[:, 0:512])
        p.op("vector", "bn_stats", bn[:, 6:12], xti[:, 512:1024], sem="s1v")
        done_x_v[pr] = p.cnt["s1v"]
        p.wait("vector", "s1v")
        p.op("vector", "bn_aggr", st[:, 0:2], bn, sem="s1v")
        p.wait("vector", "s1v")
        p.op("vector", "tensor_scalar_add", st[:, 2:3], st[:, 1:2], EPS, sem="s1v")
        p.wait("scalar", "s1v")
        p.op("scalar", "activation", st[:, 3:4], st[:, 2:3], AF.Sqrt, sem="s1s")
        p.wait("vector", "s1s")
        p.op("vector", "reciprocal", st[:, 4:5], st[:, 3:4], sem="s1v")
        p.wait("vector", "s1v")
        p.op("vector", "tensor_scalar", st[:, 5:6], st[:, 0:1], st[:, 4:5], -1.0,
             op0=ALU.mult, op1=ALU.mult)
        p.op("vector", "tensor_scalar", st[:, 9:10], st[:, 4:5], st[:, 4:5], None,
             op0=ALU.mult, sem="s1v")
        p.wait("vector", "s1v")
        p.op("vector", "tensor_scalar", st[:, 13:14], st[:, 1:2], st[:, 9:10], None,
             op0=ALU.mult, sem="s1v")
        p.wait("vector", "s1v")
        p.op("vector", "tensor_scalar_add", st[:, 14:15], st[:, 13:14], EPS, sem="s1v")
        p.wait("scalar", "s1v")
        p.op("scalar", "activation", st[:, 10:11], st[:, 14:15], AF.Sqrt, sem="s1s")
        p.wait("vector", "s1s")
        p.op("vector", "reciprocal", st[:, 6:7], st[:, 10:11], sem="s1f")

        if it >= NB:
            p.wait("scalar", f"s1stf{pr}", done_xn[pr])
            p.wait("scalar", "s1xb", done_xn_c[pr])
        p.wait("scalar", "s1v")
        p.op("scalar", "activation", xni, xti, AF.Identity,
             scale=st[:, 4:5], bias=st[:, 5:6], sem="s1xn")
        p.wait("sync", "s1xn")
        p.dma("sync", xn_f[b, r * 128:(r + 1) * 128, :], xni, f"s1stf{pr}")

        if it >= NB:
            p.wait("vector", f"s1stx{pr}", done_xnb[pr])
        p.wait("vector", "s1xn")
        p.op("vector", "tensor_copy", xbi, xni, sem="s1xb")
        done_xn_c[pr] = p.cnt["s1xb"]
        p.wait("gpsimd", "s1xb")
        p.dma("gpsimd", xn_b[b, r * 128:(r + 1) * 128, :], xbi, f"s1stx{pr}")
        if r == T // 128 - 1:
            p.dma("gpsimd", xn_sh[b, r * 128 + 1:T, :], xbi[0:127, :], f"s1stx{pr}")
        else:
            p.dma("gpsimd", xn_sh[b, r * 128 + 1:r * 128 + 129, :], xbi, f"s1stx{pr}")
        done_xnb[pr] = p.cnt[f"s1stx{pr}"]

        if it >= NB:
            p.wait("scalar", "s1hb", done_ht[pr])
        p.wait("scalar", "s1f")
        p.op("scalar", "activation", ht[pr], xni, AF.Identity, scale=st[:, 6:7], sem="s1hx")
        done_x_s[pr] = p.cnt["s1xn"]
        done_xn[pr] = p.cnt[f"s1stf{pr}"]
        done_st[pr] = p.cnt["s1hx"]
        if it >= NB:
            p.wait("vector", f"s1sth{pr}", done_hb[pr])
        p.wait("vector", "s1hx")
        p.op("vector", "tensor_tensor", ht2[pr], ht[pr], gm, op=ALU.mult, sem="s1hg")
        p.wait("vector", "s1hg")
        p.op("vector", "tensor_tensor", hbi, ht2[pr], bt, op=ALU.add, sem="s1hb")
        done_ht[pr] = p.cnt["s1hb"]
        p.wait("gpsimd", "s1hb")
        p.dma("gpsimd", h_b[b, r * 128:(r + 1) * 128, :], hbi, f"s1sth{pr}")
        done_hb[pr] = p.cnt[f"s1sth{pr}"]

    trb = [ctx.sbuf(f"s1_tr{i}", (128, T), BF16) for i in range(2)]
    done_tr = [0, 0]
    for nm in ("s1stx0", "s1stx1", "s1stx2", "s1sth0", "s1sth1", "s1sth2"):
        p.wait("sync", nm)
    i = 0
    for b in range(BPC):
        for src, dst in ((xn_b, xn_fm), (h_b, h_fm)):
            for ki in range(8):
                pr = i % 2
                if i >= 2:
                    p.wait("sync", f"s1trs{pr}", done_tr[pr])
                ldv = p.dmat("sync", trb[pr], src[b, :, ki * 128:(ki + 1) * 128], f"s1trl{pr}")
                p.wait("scalar", f"s1trl{pr}", ldv)
                p.dma("scalar", dst[b, ki * 128:(ki + 1) * 128, :], trb[pr], f"s1trs{pr}")
                done_tr[pr] = p.cnt[f"s1trs{pr}"]
                i += 1


def stage2(ctx):
    p = ctx.p
    xn_fm = ctx.d("xn_fm_bf", (BPC, D, T), BF16, "s1", ("s2",))
    xn_sh = ctx.d("xn_tm_shift", (BPC, T, D), BF16, "s1", ("s2",))
    og_d = ctx.d("og_tm_bf", (BPC, T, D), BF16, "s2", ("s2",))
    sproj_d = ctx.d("sproj_fm_bf", (BPC, D, T), BF16, "s2", ("s4",))
    a_fm_d = ctx.d("a_fm_hbm", (BPC, H, T), F32, "s2", ("s2",))
    lam_fm_d = ctx.d("lam_fm_hbm", (BPC, H, NCHUNK), F32, "s2", ("s2",))

    wgba = ctx.sbuf("s2_wgba", (128, 8 * 48), BF16)
    p.dma("gpsimd", wgba.rearrange("q (ki c) -> q ki c", ki=8),
          ctx.d("w0_gba_T", (D, 48), F32, None, ()).rearrange("(ki q) c -> q ki c", q=128),
          "cldg")
    wwT = ctx.sbuf("s2_wwT", (128, 8 * 1024), BF16)
    p.dma("gpsimd", wwT.rearrange("q (ki c) -> q ki c", ki=8),
          ctx.d("w0_write_T", (D, D), F32, None, ()).rearrange("(ki q) c -> q ki c", q=128),
          "cldg")
    woT = ctx.sbuf("s2_woT", (128, 8 * 1024), BF16)
    p.dma("gpsimd", woT.rearrange("q (ki c) -> q ki c", ki=8),
          ctx.d("w0_out_T", (D, D), F32, None, ()).rearrange("(ki q) c -> q ki c", q=128),
          "cldg")
    alog = load_const(ctx, "A_log_bc", (128, 16))
    dtb = load_const(ctx, "dt_bias_bc", (128, 16))
    ut2 = load_const(ctx, "ut2", (128, 128))
    sl2 = load_const(ctx, "sl2", (128, 128))
    al2 = load_const(ctx, "al2", (128, 128))
    halfsel = load_const(ctx, "halfsel", (2, 128))
    maskPenT = load_const(ctx, "maskPenT", (64, 64))
    ident64 = load_const(ctx, "ident64", (64, 64))
    ones_row = load_const(ctx, "ones_row", (1, 128))
    identf = load_const(ctx, "ident_f32", (128, 128))

    nea = ctx.sbuf("s2_nea", (128, 16), F32)
    nea0 = ctx.sbuf("s2_nea0", (128, 16), F32)
    p.wait("scalar", "cld")
    p.wait("scalar", "cldg")
    p.wait("scalar", "cld_g")
    p.op("scalar", "activation", nea0, alog, AF.Exp, sem="d_s")
    p.wait("vector", "d_s")
    p.op("vector", "tensor_scalar_mul", nea, nea0, -1.0, sem="d_nea")

    gate_b = ctx.sbuf("s2_gate", (128, 256), F32)
    beta_b = ctx.sbuf("s2_beta", (128, 256), F32)
    decr_b = ctx.sbuf("s2_decr", (128, 256), F32)
    a_b = ctx.sbuf("s2_a", (128, 256), F32)
    na_b = ctx.sbuf("s2_na", (128, 256), F32)
    dwe_b = ctx.sbuf("s2_dwe", (128, 256), F32)
    lam_b = ctx.sbuf("s2_lam", (128, 256), F32)
    eag_b = ctx.sbuf("s2_eag", (128, 256), F32)
    tmp16 = ctx.sbuf("s2_tmp16", (128, 16 * 7), F32)
    vb_b = ctx.sbuf("s2_vb", (128, 16 * 1024), BF16)
    Xf = [ctx.sbuf(f"s2_Xf{i}", (128, 8 * 128), BF16) for i in range(2)]
    aev = ctx.sbuf("s2_aev", (16, 128), F32)
    xnK = ctx.sbuf("s2_xnK", (128, 2049), BF16)
    xnT = ctx.sbuf("s2_xnT", (128, 16 * 128), BF16)
    af = [ctx.sbuf(f"s2_af{i}", (1, T), F32) for i in range(2)]
    lamf = ctx.sbuf("s2_lamf", (2, NCHUNK), F32)
    lam2 = ctx.sbuf("s2_lam2", (128, NCHUNK), F32)
    Esb = ctx.sbuf("s2_E", (128, 64), F32)
    J_all = ctx.sbuf("s2_J", (128, 32 * 64), BF16)
    wdw2 = ctx.sbuf("s2_wdw", (128, 128), BF16)
    S2 = [ctx.sbuf(f"s2_S{i}", (128, 64), F32) for i in range(2)]
    S2bf = [ctx.sbuf(f"s2_Sbf{i}", (128, 64), BF16) for i in range(2)]
    otmp = ctx.sbuf("s2_otmp", (128, 128), F32)
    og_hp = ctx.sbuf("s2_og", (128, 16 * 128), BF16)
    og2_hp = ctx.sbuf("s2_og2", (128, 16 * 128), BF16)
    ogf = ctx.sbuf("s2_ogf", (128, 8 * T), BF16)
    spj = [ctx.sbuf(f"s2_spj{i}", (128, 512), BF16) for i in range(2)]

    pS = ctx.psum("pS", (128, 512), F32)
    pA = ctx.psum("pA", (128, 512), F32)
    pB = ctx.psum("pB", (128, 512), F32)
    pC = ctx.psum("pC", (128, 512), F32)
    pD = ctx.psum("pD", (128, 512), F32)

    pE = ctx.psum("pE", (128, 512), F32)
    p48 = pS[:, 0:48]
    pc = pS[:, 64:112]
    pat = pS[0:16, 128:256]
    pL = pS[:, 256:288]
    pM = pC[:, 0:64]
    pB2 = pC[:, 64:128]
    pU = pE[:, 0:64]
    pOI = pD[:, 0:128]
    pOJ = pD[:, 128:256]

    p.wait("tensor", "cld")
    p.wait("tensor", "cldg")
    p.wait("tensor", "cld_g")
    p.wait("tensor", "d_nea")

    done_xf = [0, 0]
    done_p48_v = 0
    done_p48_s = 0
    done_pv = [0, 0]
    done_pc = 0
    done_pM = 0
    done_spj = [0, 0]

    for b in range(S2_NB):
        for tt in range(16):
            pr = tt % 2
            p.wait("tensor", "d_spev")
            p.wait("sync", "d_mmv", done_xf[pr])
            ldv = p.dma("sync", Xf[pr].rearrange("q (ki t) -> q ki t", ki=8),
                        xn_fm[b].rearrange("(ki q) t -> q ki t", q=128)[:, :, tt * 128:(tt + 1) * 128],
                        f"d_xf{pr}")
            p.wait("tensor", f"d_xf{pr}", ldv)
            p.wait("tensor", "d_v", done_p48_v)
            p.wait("tensor", "d_gb", done_p48_s)
            for ki in range(8):
                p.op("tensor", "matmul", p48, Xf[pr][:, ki * 128:(ki + 1) * 128],
                     wgba[:, ki * 48:(ki + 1) * 48], start=(ki == 0), stop=(ki == 7),
                     sem="d_mm48" if ki == 7 else None)
            p.wait("scalar", "d_mm48")
            p.op("scalar", "activation", gate_b[:, tt * 16:(tt + 1) * 16], p48[:, 0:16],
                 AF.Sigmoid)
            p.op("scalar", "activation", beta_b[:, tt * 16:(tt + 1) * 16], p48[:, 16:32],
                 AF.Sigmoid, sem="d_gb")
            done_p48_s = p.cnt["d_gb"]
            p.wait("vector", "d_mm48")
            p.op("vector", "tensor_tensor", tmp16[:, 0:16], p48[:, 32:48], dtb, op=ALU.add,
                 sem="d_v")
            done_p48_v = p.cnt["d_v"]
            p.wait("vector", "d_v")
            p.op("vector", "tensor_scalar_mul", tmp16[:, 16:32], tmp16[:, 0:16], -1.0, sem="d_v")
            p.wait("vector", "d_v")
            p.op("vector", "tensor_tensor", tmp16[:, 32:48], tmp16[:, 0:16], tmp16[:, 16:32],
                 op=ALU.min, sem="d_v")
            p.wait("scalar", "d_v")
            p.op("scalar", "activation", tmp16[:, 48:64], tmp16[:, 32:48], AF.Exp, sem="d_s")
            p.wait("scalar", "d_s")
            p.op("scalar", "activation", tmp16[:, 64:80], tmp16[:, 48:64], AF.Ln, bias=1.0,
                 sem="d_s")
            p.op("vector", "tensor_single_scalar", tmp16[:, 80:96], tmp16[:, 0:16], 0.0,
                 ALU.max, sem="d_v")
            p.wait("vector", "d_s")
            p.wait("vector", "d_v")
            p.op("vector", "tensor_tensor", tmp16[:, 96:112], tmp16[:, 80:96], tmp16[:, 64:80],
                 op=ALU.add, sem="d_v")
            p.wait("vector", "d_v")
            p.op("vector", "tensor_tensor", decr_b[:, tt * 16:(tt + 1) * 16], tmp16[:, 96:112],
                 nea, op=ALU.mult, sem="d_dec")

            for jh in range(2):
                pv = [pA, pB][jh]
                p.wait("tensor", "d_vev", done_pv[jh])
                for ki in range(8):
                    p.op("tensor", "matmul", pv, Xf[pr][:, ki * 128:(ki + 1) * 128],
                         wwT[:, ki * 1024 + jh * 512: ki * 1024 + jh * 512 + 512],
                         start=(ki == 0), stop=(ki == 7), sem="d_mmv" if ki == 7 else None)
                done_xf[pr] = p.cnt["d_mmv"]
                p.wait("scalar", "d_mmv")
                p.wait("scalar", "d_gb", done_p48_s)
                for h8 in range(8):
                    h = jh * 8 + h8
                    p.op("scalar", "activation",
                         vb_b[:, tt * 1024 + h * 64: tt * 1024 + h * 64 + 64],
                         pv[:, h8 * 64:(h8 + 1) * 64], AF.Identity,
                         scale=beta_b[:, tt * 16 + h: tt * 16 + h + 1],
                         sem="d_vev" if h8 == 7 else None)
                done_pv[jh] = p.cnt["d_vev"]

        for tt in range(16):
            cs = tt * 16
            p.wait("tensor", "d_dec")
            p.wait("tensor", "d_gb")
            p.wait("tensor", "d_vev")
            p.wait("tensor", "d_aev", done_pc)
            rhs = decr_b[:, cs:cs + 16]
            p.op("tensor", "matmul", pc[:, 0:16], ut2, rhs, start=True, stop=True)
            p.op("tensor", "matmul", pc[:, 16:32], sl2, rhs, start=True, stop=True)
            p.op("tensor", "matmul", pc[:, 32:48], al2, rhs, start=True, stop=True, sem="d_mmc")
            p.wait("scalar", "d_mmc")
            p.wait("scalar", "d_eag")
            p.op("scalar", "copy", a_b[:, cs:cs + 16], pc[:, 0:16])
            p.op("scalar", "activation", dwe_b[:, cs:cs + 16], pc[:, 16:32], AF.Exp)
            p.op("scalar", "activation", lam_b[:, cs:cs + 16], pc[:, 32:48], AF.Exp)
            p.op("scalar", "activation", tmp16[:, 96:112], pc[:, 0:16], AF.Exp, sem="d_s2")
            p.wait("vector", "d_s2")
            p.op("vector", "tensor_scalar_mul", na_b[:, cs:cs + 16], a_b[:, cs:cs + 16], -1.0)
            p.op("vector", "tensor_tensor", eag_b[:, cs:cs + 16], tmp16[:, 96:112],
                 gate_b[:, cs:cs + 16], op=ALU.mult, sem="d_eag")
            p.wait("tensor", "d_s2")
            p.op("tensor", "transpose", pat, a_b[:, cs:cs + 16], identf, sem="d_tra")
            p.wait("scalar", "d_tra")
            p.wait("scalar", "d_ast")
            p.op("scalar", "copy", aev, pat, sem="d_aev")
            done_pc = p.cnt["d_aev"]
            p.wait("sync", "d_aev")
            p.dma("sync", a_fm_d[b][:, tt * 128:(tt + 1) * 128], aev, "d_ast")
        p.wait("gpsimd", "d_s2")
        for par in range(2):
            p.dma("gpsimd",
                  lam_fm_d[b].rearrange("h (tt par) -> par tt h", par=2)[par:par + 1],
                  lam_b[par * 64:par * 64 + 1, :].rearrange("one (tt h) -> one tt h", h=16),
                  "d_lst", allow_slow_non_contiguous=True)

        if DEV_DUMP:
            p.wait("gpsimd", "d_eag")
            p.wait("gpsimd", "d_dec")
            p.wait("gpsimd", "d_vev")
            for nm, buf in (("dbg_dec", decr_b), ("dbg_a", a_b), ("dbg_dwe", dwe_b),
                            ("dbg_lam", lam_b), ("dbg_eag", eag_b), ("dbg_na", na_b),
                            ("dbg_gate", gate_b), ("dbg_beta", beta_b)):
                dd = ctx.d(nm, (BPC, 128, 256), F32, "s2", ("zz",))
                p.dma("gpsimd", dd[b], buf, "d_dbg")
            dv = ctx.d("dbg_vb", (BPC, 128, 16 * 1024), BF16, "s2", ("zz",))
            p.dma("gpsimd", dv[b], vb_b, "d_dbg")

        p.wait("sync", "d_ast")
        p.wait("sync", "d_lst")
        for hp in range(S2_HP if S2_DO_P3 else 0):
            h0g = hp * 2
            p.op("vector", "memset", xnK[:, 0:1], 0.0, sem="d_mz")
            p.wait("sync", "d_mz")
            p.wait("sync", "d_mmO")
            p.wait("sync", "d_mmU")
            l1 = p.dma("sync", xnK[:, 1:2049], xn_fm[b][hp * 128:(hp + 1) * 128, :], "d_hp")
            l2 = p.dma("sync", xnT.rearrange("q (tt j) -> q tt j", tt=16),
                       xn_sh[b].rearrange("(tt q) j -> q tt j", q=128)[:, :, hp * 128:(hp + 1) * 128],
                       "d_hp")
            l3 = p.dma("sync", af[0], a_fm_d[b][h0g:h0g + 1, :], "d_hp")
            l4 = p.dma("sync", af[1], a_fm_d[b][h0g + 1:h0g + 2, :], "d_hp")
            l5 = p.dma("sync", lamf, lam_fm_d[b][h0g:h0g + 2, :], "d_hp")
            p.wait("tensor", "d_hp", l5)
            p.wait("tensor", "d_aev")
            p.wait("scalar", "d_hp", l5)
            p.wait("scalar", "d_eag")
            p.op("tensor", "matmul", pL, halfsel, lamf, start=True, stop=True, sem="d_mmL")
            p.wait("vector", "d_mmL")
            p.op("vector", "tensor_copy", lam2, pL, sem="d_lam2")

            p.wait("vector", "d_mmO")
            for cp in range(16 if S2_PA else 0):
                for hh in range(2):
                    jslot = (cp * 2 + hh) * 64
                    p.wait("tensor", "d_Jev", done_pM)
                    p.wait("scalar", "d_Jev", done_pM)
                    for par in range(2):
                        i = cp * 2 + par
                        p.op("tensor", "matmul", pM[par * 64:(par + 1) * 64, :],
                             xnK[hh * 64:(hh + 1) * 64, i * 64: i * 64 + 64],
                             xnK[hh * 64:(hh + 1) * 64, 1 + i * 64: 1 + i * 64 + 64],
                             start=True, stop=True)
                        p.op("tensor", "matmul", pB2[par * 64:(par + 1) * 64, :],
                             ones_row[0:1, 0:64],
                             af[hh][0:1, i * 64: i * 64 + 64], start=True, stop=False)
                        p.op("tensor", "matmul", pB2[par * 64:(par + 1) * 64, :],
                             maskPenT, ident64, start=False, stop=True,
                             sem="d_mmJ" if par == 1 else None)
                    p.wait("scalar", "d_mmJ")
                    p.op("scalar", "activation", Esb, pB2, AF.Exp,
                         bias=na_b[:, cp * 16 + h0g + hh: cp * 16 + h0g + hh + 1], sem="d_E")
                    p.wait("vector", "d_E")
                    p.op("vector", "tensor_tensor", J_all[:, jslot:jslot + 64], Esb, pM,
                         op=ALU.mult, sem="d_Jev")
                    done_pM = p.cnt["d_Jev"]

            p.wait("vector", "d_ogst")
            p.op("vector", "memset", og2_hp[:, 0:128], 0.0, sem="d_og2z")
            p.op("vector", "memset", S2[1], 0.0, sem="d_v")
            p.wait("vector", "d_v")
            jv = p.cnt["d_Jev"]
            done_wdw = p.cnt.get("d_mmU", 0)
            done_pU = 0
            done_sbf = 0
            done_sbf_prev = 0
            done_mmO_prev = p.cnt.get("d_mmO", 0)
            done_pO = p.cnt.get("d_oev", 0)
            for i in range(NCHUNK if S2_PB else 0):
                par, cp = i % 2, i // 2
                cur, nxt = S2[(i + 1) % 2], S2[i % 2]
                sbf_cur = S2bf[i % 2]
                sbf_prev = S2bf[(i + 1) % 2]
                p.wait("scalar", "d_mmU", done_wdw)
                for hh in range(2):
                    h = h0g + hh
                    p.op("scalar", "activation",
                         wdw2[par * 64:(par + 1) * 64, hh * 64:(hh + 1) * 64],
                         xnT[par * 64:(par + 1) * 64,
                             cp * 128 + hh * 64: cp * 128 + hh * 64 + 64],
                         AF.Identity,
                         scale=dwe_b[par * 64:(par + 1) * 64, cp * 16 + h: cp * 16 + h + 1],
                         sem="d_wdw" if hh == 1 else None)
                p.wait("tensor", "d_wdw")
                p.wait("tensor", "d_Supd", done_pU)
                for hh in range(2 if S2_PB_U else 0):
                    h = h0g + hh
                    p.op("tensor", "matmul", pU[hh * 64:(hh + 1) * 64, :],
                         wdw2[par * 64:(par + 1) * 64, hh * 64:(hh + 1) * 64],
                         vb_b[par * 64:(par + 1) * 64,
                              cp * 1024 + h * 64: cp * 1024 + h * 64 + 64],
                         start=True, stop=True, sem="d_mmU" if hh == 1 else None)
                done_wdw = p.cnt["d_mmU"]
                p.wait("tensor", "d_oev", done_pO)
                for hh in range(2 if S2_PB_O else 0):
                    if i > 0 and not BISECT_OMM:
                        p.wait("tensor", "d_Sbf", done_sbf_prev)
                        p.op("tensor", "matmul",
                             pOI[par * 64:(par + 1) * 64, hh * 64:(hh + 1) * 64],
                             xnK[hh * 64:(hh + 1) * 64, 1 + i * 64: 1 + i * 64 + 64],
                             sbf_prev[hh * 64:(hh + 1) * 64, :], start=True, stop=True)
                    p.wait("tensor", "d_Jev", jv)
                    p.op("tensor", "matmul",
                         pOJ[par * 64:(par + 1) * 64, hh * 64:(hh + 1) * 64],
                         J_all[par * 64:(par + 1) * 64,
                               (cp * 2 + hh) * 64:(cp * 2 + hh) * 64 + 64],
                         vb_b[par * 64:(par + 1) * 64,
                              cp * 1024 + (h0g + hh) * 64: cp * 1024 + (h0g + hh) * 64 + 64],
                         start=True, stop=True, sem="d_mmO" if hh == 1 else None)
                done_mmO = p.cnt["d_mmO"]
                if not S2_PB_S:
                    continue
                p.wait("vector", "d_mmU")
                p.wait("vector", "d_Sbf", done_sbf)
                p.wait("vector", "d_lam2")
                p.op("vector", "scalar_tensor_tensor", nxt, cur, lam2[:, i:i + 1], pU,
                     op0=ALU.mult, op1=ALU.add, sem="d_Supd")
                done_pU = p.cnt["d_Supd"]
                p.wait("scalar", "d_Supd")
                p.wait("scalar", "d_mmO", done_mmO_prev)
                p.op("scalar", "copy", sbf_cur, nxt, sem="d_Sbf")
                done_sbf_prev = done_sbf
                done_sbf = p.cnt["d_Sbf"]
                done_mmO_prev = done_mmO
                p.wait("scalar", "d_mmO")
                p.wait("scalar", "d_oev", done_pO)
                for hh in range(2 if (S2_PB_O and S2_PB_EV) else 0):
                    h = h0g + hh
                    gcol = gate_b[par * 64:(par + 1) * 64, cp * 16 + h: cp * 16 + h + 1]
                    ecol = eag_b[par * 64:(par + 1) * 64, cp * 16 + h: cp * 16 + h + 1]
                    dst = og_hp[par * 64:(par + 1) * 64,
                                cp * 128 + hh * 64: cp * 128 + hh * 64 + 64]
                    dst2 = og2_hp[par * 64:(par + 1) * 64,
                                  cp * 128 + hh * 64: cp * 128 + hh * 64 + 64]
                    p.op("scalar", "activation", dst,
                         pOJ[par * 64:(par + 1) * 64, hh * 64:(hh + 1) * 64],
                         AF.Identity, scale=gcol,
                         sem=None if i > 0 else ("d_oev" if hh == 1 else None))
                    if i > 0:
                        p.op("scalar", "activation", dst2,
                             pOI[par * 64:(par + 1) * 64, hh * 64:(hh + 1) * 64],
                             AF.Identity, scale=ecol,
                             sem="d_oev" if hh == 1 else None)
                done_pO = p.cnt["d_oev"]

            if DEV_DUMP and b == 0:
                dj = ctx.d("dbg_J", (8, 128, 32 * 64), BF16, "s2", ("zz",))
                p.wait("gpsimd", "d_Jev")
                p.dma("gpsimd", dj[hp], J_all, "d_dbg")
            if not S2_OG:
                continue
            p.wait("gpsimd", "d_oev")
            v1 = p.dma("gpsimd",
                  og_d[b].rearrange("(tt q) j -> q tt j", q=128)[:, :, hp * 128:(hp + 1) * 128],
                  og_hp.rearrange("q (tt c) -> q tt c", tt=16), "d_ogst")
            p.wait("gpsimd", "d_ogst", v1)
            p.wait("gpsimd", "d_og2z")
            p.dma("gpsimd",
                  og_d[b].rearrange("(tt q) j -> q tt j", q=128)[:, :, hp * 128:(hp + 1) * 128],
                  og2_hp.rearrange("q (tt c) -> q tt c", tt=16), "d_ogst",
                  accum_op=ALU.add)

        if not S2_DO_S2B:
            continue
        p.wait("sync", "d_ogst")
        p.wait("sync", "d_mmS")
        for ki in range(8):
            p.dmat("sync", ogf[:, ki * T:(ki + 1) * T], og_d[b][:, ki * 128:(ki + 1) * 128],
                   "d_ogl")
        lv = p.cnt["d_ogl"]
        for ts_ in range(4):
            for mj in range(8):
                po = [pA, pB][mj % 2]
                spr = spj[mj % 2]
                p.wait("tensor", f"d_spst{mj % 2}", done_spj[mj % 2])
                p.wait("tensor", "d_ogl", lv)
                p.wait("tensor", "d_vev")
                for ki in range(8):
                    p.op("tensor", "matmul", po,
                         woT[:, ki * 1024 + mj * 128: ki * 1024 + mj * 128 + 128],
                         ogf[:, ki * T + ts_ * 512: ki * T + ts_ * 512 + 512],
                         start=(ki == 0), stop=(ki == 7), sem="d_mmS" if ki == 7 else None)
                p.wait("vector", "d_mmS")
                p.op("vector", "tensor_copy", spr, po, sem="d_spev")
                p.wait("gpsimd", "d_spev")
                p.dma("gpsimd", sproj_d[b][mj * 128:(mj + 1) * 128, ts_ * 512:(ts_ + 1) * 512],
                      spr, f"d_spst{mj % 2}")
                done_spj[mj % 2] = p.cnt[f"d_spst{mj % 2}"]


def stage3(ctx):
    p = ctx.p
    h_fm = ctx.d("h_fm_bf", (BPC, D, T), BF16, "s1", ("s3",))
    acc0 = ctx.d("acc0_fm", (BPC, D, T), F32, "s3", ("s3",))
    g0_d = ctx.d("g0_bf", (BPC, D, T), BF16, "s3", ("s4",))
    gates_d = ctx.d("gates0", (8, NT), F32, "s3", ("s3",))
    sw1_d = ctx.d("sw1", (2048, D), F32, None, ())
    sw2_d = ctx.d("sw2", (D, 2048), F32, None, ())
    ew1_d = ctx.d("ew1", (8, 2048, D), F32, None, ())
    ew2_d = ctx.d("ew2", (8, D, 2048), F32, None, ())
    rt_d = ctx.d("router_T", (D, 8), F32, None, ())

    identb = load_const(ctx, "ident_bf", (128, 128), BF16, sem="cld3")
    identf = load_const(ctx, "ident_f32", (128, 128), sem="cld3")
    ones_row = load_const(ctx, "ones_row", (1, 128), sem="cld3")

    rt = ctx.sbuf("s3_rt", (128, 8 * 8), BF16)
    p.dma("gpsimd", rt.rearrange("q (ki c) -> q ki c", ki=8),
          rt_d.rearrange("(ki q) c -> q ki c", q=128), "cld3g")

    pS = ctx.psum("pS", (128, 512), F32)
    pA = ctx.psum("pA", (128, 512), F32)
    pB = ctx.psum("pB", (128, 512), F32)
    pC = ctx.psum("pC", (128, 512), F32)
    pT = ctx.psum("pT2", (128, 512), BF16)
    pT3 = ctx.psum("pT3", (128, 512), BF16)

    hbuf = [ctx.sbuf(f"s3_h{i}", (128, 8 * 512), BF16) for i in range(2)]
    h1b = ctx.sbuf("s3_h1big", (128, 16 * 512), BF16)
    wqA = ctx.sbuf("s3_wqA", (128, 128 * 128), BF16)
    wqB = ctx.sbuf("s3_wqB", (128, 128 * 128), BF16)
    wnat = [ctx.sbuf(f"s3_wn{i}", (128, 2048), F32) for i in range(2)]
    qst = ctx.sbuf("s3_qst", (128, 8), F32)
    qt1 = ctx.sbuf("s3_qt1", (128, 2048), F32)
    qt2 = ctx.sbuf("s3_qt2", (128, 2048), F32)
    qt3 = ctx.sbuf("s3_qt3", (128, 2048), F32)
    qr = ctx.sbuf("s3_qr", (128, 2048), BF16)
    lg = ctx.sbuf("s3_lg", (8, 512), F32)
    lt = ctx.sbuf("s3_lt", (128, 8), F32)
    rst = ctx.sbuf("s3_rst", (128, 8), F32)
    exr = ctx.sbuf("s3_exr", (128, 8), F32)
    gtm = ctx.sbuf("s3_gtm", (128, 8), F32)
    gtm0 = ctx.sbuf("s3_gtm0", (128, 8), F32)
    grow = ctx.sbuf("s3_grow", (1, 512), F32)
    gbc = ctx.sbuf("s3_gbc", (128, 512), F32)
    ev = [ctx.sbuf(f"s3_ev{i}", (128, 512), F32) for i in range(2)]
    gl = [ctx.sbuf(f"s3_gl{i}", (128, 512), F32) for i in range(2)]
    gq = [ctx.sbuf(f"s3_gq{i}", (128, 512), BF16) for i in range(2)]

    p.wait("tensor", "cld3")
    p.wait("tensor", "cld3g")
    p.wait("tensor", "cld3_g")

    dn = {"hb": [0, 0], "ff": [0, 0], "ev": [0, 0], "evd": [0, 0],
          "acc_prev": 0, "gbc": 0, "gbc_war": 0, "h1": 0, "wn": [0, 0], "accb": [], "gmm": 0,
          "qr": 0, "plg": 0, "ptr": 0, "gst": 0}

    for gtt in range(16 if S3_ROUTER else 0):
        b, ts_ = divmod(gtt, 4)
        pr = gtt % 2
        p.wait("sync", "f_mm", dn["hb"][pr])
        ldv = p.dma("sync", hbuf[pr].rearrange("q (ki t) -> q ki t", ki=8),
                    h_fm[b].rearrange("(ki q) t -> q ki t", q=128)[:, :, ts_ * 512:(ts_ + 1) * 512],
                    f"f_hl{pr}")
        p.wait("tensor", f"f_hl{pr}", ldv)
        p.wait("tensor", "f_lg", dn["plg"])
        p.wait("tensor", "f_ltc", dn["ptr"])
        for ki in range(8):
            p.op("tensor", "matmul", pS[0:8, 0:512], rt[:, ki * 8:(ki + 1) * 8],
                 hbuf[pr][:, ki * 512:(ki + 1) * 512], start=(ki == 0), stop=(ki == 7),
                 sem="f_mm" if ki == 7 else None)
        dn["hb"][pr] = p.cnt["f_mm"]
        p.wait("scalar", "f_mm")
        p.op("scalar", "copy", lg, pS[0:8, 0:512], sem="f_lg")
        dn["plg"] = p.cnt["f_lg"]
        for j in range(4):
            p.wait("tensor", "f_lg", dn["plg"])
            p.wait("tensor", "f_ltc", dn["ptr"])
            p.op("tensor", "transpose", pS[0:128, 0:8], lg[:, j * 128:(j + 1) * 128],
                 identf[0:8, 0:8], sem="f_tr")
            p.wait("scalar", "f_tr")
            p.wait("scalar", "f_gdone")
            p.op("scalar", "copy", lt, pS[0:128, 0:8], sem="f_ltc")
            dn["ptr"] = p.cnt["f_ltc"]
            p.wait("vector", "f_ltc")
            p.op("vector", "reduce_max", rst[:, 0:1], lt, AX.X, sem="f_v")
            p.wait("vector", "f_v")
            p.op("vector", "tensor_scalar", gtm0, lt, rst[:, 0:1], None, op0=ALU.is_ge)
            p.op("vector", "tensor_scalar_mul", rst[:, 1:2], rst[:, 0:1], -1.0, sem="f_v")
            p.wait("scalar", "f_v")
            p.op("scalar", "activation", exr, lt, AF.Exp,
                 bias=rst[:, 1:2], accum_out=rst[:, 3:4], sem="f_s")
            p.wait("vector", "f_s")
            p.op("vector", "reciprocal", rst[:, 4:5], rst[:, 3:4], sem="f_v")
            p.wait("vector", "f_v")
            p.wait("vector", "f_gst", dn["gst"])
            p.op("vector", "tensor_scalar", gtm, gtm0, rst[:, 4:5], None, op0=ALU.mult,
                 sem="f_gdone")
            p.wait("gpsimd", "f_gdone")
            tr0 = gtt * 512 + j * 128
            p.dma("gpsimd", gates_d[:, tr0:tr0 + 128].rearrange("e t -> t e"), gtm,
                  "f_gst", allow_slow_non_contiguous=True)
            dn["gst"] = p.cnt["f_gst"]

    def quant_matrix(w_d, RR, CC, dest, slot_of):
        for fr in range(RR // 128):
            pr = fr % 2
            wn = wnat[pr]
            p.wait("sync", "q_done", dn["wn"][pr])
            lv = p.dma("sync", wn[:, 0:CC], w_d[fr * 128:(fr + 1) * 128, :], f"q_ld{pr}")
            if S3_QL < 2:
                continue
            p.wait("vector", f"q_ld{pr}", lv)
            p.op("vector", "reduce_sum", qst[:, 0:1], wn[:, 0:CC], AX.X,
                 apply_absolute_value=True, sem="q_v")
            p.wait("vector", "q_v")
            p.op("vector", "tensor_scalar_mul", qst[:, 1:2], qst[:, 0:1], 1.0 / CC, sem="q_v")
            p.wait("vector", "q_v")
            p.op("vector", "tensor_scalar_max", qst[:, 2:3], qst[:, 1:2], 1e-5, sem="q_v")
            p.wait("vector", "q_v")
            p.op("vector", "tensor_scalar_mul", qst[:, 3:4], qst[:, 2:3], 0.5)
            p.op("vector", "tensor_scalar_mul", qst[:, 4:5], qst[:, 2:3], -0.5, sem="q_v")
            p.wait("vector", "q_v")
            p.op("vector", "tensor_scalar", qt1[:, 0:CC], wn[:, 0:CC], qst[:, 3:4], None,
                 op0=ALU.is_gt)
            p.op("vector", "tensor_scalar", qt2[:, 0:CC], wn[:, 0:CC], qst[:, 4:5], None,
                 op0=ALU.is_lt, sem="q_v")
            p.wait("vector", "q_v")
            p.op("vector", "tensor_tensor", qt3[:, 0:CC], qt1[:, 0:CC], qt2[:, 0:CC],
                 op=ALU.subtract, sem="q_v")
            p.wait("vector", "q_v")
            p.wait("vector", "q_tr", dn["qr"])
            p.op("vector", "tensor_scalar", qr[:, 0:CC], qt3[:, 0:CC], qst[:, 2:3], None,
                 op0=ALU.mult, sem="q_done")
            dn["wn"][pr] = p.cnt["q_done"]
            if S3_QL < 3:
                continue
            p.wait("tensor", "q_done")
            for sub in range(CC // 128):
                p.wait("tensor", "q_ev", max(0, p.cnt.get("q_ev", 0) - 1))
                k = p.cnt.get("q_ev", 0)
                ptk = [pT, pT3][k % 2][:, ((k // 2) % 2) * 128:((k // 2) % 2) * 128 + 128]
                p.op("tensor", "transpose", ptk,
                     qr[:, sub * 128:(sub + 1) * 128], identb, sem="q_tr")
                p.wait("vector", "q_tr")
                slot = slot_of(fr, sub)
                p.op("vector", "tensor_copy", dest[:, slot * 128:(slot + 1) * 128],
                     ptk, sem="q_ev")
            dn["qr"] = p.cnt["q_tr"]
        p.wait("tensor", "q_ev")

    def ffn_pass(e):
        for gtt in range(16):
            b, ts_ = divmod(gtt, 4)
            pr = gtt % 2
            p.wait("sync", "f_mm", dn["hb"][pr])
            ldv = p.dma("sync", hbuf[pr].rearrange("q (ki t) -> q ki t", ki=8),
                        h_fm[b].rearrange("(ki q) t -> q ki t", q=128)
                        [:, :, ts_ * 512:(ts_ + 1) * 512], f"f_hl{pr}")
            if e >= 0:
                p.wait("sync", "f_mmg", dn["gmm"])
                glv = p.dma("sync", grow, gates_d[e:e + 1, gtt * 512:(gtt + 1) * 512], "f_gl")
                p.wait("tensor", "f_gl", glv)
                p.wait("tensor", "f_gbc", dn["gbc"])
                p.op("tensor", "matmul", pC[:, 0:512], ones_row[0:1, 0:128],
                     grow, start=True, stop=True, sem="f_mmg")
                dn["gmm"] = p.cnt["f_mmg"]
                p.wait("scalar", "f_mmg")
                p.wait("scalar", "f_ev", dn["gbc_war"])
                p.op("scalar", "copy", gbc, pC[:, 0:512], sem="f_gbc")
                dn["gbc"] = p.cnt["f_gbc"]
            p.wait("tensor", f"f_hl{pr}", ldv)
            p.wait("scalar", "f_mm2", dn["h1"])
            for fj in range(16):
                pp = [pA, pB][fj % 2]
                p.wait("tensor", "f_h1ev", dn["ff"][fj % 2])
                p.wait("tensor", "f_ev", dn["ev"][fj % 2])
                for ki in range(8):
                    p.op("tensor", "matmul", pp,
                         wqA[:, (ki * 16 + fj) * 128:(ki * 16 + fj) * 128 + 128],
                         hbuf[pr][:, ki * 512:(ki + 1) * 512], start=(ki == 0), stop=(ki == 7),
                         sem="f_mm" if ki == 7 else None)
                p.wait("scalar", "f_mm")
                p.op("scalar", "activation", h1b[:, fj * 512:(fj + 1) * 512], pp, AF.Silu,
                     sem="f_h1ev")
                dn["ff"][fj % 2] = p.cnt["f_h1ev"]
            dn["hb"][pr] = p.cnt["f_mm"]
            h1base = p.cnt["f_h1ev"] - 16
            for mj in range(8):
                pp = [pA, pB][mj % 2]
                p.wait("tensor", "f_ev", dn["ev"][mj % 2])
                for fk in range(16):
                    if mj == 0:
                        p.wait("tensor", "f_h1ev", h1base + fk + 1)
                    p.op("tensor", "matmul", pp,
                         wqB[:, (fk * 8 + mj) * 128:(fk * 8 + mj) * 128 + 128],
                         h1b[:, fk * 512:(fk + 1) * 512], start=(fk == 0), stop=(fk == 15),
                         sem="f_mm2" if fk == 15 else None)
                evb = ev[mj % 2]
                p.wait("scalar", "f_mm2")
                p.wait("scalar", f"f_acst{mj % 2}", dn["evd"][mj % 2])
                if e < 0:
                    p.op("scalar", "copy", evb, pp, sem="f_ev")
                else:
                    glb = gl[mj % 2]
                    p.op("scalar", "copy", glb, pp, sem="f_ev0")
                    p.wait("vector", "f_ev0")
                    p.wait("vector", "f_gbc", dn["gbc"])
                    p.wait("vector", f"f_acst{mj % 2}", dn["evd"][mj % 2])
                    p.op("vector", "tensor_tensor", evb, glb, gbc, op=ALU.mult, sem="f_ev")
                dn["ev"][mj % 2] = p.cnt["f_ev"]
                p.wait("gpsimd", "f_ev")
                if e >= 0 and dn["accb"]:
                    for nm, vv in dn["accb"]:
                        p.wait("gpsimd", nm, vv)
                    dn["accb"] = []
                p.dma("gpsimd", acc0[b][mj * 128:(mj + 1) * 128, ts_ * 512:(ts_ + 1) * 512],
                      evb, f"f_acst{mj % 2}", **({} if e < 0 else {"accum_op": ALU.add}))
                dn["evd"][mj % 2] = p.cnt[f"f_acst{mj % 2}"]
            dn["h1"] = p.cnt["f_mm2"]
            dn["gbc_war"] = p.cnt["f_ev"]
        dn["accb"] = [("f_acst0", p.cnt.get("f_acst0", 0)), ("f_acst1", p.cnt.get("f_acst1", 0))]

    quant_matrix(sw1_d, 2048, D, wqA, lambda fr, sub: sub * 16 + fr)
    quant_matrix(sw2_d, D, 2048, wqB, lambda fr, sub: sub * 8 + fr)
    p.wait("sync", "f_gst")
    if S3_SHARED:
        ffn_pass(-1)
    for e in range(S3_NEXP):
        p.wait("sync", "f_mm")
        p.wait("sync", "f_mm2")
        quant_matrix(ew1_d[e], 2048, D, wqA, lambda fr, sub: sub * 16 + fr)
        quant_matrix(ew2_d[e], D, 2048, wqB, lambda fr, sub: sub * 8 + fr)
        ffn_pass(e)

    p.wait("sync", "f_acst0")
    p.wait("sync", "f_acst1")
    if not S3_G0:
        return
    dng = [0, 0]
    dng2 = [0, 0]
    for b in range(BPC):
        for pt_ in range(8):
            for ts_ in range(4):
                pr = ts_ % 2
                p.wait("sync", "g_sil", dng[pr])
                lv = p.dma("sync", gl[pr], acc0[b][pt_ * 128:(pt_ + 1) * 128,
                                                   ts_ * 512:(ts_ + 1) * 512], f"g_ld{pr}")
                p.wait("scalar", f"g_ld{pr}", lv)
                p.wait("scalar", f"g_st{pr}", dng2[pr])
                p.op("scalar", "activation", gq[pr], gl[pr], AF.Silu, sem="g_sil")
                dng[pr] = p.cnt["g_sil"]
                p.wait("gpsimd", "g_sil")
                p.dma("gpsimd", g0_d[b][pt_ * 128:(pt_ + 1) * 128, ts_ * 512:(ts_ + 1) * 512],
                      gq[pr], f"g_st{pr}")
                dng2[pr] = p.cnt[f"g_st{pr}"]


def stage4(ctx):
    p = ctx.p
    x_tm = ctx.d("x_tm", (BPC, T, D), F32, None, ())
    xn_f = ctx.d("xn_tm_f32", (BPC, T, D), F32, "s1", ("s4",))
    sproj_d = ctx.d("sproj_fm_bf", (BPC, D, T), BF16, "s2", ("s4",))
    g0_d = ctx.d("g0_bf", (BPC, D, T), BF16, "s3", ("s4",))
    x1_d = ctx.d("x1_tm", (BPC, T, D), F32, "s4", ("s8",))
    xn2b_d = ctx.d("xn2_tm_bf", (BPC, T, D), BF16, "s4", ("s4",))
    xn2fm_d = ctx.d("xn2_fm_bf", (BPC, D, T), BF16, "s4", ("s5", "s7"))
    wsel_d = ctx.d("wsel8", (8, NT), F32, "s4", ("s7",))

    NB = 3
    xt = [ctx.sbuf(f"s4_x{i}", (128, D), F32) for i in range(NB)]
    xnt = [ctx.sbuf(f"s4_xn{i}", (128, D), F32) for i in range(NB)]
    spt = [ctx.sbuf(f"s4_sp{i}", (128, D), BF16) for i in range(NB)]
    g0t = [ctx.sbuf(f"s4_g0{i}", (128, D), BF16) for i in range(NB)]
    sb_ = [ctx.sbuf(f"s4_s{i}", (128, D), F32) for i in range(NB)]
    sgt = [ctx.sbuf(f"s4_sg{i}", (128, D), F32) for i in range(NB)]
    x1t = [ctx.sbuf(f"s4_x1{i}", (128, D), F32) for i in range(NB)]
    bns = [ctx.sbuf(f"s4_bn{i}", (128, 12), F32) for i in range(NB)]
    sts = [ctx.sbuf(f"s4_st{i}", (128, 16), F32) for i in range(NB)]
    xn2t = [ctx.sbuf(f"s4_x2n{i}", (128, D), F32) for i in range(NB)]
    xn2b = [ctx.sbuf(f"s4_x2b{i}", (128, D), BF16) for i in range(NB)]
    s01 = ctx.sbuf("s4_s01", (128, 64), F32)
    hshA = ctx.sbuf("s4_hA", (128, 64), F32)
    hshB = ctx.sbuf("s4_hB", (128, 64), F32)
    hshC = ctx.sbuf("s4_hC", (128, 64), F32)
    hshD = ctx.sbuf("s4_hD", (128, 64), F32)
    wrow = ctx.sbuf("s4_wrow", (128, 64), F32)
    wrow2 = ctx.sbuf("s4_wrow2", (128, 64), F32)

    dx = [0] * NB
    dxn = [0] * NB
    dsp = [0] * NB
    dg0 = [0] * NB
    dst_ = [0] * NB
    dx1 = [0] * NB
    dx2b = [0] * NB
    ds01 = 0
    EPS = 1e-5

    for it in range(NT // 128):
        b, r = divmod(it, T // 128)
        pr = it % NB
        t0 = r * 128
        if it >= NB:
            p.wait("sync", "c_v", dx[pr])
        lx = p.dma("sync", xt[pr], x_tm[b, t0:t0 + 128, :], f"c_lx{pr}")
        ln = p.dma("sync", xnt[pr], xn_f[b, t0:t0 + 128, :], f"c_ln{pr}")
        lsp = p.dmat("sync", spt[pr], sproj_d[b][:, t0:t0 + 128], f"c_lsp{pr}")
        lg0 = p.dmat("sync", g0t[pr], g0_d[b][:, t0:t0 + 128], f"c_lg0{pr}")
        p.wait("vector", f"c_ln{pr}", ln)
        p.wait("vector", f"c_lsp{pr}", lsp)
        p.wait("vector", f"c_lx{pr}", lx)
        p.wait("vector", f"c_lg0{pr}", lg0)
        if it >= NB:
            p.wait("vector", f"c_sxb{pr}", dx2b[pr])
            p.wait("vector", f"c_sx1{pr}", dx1[pr])
        p.op("vector", "tensor_tensor", sb_[pr], xnt[pr], spt[pr], op=ALU.add, sem="c_v")
        p.wait("vector", "c_v")
        p.op("vector", "tensor_tensor", sgt[pr], sb_[pr], g0t[pr], op=ALU.mult, sem="c_v")
        p.wait("vector", "c_v")
        p.op("vector", "tensor_tensor", x1t[pr], xt[pr], sgt[pr], op=ALU.add, sem="c_v")
        dx[pr] = p.cnt["c_v"]
        p.wait("sync", "c_v")
        p.dma("sync", x1_d[b, t0:t0 + 128, :], x1t[pr], f"c_sx1{pr}")
        dx1[pr] = p.cnt[f"c_sx1{pr}"]

        bn, st = bns[pr], sts[pr]
        if it >= NB:
            p.wait("vector", "c_hx", dst_[pr])
        p.wait("vector", "c_v")
        p.op("vector", "bn_stats", bn[:, 0:6], x1t[pr][:, 0:512])
        p.op("vector", "bn_stats", bn[:, 6:12], x1t[pr][:, 512:1024], sem="c_v")
        p.wait("vector", "c_v")
        p.op("vector", "bn_aggr", st[:, 0:2], bn, sem="c_v")
        p.wait("vector", "c_v")
        p.op("vector", "tensor_scalar_add", st[:, 2:3], st[:, 1:2], EPS, sem="c_v")
        p.wait("scalar", "c_v")
        p.op("scalar", "activation", st[:, 3:4], st[:, 2:3], AF.Sqrt, sem="c_s")
        p.wait("vector", "c_s")
        p.op("vector", "reciprocal", st[:, 4:5], st[:, 3:4], sem="c_v")
        p.wait("vector", "c_v")
        p.op("vector", "tensor_scalar", st[:, 5:6], st[:, 0:1], st[:, 4:5], -1.0,
             op0=ALU.mult, op1=ALU.mult, sem="c_v")
        p.wait("scalar", "c_v")
        p.op("scalar", "activation", xn2t[pr], x1t[pr], AF.Identity,
             scale=st[:, 4:5], bias=st[:, 5:6], sem="c_hx")
        dst_[pr] = p.cnt["c_hx"]
        p.wait("vector", "c_hx")
        p.op("vector", "tensor_tensor", s01[:, it % 64:it % 64 + 1] if False else s01[:, it:it + 1] if it < 64 else None,
             xn2t[pr][:, 0:1], xn2t[pr][:, 1:2], op=ALU.add, sem="c_s01")
        p.op("vector", "tensor_copy", xn2b[pr], xn2t[pr], sem="s4xb")
        p.wait("gpsimd", "s4xb")
        p.dma("gpsimd", xn2b_d[b, t0:t0 + 128, :], xn2b[pr], f"c_sxb{pr}")
        dx2b[pr] = p.cnt[f"c_sxb{pr}"]

    p.wait("vector", "c_s01")
    def fence(name="h_v"):
        p.wait("vector", name)
    p.op("vector", "tensor_scalar_mul", hshA, s01, 0.125, sem="h_v"); fence()
    p.op("vector", "tensor_scalar", hshB, hshA, 1.0, None, op0=ALU.is_ge, sem="h_v"); fence()
    for k in range(2, 8):
        p.op("vector", "tensor_scalar", hshC, hshA, float(k), None, op0=ALU.is_ge, sem="h_v"); fence()
        p.op("vector", "tensor_tensor", hshD, hshB, hshC, op=ALU.add, sem="h_v"); fence()
        hshB, hshD = hshD, hshB
    for k in range(0, 8):
        p.op("vector", "tensor_scalar", hshC, hshA, float(-k), None, op0=ALU.is_lt, sem="h_v"); fence()
        p.op("vector", "tensor_tensor", hshD, hshB, hshC, op=ALU.subtract, sem="h_v"); fence()
        hshB, hshD = hshD, hshB
    p.op("vector", "scalar_tensor_tensor", hshA, hshB, -8.0, s01,
         op0=ALU.mult, op1=ALU.add, sem="h_v"); fence()
    p.op("vector", "tensor_scalar", hshB, hshA, 1.0, None, op0=ALU.is_ge, sem="h_v"); fence()
    for k in range(2, 8):
        p.op("vector", "tensor_scalar", hshC, hshA, float(k), None, op0=ALU.is_ge, sem="h_v"); fence()
        p.op("vector", "tensor_tensor", hshD, hshB, hshC, op=ALU.add, sem="h_v"); fence()
        hshB, hshD = hshD, hshB
    p.op("vector", "tensor_scalar", hshC, s01, 0.0, None, op0=ALU.is_lt, sem="h_v"); fence()
    p.op("vector", "tensor_tensor", hshA, hshB, hshC, op=ALU.add, sem="h_v"); fence()
    p.op("vector", "tensor_scalar", hshC, hshA, 8.0, None, op0=ALU.is_ge, sem="h_v"); fence()
    p.op("vector", "scalar_tensor_tensor", hshB, hshC, -8.0, hshA,
         op0=ALU.mult, op1=ALU.add, sem="h_v"); fence()
    dws = 0
    for e in range(8):
        p.op("vector", "tensor_scalar", hshC, hshB, float(e), None, op0=ALU.is_equal, sem="h_v"); fence()
        p.op("vector", "tensor_scalar", hshD, hshB, float((e + 7) % 8), None,
             op0=ALU.is_equal, sem="h_v"); fence()
        p.op("vector", "tensor_tensor", wrow, hshC, hshD, op=ALU.add, sem="h_v"); fence()
        p.wait("vector", "h_ws", dws)
        p.op("vector", "tensor_scalar_mul", wrow2, wrow, 0.5, sem="h_v"); fence()
        p.wait("gpsimd", "h_v")
        p.dma("gpsimd", wsel_d[e].rearrange("(tt q) -> q tt", q=128), wrow2, "h_ws",
              allow_slow_non_contiguous=True)
        dws = p.cnt["h_ws"]

    trb = [ctx.sbuf(f"s4_tr{i}", (128, T), BF16) for i in range(2)]
    done_tr = [0, 0]
    for nm in [f"c_sxb{i}" for i in range(NB)]:
        p.wait("sync", nm)
    i = 0
    for b in range(BPC):
        for ki in range(8):
            pr = i % 2
            if i >= 2:
                p.wait("sync", f"c_trs{pr}", done_tr[pr])
            ldv = p.dmat("sync", trb[pr], xn2b_d[b, :, ki * 128:(ki + 1) * 128], f"c_trl{pr}")
            p.wait("scalar", f"c_trl{pr}", ldv)
            p.dma("scalar", xn2fm_d[b, ki * 128:(ki + 1) * 128, :], trb[pr], f"c_trs{pr}")
            done_tr[pr] = p.cnt[f"c_trs{pr}"]
            i += 1


def stage5(ctx):
    p = ctx.p
    xn2fm_d = ctx.d("xn2_fm_bf", (BPC, D, T), BF16, "s4", ("s5", "s7"))
    q_d = ctx.d("q_fm", (BPC, D, T), BF16, "s5", ("s6",))
    k_d = ctx.d("k_fm", (BPC, 256, T), BF16, "s5", ("s6",))
    v_d = ctx.d("v1_tm", (BPC, T, 256), BF16, "s5", ("s6",))
    wq_d = ctx.d("wq_T", (D, D), F32, None, ())
    wk_d = ctx.d("wk_T", (D, 256), F32, None, ())
    wv_d = ctx.d("wv_T", (D, 256), F32, None, ())

    wq = ctx.sbuf("s5_wq", (128, 8 * 1024), BF16)
    wk = ctx.sbuf("s5_wk", (128, 8 * 256), BF16)
    wv = ctx.sbuf("s5_wv", (128, 8 * 256), BF16)
    p.dma("gpsimd", wq.rearrange("q (ki c) -> q ki c", ki=8),
          wq_d.rearrange("(ki q) c -> q ki c", q=128), "a_wl")
    p.dma("gpsimd", wk.rearrange("q (ki c) -> q ki c", ki=8),
          wk_d.rearrange("(ki q) c -> q ki c", q=128), "a_wl")
    p.dma("gpsimd", wv.rearrange("q (ki c) -> q ki c", ki=8),
          wv_d.rearrange("(ki q) c -> q ki c", q=128), "a_wl")

    pA = ctx.psum("pA", (128, 512), F32)
    pB = ctx.psum("pB", (128, 512), F32)
    hbuf = [ctx.sbuf(f"s3_h{i}", (128, 8 * 512), BF16) for i in range(2)]
    evq = [ctx.sbuf(f"s5_ev{i}", (128, 512), BF16) for i in range(2)]

    p.wait("tensor", "a_wl")
    dhb = [0, 0]
    dev = [0, 0]
    devd = [0, 0]
    for gtt in range(16):
        b, ts_ = divmod(gtt, 4)
        pr = gtt % 2
        p.wait("sync", "a_mm", dhb[pr])
        ldv = p.dma("sync", hbuf[pr].rearrange("q (ki t) -> q ki t", ki=8),
                    xn2fm_d[b].rearrange("(ki q) t -> q ki t", q=128)
                    [:, :, ts_ * 512:(ts_ + 1) * 512], f"a_hl{pr}")
        p.wait("tensor", f"a_hl{pr}", ldv)
        jobs = []
        for mj in range(8):
            jobs.append(("q", mj, q_d[b][mj * 128:(mj + 1) * 128, ts_ * 512:(ts_ + 1) * 512],
                         [wq[:, ki * 1024 + mj * 128: ki * 1024 + mj * 128 + 128] for ki in range(8)]))
        for mj in range(2):
            jobs.append(("k", mj, k_d[b][mj * 128:(mj + 1) * 128, ts_ * 512:(ts_ + 1) * 512],
                         [wk[:, ki * 256 + mj * 128: ki * 256 + mj * 128 + 128] for ki in range(8)]))
        for ji, (kind, mj, dram_out, wblocks) in enumerate(jobs):
            pp = [pA, pB][ji % 2]
            p.wait("tensor", "a_ev", dev[ji % 2])
            for ki in range(8):
                p.op("tensor", "matmul", pp, wblocks[ki],
                     hbuf[pr][:, ki * 512:(ki + 1) * 512], start=(ki == 0), stop=(ki == 7),
                     sem="a_mm" if ki == 7 else None)
            p.wait("scalar", "a_mm")
            p.wait("scalar", f"a_st{ji % 2}", devd[ji % 2])
            p.op("scalar", "copy", evq[ji % 2], pp, sem="a_ev")
            dev[ji % 2] = p.cnt["a_ev"]
            p.wait("gpsimd", "a_ev")
            p.dma("gpsimd", dram_out, evq[ji % 2], f"a_st{ji % 2}")
            devd[ji % 2] = p.cnt[f"a_st{ji % 2}"]
        for sub in range(4):
            ji = 10 + sub
            pp = [pA, pB][ji % 2]
            p.wait("tensor", "a_ev", dev[ji % 2])
            for ki in range(8):
                p.op("tensor", "matmul", pp[:, 0:256],
                     hbuf[pr][:, ki * 512 + sub * 128: ki * 512 + sub * 128 + 128],
                     wv[:, ki * 256:(ki + 1) * 256], start=(ki == 0), stop=(ki == 7),
                     sem="a_mm" if ki == 7 else None)
            p.wait("scalar", "a_mm")
            p.wait("scalar", f"a_st{ji % 2}", devd[ji % 2])
            p.op("scalar", "copy", evq[ji % 2][:, 0:256], pp[:, 0:256], sem="a_ev")
            dev[ji % 2] = p.cnt["a_ev"]
            p.wait("gpsimd", "a_ev")
            p.dma("gpsimd", v_d[b][ts_ * 512 + sub * 128: ts_ * 512 + sub * 128 + 128, :],
                  evq[ji % 2][:, 0:256], f"a_st{ji % 2}")
            devd[ji % 2] = p.cnt[f"a_st{ji % 2}"]
        dhb[pr] = p.cnt["a_mm"]


def stage6(ctx):
    p = ctx.p
    q_d = ctx.d("q_fm", (BPC, D, T), BF16, "s5", ("s6",))
    k_d = ctx.d("k_fm", (BPC, 256, T), BF16, "s5", ("s6",))
    v_d = ctx.d("v1_tm", (BPC, T, 256), BF16, "s5", ("s6",))
    o_d = ctx.d("o1_fm", (BPC, D, T), BF16, "s6", ("s6",))
    s2_d = ctx.d("s2_fm_bf", (BPC, D, T), BF16, "s6", ("s8",))
    am_d = ctx.d("amask", (4, 128, 512), F32, None, ())
    wo_d = ctx.d("wo_T", (D, D), F32, None, ())
    identb = load_const(ctx, "ident_bf", (128, 128), BF16, sem="at_c")

    amask = ctx.sbuf("s6_am", (128, 4 * 512), F32)
    p.dma("sync", amask.rearrange("q (m c) -> q m c", m=4),
          am_d.rearrange("m q c -> q m c"), "at_c")
    wo = ctx.sbuf("s5_wq", (128, 8 * 1024), BF16)
    p.dma("gpsimd", wo.rearrange("q (ki c) -> q ki c", ki=8),
          wo_d.rearrange("(ki q) c -> q ki c", q=128), "at_cg")

    pA = ctx.psum("pA", (128, 512), F32)
    pB = ctx.psum("pB", (128, 512), F32)
    pT = ctx.psum("pT2", (128, 512), BF16)
    pT3 = ctx.psum("pT3", (128, 512), BF16)
    pD = ctx.psum("pD", (128, 512), F32)

    qh = ctx.sbuf("s6_qh", (64, T), BF16)
    kg = ctx.sbuf("s6_kg", (64, T), BF16)
    vb = ctx.sbuf("s6_vb", (128, 16 * 64), BF16)
    Ssb = ctx.sbuf("s6_S", (128, T), F32)
    Psb = ctx.sbuf("s6_P", (128, T), BF16)
    P2sb = ctx.sbuf("s6_P2", (128, T), BF16)
    Pt = [ctx.sbuf(f"s6_Pt{i}", (128, 128), BF16) for i in range(2)]
    ob = ctx.sbuf("s6_ob", (64, T), BF16)
    sst = ctx.sbuf("s6_sst", (128, 8), F32)
    hbuf = [ctx.sbuf(f"s3_h{i}", (128, 8 * 512), BF16) for i in range(2)]
    evq = [ctx.sbuf(f"s5_ev{i}", (128, 512), BF16) for i in range(2)]

    p.wait("tensor", "at_c")
    p.wait("tensor", "at_cg")
    dsc = [0, 0]
    dptev = [0, 0]
    dob = 0
    dqh = 0
    for b in range(BPC):
        for h in range(16):
            g = h // 4
            p.wait("sync", "at_mm", dqh)
            p.wait("sync", "at_obst", p.cnt.get("at_obst", 0))
            l1 = p.dma("sync", qh, q_d[b][h * 64:(h + 1) * 64, :], "at_l")
            l2 = p.dma("sync", kg, k_d[b][g * 64:(g + 1) * 64, :], "at_l")
            l3 = p.dma("sync", vb.rearrange("q (tt e) -> q tt e", tt=16),
                       v_d[b].rearrange("(tt q) j -> q tt j", q=128)
                       [:, :, g * 64:(g + 1) * 64], "at_l")
            p.wait("tensor", "at_l", l3)
            for qt in range(16):
                nk5 = qt // 4 + 1
                L = nk5 * 512
                for kt in range(nk5):
                    pp = [pA, pB][kt % 2]
                    p.wait("tensor", "at_sev", dsc[kt % 2])
                    p.op("tensor", "matmul", pp, qh[:, qt * 128:(qt + 1) * 128],
                         kg[:, kt * 512:(kt + 1) * 512], start=True, stop=True, sem="at_mm")
                    p.wait("vector", "at_mm")
                    if kt == qt // 4:
                        p.op("vector", "scalar_tensor_tensor", Ssb[:, kt * 512:(kt + 1) * 512],
                             pp, 0.125, amask[:, (qt % 4) * 512:(qt % 4) * 512 + 512],
                             op0=ALU.mult, op1=ALU.add, sem="at_sev")
                    else:
                        p.op("vector", "tensor_scalar_mul", Ssb[:, kt * 512:(kt + 1) * 512],
                             pp, 0.125, sem="at_sev")
                    dsc[kt % 2] = p.cnt["at_sev"]
                p.wait("vector", "at_sev")
                p.op("vector", "reduce_max", sst[:, 0:1], Ssb[:, 0:L], AX.X,
                     negate=True, sem="at_v")
                p.wait("scalar", "at_v")
                p.op("scalar", "activation", Psb[:, 0:L], Ssb[:, 0:L], AF.Exp,
                     bias=sst[:, 0:1], accum_out=sst[:, 1:2], sem="at_s")
                p.wait("vector", "at_s")
                p.op("vector", "reciprocal", sst[:, 2:3], sst[:, 1:2], sem="at_v")
                p.wait("vector", "at_v")
                p.op("vector", "tensor_scalar", P2sb[:, 0:L], Psb[:, 0:L], sst[:, 2:3],
                     None, op0=ALU.mult, sem="at_p2")
                nkt = qt + 1
                p.wait("tensor", "at_p2")
                for ktt in range(nkt):
                    pr2 = ktt % 2
                    ptk = [pT, pT3][pr2][:, 0:128]
                    p.wait("tensor", "at_ptev", max(0, p.cnt.get("at_ptev", 0) - 1))
                    p.op("tensor", "transpose", ptk, P2sb[:, ktt * 128:(ktt + 1) * 128],
                         identb, sem="at_tr")
                    p.wait("scalar", "at_tr")
                    p.op("scalar", "copy", Pt[pr2], ptk, sem="at_ptev")
                    p.wait("tensor", "at_ptev")
                    p.op("tensor", "matmul", pD[0:64, 0:128], vb[:, ktt * 64:(ktt + 1) * 64],
                         Pt[pr2], start=(ktt == 0), stop=(ktt == nkt - 1),
                         sem="at_mmo" if ktt == nkt - 1 else None)
                p.wait("scalar", "at_mmo")
                p.op("scalar", "copy", ob[:, qt * 128:(qt + 1) * 128], pD[0:64, 0:128],
                     sem="at_oev")
            dqh = p.cnt["at_mm"]
            p.wait("gpsimd", "at_oev")
            p.dma("gpsimd", o_d[b][h * 64:(h + 1) * 64, :], ob, "at_obst")

    p.wait("sync", "at_obst")
    dhb = [0, 0]
    dev = [0, 0]
    devd = [0, 0]
    for gtt in range(16):
        b, ts_ = divmod(gtt, 4)
        pr = gtt % 2
        p.wait("sync", "at_mm", dhb[pr])
        ldv = p.dma("sync", hbuf[pr].rearrange("q (ki t) -> q ki t", ki=8),
                    o_d[b].rearrange("(ki q) t -> q ki t", q=128)
                    [:, :, ts_ * 512:(ts_ + 1) * 512], f"at_ol{pr}")
        p.wait("tensor", f"at_ol{pr}", ldv)
        for mj in range(8):
            pp = [pA, pB][mj % 2]
            p.wait("tensor", "at_ev2", dev[mj % 2])
            p.wait("tensor", "at_sev")
            for ki in range(8):
                p.op("tensor", "matmul", pp, wo[:, ki * 1024 + mj * 128: ki * 1024 + mj * 128 + 128],
                     hbuf[pr][:, ki * 512:(ki + 1) * 512], start=(ki == 0), stop=(ki == 7),
                     sem="at_mm" if ki == 7 else None)
            p.wait("scalar", "at_mm")
            p.wait("scalar", f"at_s2st{mj % 2}", devd[mj % 2])
            p.op("scalar", "copy", evq[mj % 2], pp, sem="at_ev2")
            dev[mj % 2] = p.cnt["at_ev2"]
            p.wait("gpsimd", "at_ev2")
            p.dma("gpsimd", s2_d[b][mj * 128:(mj + 1) * 128, ts_ * 512:(ts_ + 1) * 512],
                  evq[mj % 2], f"at_s2st{mj % 2}")
            devd[mj % 2] = p.cnt[f"at_s2st{mj % 2}"]
        dhb[pr] = p.cnt["at_mm"]


def stage7(ctx):
    p = ctx.p
    xn2fm_d = ctx.d("xn2_fm_bf", (BPC, D, T), BF16, "s4", ("s5", "s7"))
    acc1 = ctx.d("acc1_fm", (BPC, D, T), F32, "s7", ("s7",))
    g1_d = ctx.d("g1_bf", (BPC, D, T), BF16, "s7", ("s8",))
    wsel_d = ctx.d("wsel8", (8, NT), F32, "s4", ("s7",))
    W1_d = ctx.d("W1", (8, D, 4096), F32, None, ())
    W2_d = ctx.d("W2", (8, 4096, D), F32, None, ())
    b1_d = ctx.d("b1c", (8, 128, 32), F32, None, ())
    b2_d = ctx.d("b2c", (8, 128, 8), F32, None, ())
    ones_row = load_const(ctx, "ones_row", (1, 128), sem="l_c")

    pA = ctx.psum("pA", (128, 512), F32)
    pB = ctx.psum("pB", (128, 512), F32)
    pC = ctx.psum("pC", (128, 512), F32)
    hbuf = [ctx.sbuf(f"s3_h{i}", (128, 8 * 512), BF16) for i in range(2)]
    h1b = ctx.sbuf("s3_h1big", (128, 16 * 512), BF16)
    wqA = ctx.sbuf("s3_wqA", (128, 128 * 128), BF16)
    wqB = ctx.sbuf("s3_wqB", (128, 128 * 128), BF16)
    b1sb = ctx.sbuf("s7_b1", (128, 32), F32)
    b2sb = ctx.sbuf("s7_b2", (128, 8), F32)
    grow = ctx.sbuf("s3_grow", (1, 512), F32)
    gbc = ctx.sbuf("s3_gbc", (128, 512), F32)
    ev = [ctx.sbuf(f"s3_ev{i}", (128, 512), F32) for i in range(2)]
    gl = [ctx.sbuf(f"s3_gl{i}", (128, 512), F32) for i in range(2)]
    gq = [ctx.sbuf(f"s3_gq{i}", (128, 512), BF16) for i in range(2)]

    dn = {"hb": [0, 0], "ff": [0, 0], "ev": [0, 0], "evd": [0, 0],
          "gbc": 0, "gbc_war": 0, "h1": 0, "gmm": 0, "accb": [], "wl": 0, "bl": 0}

    p.wait("tensor", "l_c")
    for e in range(8):
        for ha in range(2):
            p.wait("gpsimd", "l_mm", dn["wl"])
            p.wait("gpsimd", "l_mm2", dn["wl2"] if "wl2" in dn else 0)
            w1v = p.dma("gpsimd", wqA.rearrange("q (ki f) -> q ki f", ki=8),
                        W1_d[e].rearrange("(ki q) f -> q ki f", q=128)
                        [:, :, ha * 2048:(ha + 1) * 2048], "l_w1")
            w2v = p.dma("gpsimd", wqB.rearrange("q (fk m) -> q fk m", fk=16),
                        W2_d[e][ha * 2048:(ha + 1) * 2048, :]
                        .rearrange("(fk q) m -> q fk m", q=128), "l_w2")
            if ha == 0:
                p.wait("sync", "l_h1ev", dn["bl"])
                p.wait("sync", "l_ev0", dn["bl2"] if "bl2" in dn else 0)
                b1v = p.dma("sync", b1sb, b1_d[e], "l_b1")
                b2v = p.dma("sync", b2sb, b2_d[e], "l_b1")
            p.wait("tensor", "l_w1", w1v)
            p.wait("tensor", "l_w2", w2v)
            p.wait("scalar", "l_b1")
            for gtt in range(16):
                b, ts_ = divmod(gtt, 4)
                pr = gtt % 2
                p.wait("sync", "l_mm", dn["hb"][pr])
                ldv = p.dma("sync", hbuf[pr].rearrange("q (ki t) -> q ki t", ki=8),
                            xn2fm_d[b].rearrange("(ki q) t -> q ki t", q=128)
                            [:, :, ts_ * 512:(ts_ + 1) * 512], f"l_hl{pr}")
                if True:
                    p.wait("sync", "l_mmg", dn["gmm"])
                    glv = p.dma("sync", grow, wsel_d[e:e + 1, gtt * 512:(gtt + 1) * 512],
                                "l_gl")
                    p.wait("tensor", "l_gl", glv)
                    p.wait("tensor", "l_gbc", dn["gbc"])
                    p.op("tensor", "matmul", pC[:, 0:512], ones_row[0:1, 0:128],
                         grow, start=True, stop=True, sem="l_mmg")
                    dn["gmm"] = p.cnt["l_mmg"]
                    p.wait("scalar", "l_mmg")
                    p.wait("scalar", "l_ev", dn["gbc_war"])
                    p.op("scalar", "copy", gbc, pC[:, 0:512], sem="l_gbc")
                    dn["gbc"] = p.cnt["l_gbc"]
                p.wait("tensor", f"l_hl{pr}", ldv)
                p.wait("scalar", "l_mm2", dn["h1"])
                for fj in range(16):
                    pp = [pA, pB][fj % 2]
                    p.wait("tensor", "l_h1ev", dn["ff"][fj % 2])
                    p.wait("tensor", "l_ev", dn["ev"][fj % 2])
                    for ki in range(8):
                        p.op("tensor", "matmul", pp,
                             wqA[:, (ki * 16 + fj) * 128:(ki * 16 + fj) * 128 + 128],
                             hbuf[pr][:, ki * 512:(ki + 1) * 512],
                             start=(ki == 0), stop=(ki == 7),
                             sem="l_mm" if ki == 7 else None)
                    p.wait("scalar", "l_mm")
                    p.op("scalar", "activation", h1b[:, fj * 512:(fj + 1) * 512], pp,
                         AF.Relu, bias=b1sb[:, ha * 16 + fj: ha * 16 + fj + 1],
                         sem="l_h1ev")
                    dn["ff"][fj % 2] = p.cnt["l_h1ev"]
                dn["hb"][pr] = p.cnt["l_mm"]
                h1base = p.cnt["l_h1ev"] - 16
                for mj in range(8):
                    pp = [pA, pB][mj % 2]
                    p.wait("tensor", "l_ev", dn["ev"][mj % 2])
                    for fk in range(16):
                        if mj == 0:
                            p.wait("tensor", "l_h1ev", h1base + fk + 1)
                        p.op("tensor", "matmul", pp,
                             wqB[:, (fk * 8 + mj) * 128:(fk * 8 + mj) * 128 + 128],
                             h1b[:, fk * 512:(fk + 1) * 512],
                             start=(fk == 0), stop=(fk == 15),
                             sem="l_mm2" if fk == 15 else None)
                    evb = ev[mj % 2]
                    glb = gl[mj % 2]
                    p.wait("scalar", "l_mm2")
                    p.wait("scalar", f"l_acst{mj % 2}", dn["evd"][mj % 2])
                    if ha == 1:
                        p.op("scalar", "activation", glb, pp, AF.Identity,
                             bias=b2sb[:, mj:mj + 1], sem="l_ev0")
                    else:
                        p.op("scalar", "copy", glb, pp, sem="l_ev0")
                    p.wait("vector", "l_ev0")
                    p.wait("vector", "l_gbc", dn["gbc"])
                    p.wait("vector", f"l_acst{mj % 2}", dn["evd"][mj % 2])
                    p.op("vector", "tensor_tensor", evb, glb, gbc, op=ALU.mult, sem="l_ev")
                    dn["ev"][mj % 2] = p.cnt["l_ev"]
                    p.wait("gpsimd", "l_ev")
                    if dn["accb"]:
                        for nm, vv in dn["accb"]:
                            p.wait("gpsimd", nm, vv)
                        dn["accb"] = []
                    first = (e == 0 and ha == 0)
                    p.dma("gpsimd", acc1[b][mj * 128:(mj + 1) * 128, ts_ * 512:(ts_ + 1) * 512],
                          evb, f"l_acst{mj % 2}",
                          **({} if first else {"accum_op": ALU.add}))
                    dn["evd"][mj % 2] = p.cnt[f"l_acst{mj % 2}"]
                dn["h1"] = p.cnt["l_mm2"]
                dn["gbc_war"] = p.cnt["l_ev"]
            dn["wl"] = p.cnt["l_mm"]
            dn["wl2"] = p.cnt["l_mm2"]
            if ha == 1:
                dn["bl"] = p.cnt["l_h1ev"]
                dn["bl2"] = p.cnt["l_ev0"]
            dn["accb"] = [("l_acst0", p.cnt.get("l_acst0", 0)),
                          ("l_acst1", p.cnt.get("l_acst1", 0))]

    p.wait("sync", "l_acst0")
    p.wait("sync", "l_acst1")
    dng = [0, 0]
    dng2 = [0, 0]
    for b in range(BPC):
        for pt_ in range(8):
            for ts_ in range(4):
                pr = ts_ % 2
                p.wait("sync", "l_sil", dng[pr])
                lv = p.dma("sync", gl[pr], acc1[b][pt_ * 128:(pt_ + 1) * 128,
                                                   ts_ * 512:(ts_ + 1) * 512], f"l_gld{pr}")
                p.wait("scalar", f"l_gld{pr}", lv)
                p.wait("scalar", f"l_gst{pr}", dng2[pr])
                p.op("scalar", "activation", gq[pr], gl[pr], AF.Silu, sem="l_sil")
                dng[pr] = p.cnt["l_sil"]
                p.wait("gpsimd", "l_sil")
                p.dma("gpsimd", g1_d[b][pt_ * 128:(pt_ + 1) * 128, ts_ * 512:(ts_ + 1) * 512],
                      gq[pr], f"l_gst{pr}")
                dng2[pr] = p.cnt[f"l_gst{pr}"]


def stage8(ctx):
    p = ctx.p
    x1_d = ctx.d("x1_tm", (BPC, T, D), F32, "s4", ("s8",))
    s2_d = ctx.d("s2_fm_bf", (BPC, D, T), BF16, "s6", ("s8",))
    g1_d = ctx.d("g1_bf", (BPC, D, T), BF16, "s7", ("s8",))
    out_d = ctx.d("out_tm", (BPC, T, D), F32, "s8", ())

    NB = 3
    x1t = [ctx.sbuf(f"s8_x1{i}", (128, D), F32) for i in range(NB)]
    s2t = [ctx.sbuf(f"s8_s2{i}", (128, D), BF16) for i in range(NB)]
    g1t = [ctx.sbuf(f"s8_g1{i}", (128, D), BF16) for i in range(NB)]
    sgt = [ctx.sbuf(f"s8_sg{i}", (128, D), F32) for i in range(NB)]
    x2t = [ctx.sbuf(f"s8_x2{i}", (128, D), F32) for i in range(NB)]

    dv = [0] * NB
    dst8 = [0] * NB
    for it in range(NT // 128):
        b, r = divmod(it, T // 128)
        pr = it % NB
        t0 = r * 128
        if it >= NB:
            p.wait("sync", "o_v", dv[pr])
        lx = p.dma("sync", x1t[pr], x1_d[b, t0:t0 + 128, :], f"o_lx{pr}")
        ls = p.dmat("sync", s2t[pr], s2_d[b][:, t0:t0 + 128], f"o_ls{pr}")
        lg = p.dmat("sync", g1t[pr], g1_d[b][:, t0:t0 + 128], f"o_lg{pr}")
        p.wait("vector", f"o_lx{pr}", lx)
        p.wait("vector", f"o_ls{pr}", ls)
        p.wait("vector", f"o_lg{pr}", lg)
        if it >= NB:
            p.wait("vector", f"o_st{pr}", dst8[pr])
        p.op("vector", "tensor_tensor", sgt[pr], s2t[pr], g1t[pr], op=ALU.mult, sem="o_v")
        p.wait("vector", "o_v")
        p.op("vector", "tensor_tensor", x2t[pr], x1t[pr], sgt[pr], op=ALU.add, sem="o_v")
        dv[pr] = p.cnt["o_v"]
        p.wait("sync", "o_v")
        p.dma("sync", out_d[b, t0:t0 + 128, :], x2t[pr], f"o_st{pr}")
        dst8[pr] = p.cnt[f"o_st{pr}"]


STAGE_FNS = {"s1": stage1, "s2": stage2, "s3": stage3, "s4": stage4,
             "s5": stage5, "s6": stage6, "s7": stage7, "s8": stage8}


def build_launch(stages, core_input_names):
    ctx = Ctx(stages, core_input_names)
    with ctx.stack:
        dram_common(ctx)
        first = True
        for s in ALL_STAGES:
            if s in ctx.stages:
                if not first:
                    ctx.p.barrier()
                ctx.new_stage()
                STAGE_FNS[s](ctx)
                first = False
        check_deadlock(ctx.p)
        ctx.finalize()
    return ctx


LAUNCH_PLAN = [tuple(ALL_STAGES)]


def kernel(**inputs):
    in_maps = host_prepare(inputs)
    core_input_names = set(in_maps[0].keys())
    state = [dict(m) for m in in_maps]
    for stages in LAUNCH_PLAN:
        ctx = build_launch(stages, core_input_names)
        launch_maps = [{k: st[k] for k in ctx.ext_in} for st in state]
        res = run_bass_kernel_spmd(ctx.nc, launch_maps, core_ids=list(range(NCORES)))
        for c in range(NCORES):
            for name in ctx.ext_out:
                state[c][name] = np.asarray(res.results[c][name])
    out = np.concatenate([state[c]["out_tm"] for c in range(NCORES)], axis=0)
    return out.astype(np.float32)


# version 04 (final)
# speedup vs baseline: 1.1527x; 1.0913x over previous
import contextlib
import math
import numpy as np
import ml_dtypes
import concourse.bass as bass
import concourse.mybir as mybir
from concourse.bass_utils import run_bass_kernel_spmd

F32 = mybir.dt.float32
BF16 = mybir.dt.bfloat16
AF = mybir.ActivationFunctionType
ALU = mybir.AluOpType
AX = mybir.AxisListType
BF = ml_dtypes.bfloat16

NCORES = 8
BPC = 4
T = 2048
D = 1024
NT = BPC * T
H, HD, CK = 16, 64, 64
NCHUNK = T // CK
NEG = -1.0e30

ENGINES = ("sync", "scalar", "vector", "tensor", "gpsimd")

ALL_STAGES = ("s1", "s2", "s3", "s4", "s5", "s6", "s7", "s8")
DEBUG_NAMES = set()
DEV_DUMP = False
BISECT_EVAC = False
BISECT_OMM = False
BISECT_POJ = False
BISECT_TS = False
BISECT_NOACT = False
S2_NB = 4
S2_HP = 8
S2_DO_P3 = True
S2_PA = True
S2_PB = True
S2_OG = True
S2_PB_S = True
S2_PB_O = True
S2_PB_U = True
S2_PB_EV = True
S3_ROUTER = True
S3_SHARED = True
S3_NEXP = 8
S3_G0 = True
S3_QL = 3
S2_DO_S2B = True


class P:

    def __init__(self, nc, stack):
        self.nc = nc
        self.stack = stack
        self.plan = {e: [] for e in ENGINES}
        self.meta = {e: [] for e in ENGINES}
        self.handles = []
        self.hcnt = []
        self.epoch = 0
        self.name2h = {}
        self.pool_ptr = 0
        self.sem_queues = {}
        self.dma_handles = {}
        self._barn = 0

    def _hid(self, name):
        key = (self.epoch, name)
        if key not in self.name2h:
            if self.pool_ptr >= len(self.handles):
                h = self.stack.enter_context(self.nc.semaphore(f"s{len(self.handles)}"))
                self.handles.append(h)
                self.hcnt.append(0)
            self.name2h[key] = self.pool_ptr
            self.pool_ptr += 1
        return self.name2h[key]

    class _CntView:
        def __init__(self, p):
            self.p = p

        def __getitem__(self, name):
            return self.p.hcnt[self.p._hid(name)]

        def get(self, name, default=0):
            return self.p.hcnt[self.p._hid(name)]

        def __setitem__(self, name, v):
            self.p.hcnt[self.p._hid(name)] = v

        def __contains__(self, name):
            return (self.p.epoch, name) in self.p.name2h

    @property
    def cnt(self):
        return P._CntView(self)

    def sem(self, name):
        return self.handles[self._hid(name)]

    def wait(self, eng, name, val=None):
        hid = self._hid(name)
        h = self.handles[hid]
        v = self.hcnt[hid] if val is None else val
        self.plan[eng].append(lambda e, h=h, v=v: e.wait_ge(h, v))
        self.meta[eng].append(("wait", hid, v))
        return v

    def dma(self, q, out, in_, sem, **kw):
        hid = self._hid(sem)
        h = self.handles[hid]
        self.hcnt[hid] += 16
        qc = "sw" if q == "gpsimd" else "hw"
        key = (self.epoch, sem)
        self.sem_queues.setdefault(key, set()).add(q)
        assert all(("sw" if x == "gpsimd" else "hw") == qc for x in self.sem_queues[key]), \
            f"sem {sem} mixes SWDGE and HWDGE queues"
        self.dma_handles.setdefault(hid, set()).add(q)
        def f(e, out=out, in_=in_, h=h, kw=kw):
            e.dma_start(out=out, in_=in_, **kw).then_inc(h, 16)
        self.plan[q].append(f)
        self.meta[q].append(("inc", hid, 16))
        return self.hcnt[hid]

    def dmat(self, q, out, in_, sem):
        hid = self._hid(sem)
        h = self.handles[hid]
        self.hcnt[hid] += 16
        key = (self.epoch, sem)
        self.sem_queues.setdefault(key, set()).add(q)
        self.dma_handles.setdefault(hid, set()).add(q)
        def f(e, out=out, in_=in_, h=h):
            e.dma_start_transpose(out, in_).then_inc(h, 16)
        self.plan[q].append(f)
        self.meta[q].append(("inc", hid, 16))
        return self.hcnt[hid]

    def op(self, eng, meth, *args, sem=None, **kw):
        h = None
        if sem is not None:
            hid = self._hid(sem)
            h = self.handles[hid]
            self.hcnt[hid] += 1
        def f(e, meth=meth, args=args, kw=kw, h=h):
            ins = getattr(e, meth)(*args, **kw)
            if h is not None:
                ins.then_inc(h, 1)
        self.plan[eng].append(f)
        if sem is not None:
            self.meta[eng].append(("inc", hid, 1))
            return self.hcnt[hid]
        self.meta[eng].append(("op", -1, 0))
        return None

    def barrier(self):
        for (ep, name), queues in list(self.sem_queues.items()):
            if ep != self.epoch:
                continue
            for q in queues:
                self.wait(q, name)
        self._barn += 1
        hid = self._hid(f"__bar{self._barn}")
        h = self.handles[hid]
        for e in ENGINES:
            self.hcnt[hid] += 1
            def f(en, h=h):
                en.sem_inc(h, 1)
            self.plan[e].append(f)
            self.meta[e].append(("inc", hid, 1))
        tgt = self.hcnt[hid]
        for e in ENGINES:
            self.plan[e].append(lambda en, h=h, v=tgt: en.wait_ge(h, v))
            self.meta[e].append(("wait", hid, tgt))
        self.epoch += 1
        self.pool_ptr = 0

    def final_epilogue(self):
        for hid, queues in self.dma_handles.items():
            h = self.handles[hid]
            v = self.hcnt[hid]
            for q in queues:
                self.plan[q].append(lambda e, h=h, v=v: e.wait_ge(h, v))
                self.meta[q].append(("wait", hid, v))


def check_deadlock(p):
    pos = {e: 0 for e in ENGINES}
    sems = {}
    progress = True
    while progress:
        progress = False
        for e in ENGINES:
            m = p.meta[e]
            while pos[e] < len(m):
                kind, a, b = m[pos[e]]
                if kind == "wait" and sems.get(a, 0) < b:
                    break
                if kind == "inc":
                    sems[a] = sems.get(a, 0) + b
                pos[e] += 1
                progress = True
    stuck = {e: pos[e] for e in ENGINES if pos[e] < len(p.meta[e])}
    if stuck:
        for e, i in stuck.items():
            kind, a, b = p.meta[e][i]
            print(f"DEADLOCK {e}@{i}/{len(p.meta[e])}: waiting h{a} >= {b} "
                  f"(have {sems.get(a, 0)})")
        raise RuntimeError("plan deadlock")


class Ctx:

    def __init__(self, stages, core_inputs_names):
        self.stages = set(stages)
        self.nc = bass.Bass("TRN2", target_bir_lowering=False)
        self.stack = contextlib.ExitStack()
        self.p = P(self.nc, self.stack)
        self.dram = {}
        self.sb = {}
        self.ps = {}
        self.ext_in = []
        self.ext_out = []
        self._input_names = set(core_inputs_names)
        self.fused = len(self.stages) > 1
        self.cursor = 20480

    def d(self, name, shape, dtype, producer, consumers):
        if name in self.dram:
            return self.dram[name]
        if producer is None:
            kind = "ExternalInput"
            self.ext_in.append(name)
        elif producer in self.stages:
            if (name == "out_tm" or name in DEBUG_NAMES
                    or any(c not in self.stages for c in consumers)):
                kind = "ExternalOutput"
                self.ext_out.append(name)
            else:
                kind = "Internal"
        else:
            kind = "ExternalInput"
            self.ext_in.append(name)
        t = self.nc.dram_tensor(name, list(shape), dtype, kind=kind)
        ap = t.ap()
        self.dram[name] = ap
        return ap

    def sbuf(self, name, shape, dtype):
        if name not in self.sb:
            if self.fused:
                nb = shape[1] * (2 if dtype == BF16 else 4)
                nb = (nb + 63) & ~63
                assert self.cursor + nb <= 196608 - 128, f"sbuf overflow at {name}"
                h = self.nc.alloc_sbuf_tensor_at(name, list(shape), dtype,
                                                 offset=self.cursor)
                self.cursor += nb
            else:
                h = self.nc.alloc_sbuf_tensor(name, list(shape), dtype)
            self.sb[name] = h[:]
        return self.sb[name]

    def new_stage(self):
        if self.fused:
            self.cursor = 20480
            self.sb = {}

    def psum(self, name, shape, dtype=F32):
        if name not in self.ps:
            h = self.nc.alloc_psum_tensor(name, list(shape), dtype)
            self.ps[name] = h[:]
        return self.ps[name]

    def finalize(self):
        nc, p = self.nc, self.p
        p.final_epilogue()
        with nc.Block() as blk:
            for eng in ENGINES:
                fns = p.plan[eng]
                if not fns:
                    continue
                getattr(blk, eng)(lambda e, fns=fns: [f(e) for f in fns])



def host_prepare(inputs):
    f32 = np.float32
    asc = np.ascontiguousarray

    shared = {}
    shared["w0_write_T"] = asc(inputs["L0_attn_W_write"].T)
    shared["w0_out_T"] = asc(inputs["L0_attn_W_out"].T)
    shared["w0_gba_T"] = asc(np.concatenate(
        [inputs["L0_attn_W_gate"].T, inputs["L0_attn_W_beta"].T,
         inputs["L0_attn_W_alpha"].T], axis=1))
    shared["A_log_bc"] = asc(np.broadcast_to(inputs["L0_attn_A_log"][None, :], (128, 16))).astype(f32)
    shared["dt_bias_bc"] = asc(np.broadcast_to(inputs["L0_attn_dt_bias"][None, :], (128, 16))).astype(f32)
    shared["sw1"] = asc(inputs["L0_ffn_shared_w1"])
    shared["sw2"] = asc(inputs["L0_ffn_shared_w2"])
    shared["ew1"] = asc(inputs["L0_ffn_expert_w1"])
    shared["ew2"] = asc(inputs["L0_ffn_expert_w2"])
    shared["router_T"] = asc(inputs["L0_ffn_router_w"].T)
    shared["gamma_bc"] = asc(np.broadcast_to(inputs["L0_ffn_ln_gamma"][None, :], (128, 1024))).astype(f32)
    shared["beta_bc"] = asc(np.broadcast_to(inputs["L0_ffn_ln_beta"][None, :], (128, 1024))).astype(f32)
    shared["wq_T"] = asc(inputs["L1_attn_Wq"].T)
    shared["wk_T"] = asc(inputs["L1_attn_Wk"].T)
    shared["wv_T"] = asc(inputs["L1_attn_Wv"].T)
    shared["wo_T"] = asc(inputs["L1_attn_Wo"].T)
    shared["W1"] = asc(inputs["L1_ffn_W1"])
    shared["W2"] = asc(inputs["L1_ffn_W2"])
    shared["b1c"] = asc(inputs["L1_ffn_b1"].reshape(8, 32, 128).transpose(0, 2, 1))
    shared["b2c"] = asc(inputs["L1_ffn_b2"].reshape(8, 8, 128).transpose(0, 2, 1))

    shared["ident_bf"] = np.eye(128, dtype=BF)
    shared["ident_f32"] = np.eye(128, dtype=f32)
    ut = np.triu(np.ones((CK, CK), f32))
    sl = np.tril(np.ones((CK, CK), f32), -1)
    al = np.ones((CK, CK), f32)
    z = np.zeros((CK, CK), f32)
    shared["ut2"] = asc(np.block([[ut, z], [z, ut]]))
    shared["sl2"] = asc(np.block([[sl, z], [z, sl]]))
    shared["al2"] = asc(np.block([[al, z], [z, al]]))
    shared["halfsel"] = asc(np.block([[np.ones((1, 64), f32), np.zeros((1, 64), f32)],
                                      [np.zeros((1, 64), f32), np.ones((1, 64), f32)]]))
    mT = np.triu(np.ones((CK, CK), f32), 1)
    shared["maskT2"] = asc(np.concatenate([mT, mT], axis=0))
    mp = np.where(np.triu(np.ones((CK, CK), f32), 1) > 0, 0.0, NEG).astype(f32)
    shared["maskPenT"] = asc(mp.T)
    shared["ident64"] = np.eye(64, dtype=f32)
    shared["ones_row"] = np.ones((1, 128), f32)
    amask = np.zeros((4, 128, 512), f32)
    for m in range(4):
        for pp in range(128):
            for c in range(512):
                if c > m * 128 + pp:
                    amask[m, pp, c] = NEG
    shared["amask"] = amask

    x = inputs["L0_attn_out"]
    in_maps = []
    for c in range(NCORES):
        m = dict(shared)
        m["x_tm"] = asc(x[c * BPC:(c + 1) * BPC])
        in_maps.append(m)
    return in_maps



def dram_common(ctx):
    pass


def load_const(ctx, name, shape, dtype=F32, sb_dtype=None, q="sync", sem="cld"):
    ap = ctx.d(name, shape, dtype, None, ())
    sbd = sb_dtype or dtype
    sb = ctx.sbuf("c_" + name, shape, sbd)
    if sbd != dtype:
        ctx.p.dma("gpsimd", sb, ap, sem + "_g")
    else:
        ctx.p.dma(q, sb, ap, sem)
    return sb


def stage1(ctx):
    p = ctx.p
    x_tm = ctx.d("x_tm", (BPC, T, D), F32, None, ())
    xn_f = ctx.d("xn_tm_f32", (BPC, T, D), F32, "s1", ("s4",))
    xn_b = ctx.d("xn_tm_bf", (BPC, T, D), BF16, "s1", ("s1",))
    h_b = ctx.d("h_tm_bf", (BPC, T, D), BF16, "s1", ("s1",))
    xn_sh = ctx.d("xn_tm_shift", (BPC, T, D), BF16, "s1", ("s2",))
    xn_fm = ctx.d("xn_fm_bf", (BPC, D, T), BF16, "s1", ("s2",))
    h_fm = ctx.d("h_fm_bf", (BPC, D, T), BF16, "s1", ("s3",))
    gamma = ctx.d("gamma_bc", (128, D), F32, None, ())
    beta = ctx.d("beta_bc", (128, D), F32, None, ())

    gm = ctx.sbuf("s1_gamma", (128, D), F32)
    bt = ctx.sbuf("s1_beta", (128, D), F32)
    p.dma("sync", gm, gamma, "s1cl")
    p.dma("sync", bt, beta, "s1cl")

    zrow = ctx.sbuf("s1_zrow", (1, D), BF16)
    p.op("vector", "memset", zrow, 0.0, sem="s1z")
    p.wait("gpsimd", "s1z")
    for b in range(BPC):
        p.dma("gpsimd", xn_sh[b, 0:1, :], zrow, "s1sh")

    NB = 3
    xt = [ctx.sbuf(f"s1_x{i}", (128, D), F32) for i in range(NB)]
    bns = [ctx.sbuf(f"s1_bn{i}", (128, 12), F32) for i in range(NB)]
    sts = [ctx.sbuf(f"s1_stats{i}", (128, 16), F32) for i in range(NB)]
    xnt = [ctx.sbuf(f"s1_xn{i}", (128, D), F32) for i in range(NB)]
    xnb = [ctx.sbuf(f"s1_xnb{i}", (128, D), BF16) for i in range(NB)]
    ht = [ctx.sbuf(f"s1_h{i}", (128, D), F32) for i in range(NB)]
    ht2 = [ctx.sbuf(f"s1_h2{i}", (128, D), F32) for i in range(NB)]
    hb = [ctx.sbuf(f"s1_hb{i}", (128, D), BF16) for i in range(NB)]

    done_x_s = [0] * NB
    done_x_v = [0] * NB
    done_xn = [0] * NB
    done_xn_c = [0] * NB
    done_xnb = [0] * NB
    done_hb = [0] * NB
    done_ht = [0] * NB
    done_st = [0] * NB
    done_sqv = [0] * NB

    EPS = 1e-5
    for it in range(NT // 128):
        b, r = divmod(it, T // 128)
        pr = it % NB
        xti, xni, xbi, hbi = xt[pr], xnt[pr], xnb[pr], hb[pr]
        bn, st = bns[pr], sts[pr]
        if it >= NB:
            p.wait("sync", "s1xn", done_x_s[pr])
            p.wait("sync", "s1v", done_x_v[pr])
        ldv = p.dma("sync", xti, x_tm[b, r * 128:(r + 1) * 128, :], f"s1ld{pr}")

        if it >= NB:
            p.wait("vector", "s1hx", done_st[pr])
        if it == 0:
            p.wait("vector", "s1cl")
        p.wait("vector", f"s1ld{pr}", ldv)
        p.op("vector", "bn_stats", bn[:, 0:6], xti[:, 0:512])
        p.op("vector", "bn_stats", bn[:, 6:12], xti[:, 512:1024], sem="s1v")
        done_x_v[pr] = p.cnt["s1v"]
        p.wait("vector", "s1v")
        p.op("vector", "bn_aggr", st[:, 0:2], bn, sem="s1v")
        p.wait("vector", "s1v")
        p.op("vector", "tensor_scalar_add", st[:, 2:3], st[:, 1:2], EPS, sem="s1v")
        p.wait("scalar", "s1v")
        p.op("scalar", "activation", st[:, 3:4], st[:, 2:3], AF.Sqrt, sem="s1s")
        p.wait("vector", "s1s")
        p.op("vector", "reciprocal", st[:, 4:5], st[:, 3:4], sem="s1v")
        p.wait("vector", "s1v")
        p.op("vector", "tensor_scalar", st[:, 5:6], st[:, 0:1], st[:, 4:5], -1.0,
             op0=ALU.mult, op1=ALU.mult)
        p.op("vector", "tensor_scalar", st[:, 9:10], st[:, 4:5], st[:, 4:5], None,
             op0=ALU.mult, sem="s1v")
        p.wait("vector", "s1v")
        p.op("vector", "tensor_scalar", st[:, 13:14], st[:, 1:2], st[:, 9:10], None,
             op0=ALU.mult, sem="s1v")
        p.wait("vector", "s1v")
        p.op("vector", "tensor_scalar_add", st[:, 14:15], st[:, 13:14], EPS, sem="s1v")
        p.wait("scalar", "s1v")
        p.op("scalar", "activation", st[:, 10:11], st[:, 14:15], AF.Sqrt, sem="s1s")
        p.wait("vector", "s1s")
        p.op("vector", "reciprocal", st[:, 6:7], st[:, 10:11], sem="s1f")

        if it >= NB:
            p.wait("scalar", f"s1stf{pr}", done_xn[pr])
            p.wait("scalar", "s1xb", done_xn_c[pr])
        p.wait("scalar", "s1v")
        p.op("scalar", "activation", xni, xti, AF.Identity,
             scale=st[:, 4:5], bias=st[:, 5:6], sem="s1xn")
        p.wait("sync", "s1xn")
        p.dma("sync", xn_f[b, r * 128:(r + 1) * 128, :], xni, f"s1stf{pr}")

        if it >= NB:
            p.wait("vector", f"s1stx{pr}", done_xnb[pr])
        p.wait("vector", "s1xn")
        p.op("vector", "tensor_copy", xbi, xni, sem="s1xb")
        done_xn_c[pr] = p.cnt["s1xb"]
        p.wait("gpsimd", "s1xb")
        p.dma("gpsimd", xn_b[b, r * 128:(r + 1) * 128, :], xbi, f"s1stx{pr}")
        if r == T // 128 - 1:
            p.dma("gpsimd", xn_sh[b, r * 128 + 1:T, :], xbi[0:127, :], f"s1stx{pr}")
        else:
            p.dma("gpsimd", xn_sh[b, r * 128 + 1:r * 128 + 129, :], xbi, f"s1stx{pr}")
        done_xnb[pr] = p.cnt[f"s1stx{pr}"]

        if it >= NB:
            p.wait("scalar", "s1hb", done_ht[pr])
        p.wait("scalar", "s1f")
        p.op("scalar", "activation", ht[pr], xni, AF.Identity, scale=st[:, 6:7], sem="s1hx")
        done_x_s[pr] = p.cnt["s1xn"]
        done_xn[pr] = p.cnt[f"s1stf{pr}"]
        done_st[pr] = p.cnt["s1hx"]
        if it >= NB:
            p.wait("vector", f"s1sth{pr}", done_hb[pr])
        p.wait("vector", "s1hx")
        p.op("vector", "tensor_tensor", ht2[pr], ht[pr], gm, op=ALU.mult, sem="s1hg")
        p.wait("vector", "s1hg")
        p.op("vector", "tensor_tensor", hbi, ht2[pr], bt, op=ALU.add, sem="s1hb")
        done_ht[pr] = p.cnt["s1hb"]
        p.wait("gpsimd", "s1hb")
        p.dma("gpsimd", h_b[b, r * 128:(r + 1) * 128, :], hbi, f"s1sth{pr}")
        done_hb[pr] = p.cnt[f"s1sth{pr}"]

    trb = [ctx.sbuf(f"s1_tr{i}", (128, T), BF16) for i in range(2)]
    done_tr = [0, 0]
    for nm in ("s1stx0", "s1stx1", "s1stx2", "s1sth0", "s1sth1", "s1sth2"):
        p.wait("sync", nm)
    i = 0
    for b in range(BPC):
        for src, dst in ((xn_b, xn_fm), (h_b, h_fm)):
            for ki in range(8):
                pr = i % 2
                if i >= 2:
                    p.wait("sync", f"s1trs{pr}", done_tr[pr])
                ldv = p.dmat("sync", trb[pr], src[b, :, ki * 128:(ki + 1) * 128], f"s1trl{pr}")
                p.wait("scalar", f"s1trl{pr}", ldv)
                p.dma("scalar", dst[b, ki * 128:(ki + 1) * 128, :], trb[pr], f"s1trs{pr}")
                done_tr[pr] = p.cnt[f"s1trs{pr}"]
                i += 1


def stage2(ctx):
    p = ctx.p
    xn_fm = ctx.d("xn_fm_bf", (BPC, D, T), BF16, "s1", ("s2",))
    xn_sh = ctx.d("xn_tm_shift", (BPC, T, D), BF16, "s1", ("s2",))
    og_d = ctx.d("og_tm_bf", (BPC, T, D), BF16, "s2", ("s2",))
    sproj_d = ctx.d("sproj_fm_bf", (BPC, D, T), BF16, "s2", ("s4",))
    a_fm_d = ctx.d("a_fm_hbm", (BPC, H, T), F32, "s2", ("s2",))
    lam_fm_d = ctx.d("lam_fm_hbm", (BPC, H, NCHUNK), F32, "s2", ("s2",))

    wgba = ctx.sbuf("s2_wgba", (128, 8 * 48), BF16)
    p.dma("gpsimd", wgba.rearrange("q (ki c) -> q ki c", ki=8),
          ctx.d("w0_gba_T", (D, 48), F32, None, ()).rearrange("(ki q) c -> q ki c", q=128),
          "cldg")
    wwT = ctx.sbuf("s2_wwT", (128, 8 * 1024), BF16)
    p.dma("gpsimd", wwT.rearrange("q (ki c) -> q ki c", ki=8),
          ctx.d("w0_write_T", (D, D), F32, None, ()).rearrange("(ki q) c -> q ki c", q=128),
          "cldg")
    woT = ctx.sbuf("s2_woT", (128, 8 * 1024), BF16)
    p.dma("gpsimd", woT.rearrange("q (ki c) -> q ki c", ki=8),
          ctx.d("w0_out_T", (D, D), F32, None, ()).rearrange("(ki q) c -> q ki c", q=128),
          "cldg")
    alog = load_const(ctx, "A_log_bc", (128, 16))
    dtb = load_const(ctx, "dt_bias_bc", (128, 16))
    ut2 = load_const(ctx, "ut2", (128, 128))
    sl2 = load_const(ctx, "sl2", (128, 128))
    al2 = load_const(ctx, "al2", (128, 128))
    halfsel = load_const(ctx, "halfsel", (2, 128))
    maskPenT = load_const(ctx, "maskPenT", (64, 64))
    ident64 = load_const(ctx, "ident64", (64, 64))
    ones_row = load_const(ctx, "ones_row", (1, 128))
    identf = load_const(ctx, "ident_f32", (128, 128))

    nea = ctx.sbuf("s2_nea", (128, 16), F32)
    nea0 = ctx.sbuf("s2_nea0", (128, 16), F32)
    p.wait("scalar", "cld")
    p.wait("scalar", "cldg")
    p.wait("scalar", "cld_g")
    p.op("scalar", "activation", nea0, alog, AF.Exp, sem="d_s")
    p.wait("vector", "d_s")
    p.op("vector", "tensor_scalar_mul", nea, nea0, -1.0, sem="d_nea")

    gate_b = ctx.sbuf("s2_gate", (128, 256), F32)
    beta_b = ctx.sbuf("s2_beta", (128, 256), F32)
    decr_b = ctx.sbuf("s2_decr", (128, 256), F32)
    a_b = ctx.sbuf("s2_a", (128, 256), F32)
    na_b = ctx.sbuf("s2_na", (128, 256), F32)
    dwe_b = ctx.sbuf("s2_dwe", (128, 256), F32)
    lam_b = ctx.sbuf("s2_lam", (128, 256), F32)
    eag_b = ctx.sbuf("s2_eag", (128, 256), F32)
    tmp16 = ctx.sbuf("s2_tmp16", (128, 16 * 7), F32)
    vb_b = ctx.sbuf("s2_vb", (128, 16 * 1024), BF16)
    Xf = [ctx.sbuf(f"s2_Xf{i}", (128, 8 * 128), BF16) for i in range(2)]
    aev = ctx.sbuf("s2_aev", (16, 128), F32)
    xnK = ctx.sbuf("s2_xnK", (128, 2049), BF16)
    xnT = ctx.sbuf("s2_xnT", (128, 16 * 128), BF16)
    af = [ctx.sbuf(f"s2_af{i}", (1, T), F32) for i in range(2)]
    lamf = ctx.sbuf("s2_lamf", (2, NCHUNK), F32)
    lam2 = ctx.sbuf("s2_lam2", (128, NCHUNK), F32)
    Esb = ctx.sbuf("s2_E", (128, 64), F32)
    J_all = ctx.sbuf("s2_J", (128, 32 * 64), BF16)
    wdw2 = ctx.sbuf("s2_wdw", (128, 128), BF16)
    S2 = [ctx.sbuf(f"s2_S{i}", (128, 64), F32) for i in range(2)]
    S2bf = [ctx.sbuf(f"s2_Sbf{i}", (128, 64), BF16) for i in range(2)]
    otmp = ctx.sbuf("s2_otmp", (128, 128), F32)
    og_hp = ctx.sbuf("s2_og", (128, 16 * 128), BF16)
    og2_hp = ctx.sbuf("s2_og2", (128, 16 * 128), BF16)
    ogf = ctx.sbuf("s2_ogf", (128, 8 * T), BF16)
    spj = [ctx.sbuf(f"s2_spj{i}", (128, 512), BF16) for i in range(2)]

    pS = ctx.psum("pS", (128, 512), F32)
    pA = ctx.psum("pA", (128, 512), F32)
    pB = ctx.psum("pB", (128, 512), F32)
    pC = ctx.psum("pC", (128, 512), F32)
    pD = ctx.psum("pD", (128, 512), F32)

    pE = ctx.psum("pE", (128, 512), F32)
    p48 = pS[:, 0:48]
    pc = pS[:, 64:112]
    pat = pS[0:16, 128:256]
    pL = pS[:, 256:288]
    pM = pC[:, 0:64]
    pB2 = pC[:, 64:128]
    pU = pE[:, 0:64]
    pOI = pD[:, 0:128]
    pOJ = pD[:, 128:256]

    p.wait("tensor", "cld")
    p.wait("tensor", "cldg")
    p.wait("tensor", "cld_g")
    p.wait("tensor", "d_nea")

    done_xf = [0, 0]
    done_p48_v = 0
    done_p48_s = 0
    done_pv = [0, 0]
    done_pc = 0
    done_pM = 0
    done_spj = [0, 0]

    for b in range(S2_NB):
        for tt in range(16):
            pr = tt % 2
            p.wait("tensor", "d_spev")
            p.wait("sync", "d_mmv", done_xf[pr])
            ldv = p.dma("sync", Xf[pr].rearrange("q (ki t) -> q ki t", ki=8),
                        xn_fm[b].rearrange("(ki q) t -> q ki t", q=128)[:, :, tt * 128:(tt + 1) * 128],
                        f"d_xf{pr}")
            p.wait("tensor", f"d_xf{pr}", ldv)
            p.wait("tensor", "d_v", done_p48_v)
            p.wait("tensor", "d_gb", done_p48_s)
            for ki in range(8):
                p.op("tensor", "matmul", p48, Xf[pr][:, ki * 128:(ki + 1) * 128],
                     wgba[:, ki * 48:(ki + 1) * 48], start=(ki == 0), stop=(ki == 7),
                     sem="d_mm48" if ki == 7 else None)
            p.wait("scalar", "d_mm48")
            p.op("scalar", "activation", gate_b[:, tt * 16:(tt + 1) * 16], p48[:, 0:16],
                 AF.Sigmoid)
            p.op("scalar", "activation", beta_b[:, tt * 16:(tt + 1) * 16], p48[:, 16:32],
                 AF.Sigmoid, sem="d_gb")
            done_p48_s = p.cnt["d_gb"]
            p.wait("vector", "d_mm48")
            p.op("vector", "tensor_tensor", tmp16[:, 0:16], p48[:, 32:48], dtb, op=ALU.add,
                 sem="d_v")
            done_p48_v = p.cnt["d_v"]
            p.wait("vector", "d_v")
            p.op("vector", "tensor_scalar_mul", tmp16[:, 16:32], tmp16[:, 0:16], -1.0, sem="d_v")
            p.wait("vector", "d_v")
            p.op("vector", "tensor_tensor", tmp16[:, 32:48], tmp16[:, 0:16], tmp16[:, 16:32],
                 op=ALU.min, sem="d_v")
            p.wait("scalar", "d_v")
            p.op("scalar", "activation", tmp16[:, 48:64], tmp16[:, 32:48], AF.Exp, sem="d_s")
            p.wait("scalar", "d_s")
            p.op("scalar", "activation", tmp16[:, 64:80], tmp16[:, 48:64], AF.Ln, bias=1.0,
                 sem="d_s")
            p.op("vector", "tensor_single_scalar", tmp16[:, 80:96], tmp16[:, 0:16], 0.0,
                 ALU.max, sem="d_v")
            p.wait("vector", "d_s")
            p.wait("vector", "d_v")
            p.op("vector", "tensor_tensor", tmp16[:, 96:112], tmp16[:, 80:96], tmp16[:, 64:80],
                 op=ALU.add, sem="d_v")
            p.wait("vector", "d_v")
            p.op("vector", "tensor_tensor", decr_b[:, tt * 16:(tt + 1) * 16], tmp16[:, 96:112],
                 nea, op=ALU.mult, sem="d_dec")

            for jh in range(2):
                pv = [pA, pB][jh]
                p.wait("tensor", "d_vev", done_pv[jh])
                for ki in range(8):
                    p.op("tensor", "matmul", pv, Xf[pr][:, ki * 128:(ki + 1) * 128],
                         wwT[:, ki * 1024 + jh * 512: ki * 1024 + jh * 512 + 512],
                         start=(ki == 0), stop=(ki == 7), sem="d_mmv" if ki == 7 else None)
                done_xf[pr] = p.cnt["d_mmv"]
                p.wait("scalar", "d_mmv")
                p.wait("scalar", "d_gb", done_p48_s)
                for h8 in range(8):
                    h = jh * 8 + h8
                    p.op("scalar", "activation",
                         vb_b[:, tt * 1024 + h * 64: tt * 1024 + h * 64 + 64],
                         pv[:, h8 * 64:(h8 + 1) * 64], AF.Identity,
                         scale=beta_b[:, tt * 16 + h: tt * 16 + h + 1],
                         sem="d_vev" if h8 == 7 else None)
                done_pv[jh] = p.cnt["d_vev"]

        for tt in range(16):
            cs = tt * 16
            p.wait("tensor", "d_dec")
            p.wait("tensor", "d_gb")
            p.wait("tensor", "d_vev")
            p.wait("tensor", "d_aev", done_pc)
            rhs = decr_b[:, cs:cs + 16]
            p.op("tensor", "matmul", pc[:, 0:16], ut2, rhs, start=True, stop=True)
            p.op("tensor", "matmul", pc[:, 16:32], sl2, rhs, start=True, stop=True)
            p.op("tensor", "matmul", pc[:, 32:48], al2, rhs, start=True, stop=True, sem="d_mmc")
            p.wait("scalar", "d_mmc")
            p.wait("scalar", "d_eag")
            p.op("scalar", "copy", a_b[:, cs:cs + 16], pc[:, 0:16])
            p.op("scalar", "activation", dwe_b[:, cs:cs + 16], pc[:, 16:32], AF.Exp)
            p.op("scalar", "activation", lam_b[:, cs:cs + 16], pc[:, 32:48], AF.Exp)
            p.op("scalar", "activation", tmp16[:, 96:112], pc[:, 0:16], AF.Exp, sem="d_s2")
            p.wait("vector", "d_s2")
            p.op("vector", "tensor_scalar_mul", na_b[:, cs:cs + 16], a_b[:, cs:cs + 16], -1.0)
            p.op("vector", "tensor_tensor", eag_b[:, cs:cs + 16], tmp16[:, 96:112],
                 gate_b[:, cs:cs + 16], op=ALU.mult, sem="d_eag")
            p.wait("tensor", "d_s2")
            p.op("tensor", "transpose", pat, a_b[:, cs:cs + 16], identf, sem="d_tra")
            p.wait("scalar", "d_tra")
            p.wait("scalar", "d_ast")
            p.op("scalar", "copy", aev, pat, sem="d_aev")
            done_pc = p.cnt["d_aev"]
            p.wait("sync", "d_aev")
            p.dma("sync", a_fm_d[b][:, tt * 128:(tt + 1) * 128], aev, "d_ast")
        p.wait("gpsimd", "d_s2")
        for par in range(2):
            p.dma("gpsimd",
                  lam_fm_d[b].rearrange("h (tt par) -> par tt h", par=2)[par:par + 1],
                  lam_b[par * 64:par * 64 + 1, :].rearrange("one (tt h) -> one tt h", h=16),
                  "d_lst", allow_slow_non_contiguous=True)

        if DEV_DUMP:
            p.wait("gpsimd", "d_eag")
            p.wait("gpsimd", "d_dec")
            p.wait("gpsimd", "d_vev")
            for nm, buf in (("dbg_dec", decr_b), ("dbg_a", a_b), ("dbg_dwe", dwe_b),
                            ("dbg_lam", lam_b), ("dbg_eag", eag_b), ("dbg_na", na_b),
                            ("dbg_gate", gate_b), ("dbg_beta", beta_b)):
                dd = ctx.d(nm, (BPC, 128, 256), F32, "s2", ("zz",))
                p.dma("gpsimd", dd[b], buf, "d_dbg")
            dv = ctx.d("dbg_vb", (BPC, 128, 16 * 1024), BF16, "s2", ("zz",))
            p.dma("gpsimd", dv[b], vb_b, "d_dbg")

        p.wait("sync", "d_ast")
        p.wait("sync", "d_lst")
        for hp in range(S2_HP if S2_DO_P3 else 0):
            h0g = hp * 2
            p.op("vector", "memset", xnK[:, 0:1], 0.0, sem="d_mz")
            p.wait("sync", "d_mz")
            p.wait("sync", "d_mmO")
            p.wait("sync", "d_mmU")
            l1 = p.dma("sync", xnK[:, 1:2049], xn_fm[b][hp * 128:(hp + 1) * 128, :], "d_hp")
            l2 = p.dma("sync", xnT.rearrange("q (tt j) -> q tt j", tt=16),
                       xn_sh[b].rearrange("(tt q) j -> q tt j", q=128)[:, :, hp * 128:(hp + 1) * 128],
                       "d_hp")
            l3 = p.dma("sync", af[0], a_fm_d[b][h0g:h0g + 1, :], "d_hp")
            l4 = p.dma("sync", af[1], a_fm_d[b][h0g + 1:h0g + 2, :], "d_hp")
            l5 = p.dma("sync", lamf, lam_fm_d[b][h0g:h0g + 2, :], "d_hp")
            p.wait("tensor", "d_hp", l5)
            p.wait("tensor", "d_aev")
            p.wait("scalar", "d_hp", l5)
            p.wait("scalar", "d_eag")
            p.op("tensor", "matmul", pL, halfsel, lamf, start=True, stop=True, sem="d_mmL")
            p.wait("vector", "d_mmL")
            p.op("vector", "tensor_copy", lam2, pL, sem="d_lam2")

            p.wait("vector", "d_mmO")
            for cp in range(16 if S2_PA else 0):
                for hh in range(2):
                    jslot = (cp * 2 + hh) * 64
                    p.wait("tensor", "d_Jev", done_pM)
                    p.wait("scalar", "d_Jev", done_pM)
                    for par in range(2):
                        i = cp * 2 + par
                        p.op("tensor", "matmul", pM[par * 64:(par + 1) * 64, :],
                             xnK[hh * 64:(hh + 1) * 64, i * 64: i * 64 + 64],
                             xnK[hh * 64:(hh + 1) * 64, 1 + i * 64: 1 + i * 64 + 64],
                             start=True, stop=True)
                        p.op("tensor", "matmul", pB2[par * 64:(par + 1) * 64, :],
                             ones_row[0:1, 0:64],
                             af[hh][0:1, i * 64: i * 64 + 64], start=True, stop=False)
                        p.op("tensor", "matmul", pB2[par * 64:(par + 1) * 64, :],
                             maskPenT, ident64, start=False, stop=True,
                             sem="d_mmJ" if par == 1 else None)
                    p.wait("scalar", "d_mmJ")
                    p.op("scalar", "activation", Esb, pB2, AF.Exp,
                         bias=na_b[:, cp * 16 + h0g + hh: cp * 16 + h0g + hh + 1], sem="d_E")
                    p.wait("vector", "d_E")
                    p.op("vector", "tensor_tensor", J_all[:, jslot:jslot + 64], Esb, pM,
                         op=ALU.mult, sem="d_Jev")
                    done_pM = p.cnt["d_Jev"]

            p.wait("vector", "d_ogst")
            p.op("vector", "memset", og2_hp[:, 0:128], 0.0, sem="d_og2z")
            p.op("vector", "memset", S2[1], 0.0, sem="d_v")
            p.wait("vector", "d_v")
            jv = p.cnt["d_Jev"]
            done_wdw = p.cnt.get("d_mmU", 0)
            done_pU = 0
            done_sbf = 0
            done_sbf_prev = 0
            done_mmO_prev = p.cnt.get("d_mmO", 0)
            done_pO = p.cnt.get("d_oev", 0)
            done_pO_prev2 = done_pO
            for i in range(NCHUNK if S2_PB else 0):
                par, cp = i % 2, i // 2
                cur, nxt = S2[(i + 1) % 2], S2[i % 2]
                sbf_cur = S2bf[i % 2]
                sbf_prev = S2bf[(i + 1) % 2]
                p.wait("scalar", "d_mmU", done_wdw)
                for hh in range(2):
                    h = h0g + hh
                    p.op("scalar", "activation",
                         wdw2[par * 64:(par + 1) * 64, hh * 64:(hh + 1) * 64],
                         xnT[par * 64:(par + 1) * 64,
                             cp * 128 + hh * 64: cp * 128 + hh * 64 + 64],
                         AF.Identity,
                         scale=dwe_b[par * 64:(par + 1) * 64, cp * 16 + h: cp * 16 + h + 1],
                         sem="d_wdw" if hh == 1 else None)
                p.wait("tensor", "d_wdw")
                p.wait("tensor", "d_Supd", done_pU)
                for hh in range(2 if S2_PB_U else 0):
                    h = h0g + hh
                    p.op("tensor", "matmul", pU[hh * 64:(hh + 1) * 64, :],
                         wdw2[par * 64:(par + 1) * 64, hh * 64:(hh + 1) * 64],
                         vb_b[par * 64:(par + 1) * 64,
                              cp * 1024 + h * 64: cp * 1024 + h * 64 + 64],
                         start=True, stop=True, sem="d_mmU" if hh == 1 else None)
                done_wdw = p.cnt["d_mmU"]
                p.wait("tensor", "d_oev", done_pO_prev2)
                for hh in range(2 if S2_PB_O else 0):
                    if i > 0 and not BISECT_OMM:
                        p.wait("tensor", "d_Sbf", done_sbf_prev)
                        p.op("tensor", "matmul",
                             pOI[par * 64:(par + 1) * 64, hh * 64:(hh + 1) * 64],
                             xnK[hh * 64:(hh + 1) * 64, 1 + i * 64: 1 + i * 64 + 64],
                             sbf_prev[hh * 64:(hh + 1) * 64, :], start=True, stop=True)
                    p.wait("tensor", "d_Jev", jv)
                    p.op("tensor", "matmul",
                         pOJ[par * 64:(par + 1) * 64, hh * 64:(hh + 1) * 64],
                         J_all[par * 64:(par + 1) * 64,
                               (cp * 2 + hh) * 64:(cp * 2 + hh) * 64 + 64],
                         vb_b[par * 64:(par + 1) * 64,
                              cp * 1024 + (h0g + hh) * 64: cp * 1024 + (h0g + hh) * 64 + 64],
                         start=True, stop=True, sem="d_mmO" if hh == 1 else None)
                done_mmO = p.cnt["d_mmO"]
                if not S2_PB_S:
                    continue
                p.wait("vector", "d_mmU")
                p.wait("vector", "d_Sbf", done_sbf)
                p.wait("vector", "d_lam2")
                p.op("vector", "scalar_tensor_tensor", nxt, cur, lam2[:, i:i + 1], pU,
                     op0=ALU.mult, op1=ALU.add, sem="d_Supd")
                done_pU = p.cnt["d_Supd"]
                p.wait("scalar", "d_Supd")
                p.wait("scalar", "d_mmO", done_mmO_prev)
                p.op("scalar", "copy", sbf_cur, nxt, sem="d_Sbf")
                done_sbf_prev = done_sbf
                done_sbf = p.cnt["d_Sbf"]
                done_mmO_prev = done_mmO
                p.wait("scalar", "d_mmO")
                p.wait("scalar", "d_oev", done_pO)
                for hh in range(2 if (S2_PB_O and S2_PB_EV) else 0):
                    h = h0g + hh
                    gcol = gate_b[par * 64:(par + 1) * 64, cp * 16 + h: cp * 16 + h + 1]
                    ecol = eag_b[par * 64:(par + 1) * 64, cp * 16 + h: cp * 16 + h + 1]
                    dst = og_hp[par * 64:(par + 1) * 64,
                                cp * 128 + hh * 64: cp * 128 + hh * 64 + 64]
                    dst2 = og2_hp[par * 64:(par + 1) * 64,
                                  cp * 128 + hh * 64: cp * 128 + hh * 64 + 64]
                    p.op("scalar", "activation", dst,
                         pOJ[par * 64:(par + 1) * 64, hh * 64:(hh + 1) * 64],
                         AF.Identity, scale=gcol,
                         sem=None if i > 0 else ("d_oev" if hh == 1 else None))
                    if i > 0:
                        p.op("scalar", "activation", dst2,
                             pOI[par * 64:(par + 1) * 64, hh * 64:(hh + 1) * 64],
                             AF.Identity, scale=ecol,
                             sem="d_oev" if hh == 1 else None)
                done_pO = p.cnt["d_oev"]

            if DEV_DUMP and b == 0:
                dj = ctx.d("dbg_J", (8, 128, 32 * 64), BF16, "s2", ("zz",))
                p.wait("gpsimd", "d_Jev")
                p.dma("gpsimd", dj[hp], J_all, "d_dbg")
            if not S2_OG:
                continue
            p.wait("gpsimd", "d_oev")
            v1 = p.dma("gpsimd",
                  og_d[b].rearrange("(tt q) j -> q tt j", q=128)[:, :, hp * 128:(hp + 1) * 128],
                  og_hp.rearrange("q (tt c) -> q tt c", tt=16), "d_ogst")
            p.wait("gpsimd", "d_ogst", v1)
            p.wait("gpsimd", "d_og2z")
            p.dma("gpsimd",
                  og_d[b].rearrange("(tt q) j -> q tt j", q=128)[:, :, hp * 128:(hp + 1) * 128],
                  og2_hp.rearrange("q (tt c) -> q tt c", tt=16), "d_ogst",
                  accum_op=ALU.add)

        if not S2_DO_S2B:
            continue
        p.wait("sync", "d_ogst")
        p.wait("sync", "d_mmS")
        for ki in range(8):
            p.dmat("sync", ogf[:, ki * T:(ki + 1) * T], og_d[b][:, ki * 128:(ki + 1) * 128],
                   "d_ogl")
        lv = p.cnt["d_ogl"]
        for ts_ in range(4):
            for mj in range(8):
                po = [pA, pB][mj % 2]
                spr = spj[mj % 2]
                p.wait("tensor", f"d_spst{mj % 2}", done_spj[mj % 2])
                p.wait("tensor", "d_ogl", lv)
                p.wait("tensor", "d_vev")
                for ki in range(8):
                    p.op("tensor", "matmul", po,
                         woT[:, ki * 1024 + mj * 128: ki * 1024 + mj * 128 + 128],
                         ogf[:, ki * T + ts_ * 512: ki * T + ts_ * 512 + 512],
                         start=(ki == 0), stop=(ki == 7), sem="d_mmS" if ki == 7 else None)
                p.wait("vector", "d_mmS")
                p.op("vector", "tensor_copy", spr, po, sem="d_spev")
                p.wait("gpsimd", "d_spev")
                p.dma("gpsimd", sproj_d[b][mj * 128:(mj + 1) * 128, ts_ * 512:(ts_ + 1) * 512],
                      spr, f"d_spst{mj % 2}")
                done_spj[mj % 2] = p.cnt[f"d_spst{mj % 2}"]


def stage3(ctx):
    p = ctx.p
    h_fm = ctx.d("h_fm_bf", (BPC, D, T), BF16, "s1", ("s3",))
    acc0 = ctx.d("acc0_fm", (BPC, D, T), F32, "s3", ("s3",))
    g0_d = ctx.d("g0_bf", (BPC, D, T), BF16, "s3", ("s4",))
    gates_d = ctx.d("gates0", (8, NT), F32, "s3", ("s3",))
    sw1_d = ctx.d("sw1", (2048, D), F32, None, ())
    sw2_d = ctx.d("sw2", (D, 2048), F32, None, ())
    ew1_d = ctx.d("ew1", (8, 2048, D), F32, None, ())
    ew2_d = ctx.d("ew2", (8, D, 2048), F32, None, ())
    rt_d = ctx.d("router_T", (D, 8), F32, None, ())

    identb = load_const(ctx, "ident_bf", (128, 128), BF16, sem="cld3")
    identf = load_const(ctx, "ident_f32", (128, 128), sem="cld3")
    ones_row = load_const(ctx, "ones_row", (1, 128), sem="cld3")

    rt = ctx.sbuf("s3_rt", (128, 8 * 8), BF16)
    p.dma("gpsimd", rt.rearrange("q (ki c) -> q ki c", ki=8),
          rt_d.rearrange("(ki q) c -> q ki c", q=128), "cld3g")

    pS = ctx.psum("pS", (128, 512), F32)
    pA = ctx.psum("pA", (128, 512), F32)
    pB = ctx.psum("pB", (128, 512), F32)
    pC = ctx.psum("pC", (128, 512), F32)
    pT = ctx.psum("pT2", (128, 512), BF16)
    pT3 = ctx.psum("pT3", (128, 512), BF16)

    hbuf = [ctx.sbuf(f"s3_h{i}", (128, 8 * 512), BF16) for i in range(2)]
    h1b = ctx.sbuf("s3_h1big", (128, 16 * 512), BF16)
    wqA = ctx.sbuf("s3_wqA", (128, 128 * 128), BF16)
    wqB = ctx.sbuf("s3_wqB", (128, 128 * 128), BF16)
    wnat = [ctx.sbuf(f"s3_wn{i}", (128, 2048), F32) for i in range(2)]
    qst = ctx.sbuf("s3_qst", (128, 8), F32)
    qt1 = ctx.sbuf("s3_qt1", (128, 2048), F32)
    qt2 = ctx.sbuf("s3_qt2", (128, 2048), F32)
    qt3 = ctx.sbuf("s3_qt3", (128, 2048), F32)
    qr = ctx.sbuf("s3_qr", (128, 2048), BF16)
    lg = ctx.sbuf("s3_lg", (8, 512), F32)
    lt = ctx.sbuf("s3_lt", (128, 8), F32)
    rst = ctx.sbuf("s3_rst", (128, 8), F32)
    exr = ctx.sbuf("s3_exr", (128, 8), F32)
    gtm = ctx.sbuf("s3_gtm", (128, 8), F32)
    gtm0 = ctx.sbuf("s3_gtm0", (128, 8), F32)
    grow = ctx.sbuf("s3_grow", (1, 512), F32)
    gbc = ctx.sbuf("s3_gbc", (128, 512), F32)
    ev = [ctx.sbuf(f"s3_ev{i}", (128, 512), F32) for i in range(2)]
    gl = [ctx.sbuf(f"s3_gl{i}", (128, 512), F32) for i in range(2)]
    gq = [ctx.sbuf(f"s3_gq{i}", (128, 512), BF16) for i in range(2)]

    p.wait("tensor", "cld3")
    p.wait("tensor", "cld3g")
    p.wait("tensor", "cld3_g")

    dn = {"hb": [0, 0], "ff": [0, 0], "ev": [0, 0], "evd": [0, 0],
          "acc_prev": 0, "gbc": 0, "gbc_war": 0, "h1": 0, "wn": [0, 0], "accb": [], "gmm": 0,
          "qr": 0, "plg": 0, "ptr": 0, "gst": 0}

    for gtt in range(16 if S3_ROUTER else 0):
        b, ts_ = divmod(gtt, 4)
        pr = gtt % 2
        p.wait("sync", "f_mm", dn["hb"][pr])
        ldv = p.dma("sync", hbuf[pr].rearrange("q (ki t) -> q ki t", ki=8),
                    h_fm[b].rearrange("(ki q) t -> q ki t", q=128)[:, :, ts_ * 512:(ts_ + 1) * 512],
                    f"f_hl{pr}")
        p.wait("tensor", f"f_hl{pr}", ldv)
        p.wait("tensor", "f_lg", dn["plg"])
        p.wait("tensor", "f_ltc", dn["ptr"])
        for ki in range(8):
            p.op("tensor", "matmul", pS[0:8, 0:512], rt[:, ki * 8:(ki + 1) * 8],
                 hbuf[pr][:, ki * 512:(ki + 1) * 512], start=(ki == 0), stop=(ki == 7),
                 sem="f_mm" if ki == 7 else None)
        dn["hb"][pr] = p.cnt["f_mm"]
        p.wait("scalar", "f_mm")
        p.op("scalar", "copy", lg, pS[0:8, 0:512], sem="f_lg")
        dn["plg"] = p.cnt["f_lg"]
        for j in range(4):
            p.wait("tensor", "f_lg", dn["plg"])
            p.wait("tensor", "f_ltc", dn["ptr"])
            p.op("tensor", "transpose", pS[0:128, 0:8], lg[:, j * 128:(j + 1) * 128],
                 identf[0:8, 0:8], sem="f_tr")
            p.wait("scalar", "f_tr")
            p.wait("scalar", "f_gdone")
            p.op("scalar", "copy", lt, pS[0:128, 0:8], sem="f_ltc")
            dn["ptr"] = p.cnt["f_ltc"]
            p.wait("vector", "f_ltc")
            p.op("vector", "reduce_max", rst[:, 0:1], lt, AX.X, sem="f_v")
            p.wait("vector", "f_v")
            p.op("vector", "tensor_scalar", gtm0, lt, rst[:, 0:1], None, op0=ALU.is_ge)
            p.op("vector", "tensor_scalar_mul", rst[:, 1:2], rst[:, 0:1], -1.0, sem="f_v")
            p.wait("scalar", "f_v")
            p.op("scalar", "activation", exr, lt, AF.Exp,
                 bias=rst[:, 1:2], accum_out=rst[:, 3:4], sem="f_s")
            p.wait("vector", "f_s")
            p.op("vector", "reciprocal", rst[:, 4:5], rst[:, 3:4], sem="f_v")
            p.wait("vector", "f_v")
            p.wait("vector", "f_gst", dn["gst"])
            p.op("vector", "tensor_scalar", gtm, gtm0, rst[:, 4:5], None, op0=ALU.mult,
                 sem="f_gdone")
            p.wait("gpsimd", "f_gdone")
            tr0 = gtt * 512 + j * 128
            p.dma("gpsimd", gates_d[:, tr0:tr0 + 128].rearrange("e t -> t e"), gtm,
                  "f_gst", allow_slow_non_contiguous=True)
            dn["gst"] = p.cnt["f_gst"]

    def quant_matrix(w_d, RR, CC, dest, slot_of):
        for fr in range(RR // 128):
            pr = fr % 2
            wn = wnat[pr]
            p.wait("sync", "q_done", dn["wn"][pr])
            lv = p.dma("sync", wn[:, 0:CC], w_d[fr * 128:(fr + 1) * 128, :], f"q_ld{pr}")
            if S3_QL < 2:
                continue
            p.wait("vector", f"q_ld{pr}", lv)
            p.op("vector", "reduce_sum", qst[:, 0:1], wn[:, 0:CC], AX.X,
                 apply_absolute_value=True, sem="q_v")
            p.wait("vector", "q_v")
            p.op("vector", "tensor_scalar_mul", qst[:, 1:2], qst[:, 0:1], 1.0 / CC, sem="q_v")
            p.wait("vector", "q_v")
            p.op("vector", "tensor_scalar_max", qst[:, 2:3], qst[:, 1:2], 1e-5, sem="q_v")
            p.wait("vector", "q_v")
            p.op("vector", "tensor_scalar_mul", qst[:, 3:4], qst[:, 2:3], 0.5)
            p.op("vector", "tensor_scalar_mul", qst[:, 4:5], qst[:, 2:3], -0.5, sem="q_v")
            p.wait("vector", "q_v")
            p.op("vector", "tensor_scalar", qt1[:, 0:CC], wn[:, 0:CC], qst[:, 3:4], None,
                 op0=ALU.is_gt)
            p.op("vector", "tensor_scalar", qt2[:, 0:CC], wn[:, 0:CC], qst[:, 4:5], None,
                 op0=ALU.is_lt, sem="q_v")
            p.wait("vector", "q_v")
            p.op("vector", "tensor_tensor", qt3[:, 0:CC], qt1[:, 0:CC], qt2[:, 0:CC],
                 op=ALU.subtract, sem="q_v")
            p.wait("vector", "q_v")
            p.wait("vector", "q_tr", dn["qr"])
            p.op("vector", "tensor_scalar", qr[:, 0:CC], qt3[:, 0:CC], qst[:, 2:3], None,
                 op0=ALU.mult, sem="q_done")
            dn["wn"][pr] = p.cnt["q_done"]
            if S3_QL < 3:
                continue
            p.wait("tensor", "q_done")
            for sub in range(CC // 128):
                p.wait("tensor", "q_ev", max(0, p.cnt.get("q_ev", 0) - 1))
                k = p.cnt.get("q_ev", 0)
                ptk = [pT, pT3][k % 2][:, ((k // 2) % 2) * 128:((k // 2) % 2) * 128 + 128]
                p.op("tensor", "transpose", ptk,
                     qr[:, sub * 128:(sub + 1) * 128], identb, sem="q_tr")
                p.wait("vector", "q_tr")
                slot = slot_of(fr, sub)
                p.op("vector", "tensor_copy", dest[:, slot * 128:(slot + 1) * 128],
                     ptk, sem="q_ev")
            dn["qr"] = p.cnt["q_tr"]
        p.wait("tensor", "q_ev")

    def ffn_pass(e):
        for gtt in range(16):
            b, ts_ = divmod(gtt, 4)
            pr = gtt % 2
            p.wait("sync", "f_mm", dn["hb"][pr])
            ldv = p.dma("sync", hbuf[pr].rearrange("q (ki t) -> q ki t", ki=8),
                        h_fm[b].rearrange("(ki q) t -> q ki t", q=128)
                        [:, :, ts_ * 512:(ts_ + 1) * 512], f"f_hl{pr}")
            if e >= 0:
                p.wait("sync", "f_mmg", dn["gmm"])
                glv = p.dma("sync", grow, gates_d[e:e + 1, gtt * 512:(gtt + 1) * 512], "f_gl")
                p.wait("tensor", "f_gl", glv)
                p.wait("tensor", "f_gbc", dn["gbc"])
                p.op("tensor", "matmul", pC[:, 0:512], ones_row[0:1, 0:128],
                     grow, start=True, stop=True, sem="f_mmg")
                dn["gmm"] = p.cnt["f_mmg"]
                p.wait("scalar", "f_mmg")
                p.wait("scalar", "f_ev", dn["gbc_war"])
                p.op("scalar", "copy", gbc, pC[:, 0:512], sem="f_gbc")
                dn["gbc"] = p.cnt["f_gbc"]
            p.wait("tensor", f"f_hl{pr}", ldv)
            p.wait("scalar", "f_mm2", dn["h1"])
            for fj in range(16):
                pp = [pA, pB][fj % 2]
                p.wait("tensor", "f_h1ev", dn["ff"][fj % 2])
                p.wait("tensor", "f_ev", dn["ev"][fj % 2])
                for ki in range(8):
                    p.op("tensor", "matmul", pp,
                         wqA[:, (ki * 16 + fj) * 128:(ki * 16 + fj) * 128 + 128],
                         hbuf[pr][:, ki * 512:(ki + 1) * 512], start=(ki == 0), stop=(ki == 7),
                         sem="f_mm" if ki == 7 else None)
                p.wait("scalar", "f_mm")
                p.op("scalar", "activation", h1b[:, fj * 512:(fj + 1) * 512], pp, AF.Silu,
                     sem="f_h1ev")
                dn["ff"][fj % 2] = p.cnt["f_h1ev"]
            dn["hb"][pr] = p.cnt["f_mm"]
            h1base = p.cnt["f_h1ev"] - 16
            for mj in range(8):
                pp = [pA, pB][mj % 2]
                p.wait("tensor", "f_ev", dn["ev"][mj % 2])
                for fk in range(16):
                    if mj == 0:
                        p.wait("tensor", "f_h1ev", h1base + fk + 1)
                    p.op("tensor", "matmul", pp,
                         wqB[:, (fk * 8 + mj) * 128:(fk * 8 + mj) * 128 + 128],
                         h1b[:, fk * 512:(fk + 1) * 512], start=(fk == 0), stop=(fk == 15),
                         sem="f_mm2" if fk == 15 else None)
                evb = ev[mj % 2]
                p.wait("scalar", "f_mm2")
                p.wait("scalar", f"f_acst{mj % 2}", dn["evd"][mj % 2])
                if e < 0:
                    p.op("scalar", "copy", evb, pp, sem="f_ev")
                else:
                    glb = gl[mj % 2]
                    p.op("scalar", "copy", glb, pp, sem="f_ev0")
                    p.wait("vector", "f_ev0")
                    p.wait("vector", "f_gbc", dn["gbc"])
                    p.wait("vector", f"f_acst{mj % 2}", dn["evd"][mj % 2])
                    p.op("vector", "tensor_tensor", evb, glb, gbc, op=ALU.mult, sem="f_ev")
                dn["ev"][mj % 2] = p.cnt["f_ev"]
                p.wait("gpsimd", "f_ev")
                if e >= 0 and dn["accb"]:
                    for nm, vv in dn["accb"]:
                        p.wait("gpsimd", nm, vv)
                    dn["accb"] = []
                p.dma("gpsimd", acc0[b][mj * 128:(mj + 1) * 128, ts_ * 512:(ts_ + 1) * 512],
                      evb, f"f_acst{mj % 2}", **({} if e < 0 else {"accum_op": ALU.add}))
                dn["evd"][mj % 2] = p.cnt[f"f_acst{mj % 2}"]
            dn["h1"] = p.cnt["f_mm2"]
            dn["gbc_war"] = p.cnt["f_ev"]
        dn["accb"] = [("f_acst0", p.cnt.get("f_acst0", 0)), ("f_acst1", p.cnt.get("f_acst1", 0))]

    quant_matrix(sw1_d, 2048, D, wqA, lambda fr, sub: sub * 16 + fr)
    quant_matrix(sw2_d, D, 2048, wqB, lambda fr, sub: sub * 8 + fr)
    p.wait("sync", "f_gst")
    if S3_SHARED:
        ffn_pass(-1)
    for e in range(S3_NEXP):
        p.wait("sync", "f_mm")
        p.wait("sync", "f_mm2")
        quant_matrix(ew1_d[e], 2048, D, wqA, lambda fr, sub: sub * 16 + fr)
        quant_matrix(ew2_d[e], D, 2048, wqB, lambda fr, sub: sub * 8 + fr)
        ffn_pass(e)

    p.wait("sync", "f_acst0")
    p.wait("sync", "f_acst1")
    if not S3_G0:
        return
    dng = [0, 0]
    dng2 = [0, 0]
    for b in range(BPC):
        for pt_ in range(8):
            for ts_ in range(4):
                pr = ts_ % 2
                p.wait("sync", "g_sil", dng[pr])
                lv = p.dma("sync", gl[pr], acc0[b][pt_ * 128:(pt_ + 1) * 128,
                                                   ts_ * 512:(ts_ + 1) * 512], f"g_ld{pr}")
                p.wait("scalar", f"g_ld{pr}", lv)
                p.wait("scalar", f"g_st{pr}", dng2[pr])
                p.op("scalar", "activation", gq[pr], gl[pr], AF.Silu, sem="g_sil")
                dng[pr] = p.cnt["g_sil"]
                p.wait("gpsimd", "g_sil")
                p.dma("gpsimd", g0_d[b][pt_ * 128:(pt_ + 1) * 128, ts_ * 512:(ts_ + 1) * 512],
                      gq[pr], f"g_st{pr}")
                dng2[pr] = p.cnt[f"g_st{pr}"]


def stage4(ctx):
    p = ctx.p
    x_tm = ctx.d("x_tm", (BPC, T, D), F32, None, ())
    xn_f = ctx.d("xn_tm_f32", (BPC, T, D), F32, "s1", ("s4",))
    sproj_d = ctx.d("sproj_fm_bf", (BPC, D, T), BF16, "s2", ("s4",))
    g0_d = ctx.d("g0_bf", (BPC, D, T), BF16, "s3", ("s4",))
    x1_d = ctx.d("x1_tm", (BPC, T, D), F32, "s4", ("s8",))
    xn2b_d = ctx.d("xn2_tm_bf", (BPC, T, D), BF16, "s4", ("s4",))
    xn2fm_d = ctx.d("xn2_fm_bf", (BPC, D, T), BF16, "s4", ("s5", "s7"))
    wsel_d = ctx.d("wsel8", (8, NT), F32, "s4", ("s7",))

    NB = 3
    xt = [ctx.sbuf(f"s4_x{i}", (128, D), F32) for i in range(NB)]
    xnt = [ctx.sbuf(f"s4_xn{i}", (128, D), F32) for i in range(NB)]
    spt = [ctx.sbuf(f"s4_sp{i}", (128, D), BF16) for i in range(NB)]
    g0t = [ctx.sbuf(f"s4_g0{i}", (128, D), BF16) for i in range(NB)]
    sb_ = [ctx.sbuf(f"s4_s{i}", (128, D), F32) for i in range(NB)]
    sgt = [ctx.sbuf(f"s4_sg{i}", (128, D), F32) for i in range(NB)]
    x1t = [ctx.sbuf(f"s4_x1{i}", (128, D), F32) for i in range(NB)]
    bns = [ctx.sbuf(f"s4_bn{i}", (128, 12), F32) for i in range(NB)]
    sts = [ctx.sbuf(f"s4_st{i}", (128, 16), F32) for i in range(NB)]
    xn2t = [ctx.sbuf(f"s4_x2n{i}", (128, D), F32) for i in range(NB)]
    xn2b = [ctx.sbuf(f"s4_x2b{i}", (128, D), BF16) for i in range(NB)]
    s01 = ctx.sbuf("s4_s01", (128, 64), F32)
    hshA = ctx.sbuf("s4_hA", (128, 64), F32)
    hshB = ctx.sbuf("s4_hB", (128, 64), F32)
    hshC = ctx.sbuf("s4_hC", (128, 64), F32)
    hshD = ctx.sbuf("s4_hD", (128, 64), F32)
    wrow = ctx.sbuf("s4_wrow", (128, 64), F32)
    wrow2 = ctx.sbuf("s4_wrow2", (128, 64), F32)

    dx = [0] * NB
    dxn = [0] * NB
    dsp = [0] * NB
    dg0 = [0] * NB
    dst_ = [0] * NB
    dx1 = [0] * NB
    dx2b = [0] * NB
    ds01 = 0
    EPS = 1e-5

    for it in range(NT // 128):
        b, r = divmod(it, T // 128)
        pr = it % NB
        t0 = r * 128
        if it >= NB:
            p.wait("sync", "c_v", dx[pr])
        lx = p.dma("sync", xt[pr], x_tm[b, t0:t0 + 128, :], f"c_lx{pr}")
        ln = p.dma("sync", xnt[pr], xn_f[b, t0:t0 + 128, :], f"c_ln{pr}")
        lsp = p.dmat("sync", spt[pr], sproj_d[b][:, t0:t0 + 128], f"c_lsp{pr}")
        lg0 = p.dmat("sync", g0t[pr], g0_d[b][:, t0:t0 + 128], f"c_lg0{pr}")
        p.wait("vector", f"c_ln{pr}", ln)
        p.wait("vector", f"c_lsp{pr}", lsp)
        p.wait("vector", f"c_lx{pr}", lx)
        p.wait("vector", f"c_lg0{pr}", lg0)
        if it >= NB:
            p.wait("vector", f"c_sxb{pr}", dx2b[pr])
            p.wait("vector", f"c_sx1{pr}", dx1[pr])
        p.op("vector", "tensor_tensor", sb_[pr], xnt[pr], spt[pr], op=ALU.add, sem="c_v")
        p.wait("vector", "c_v")
        p.op("vector", "tensor_tensor", sgt[pr], sb_[pr], g0t[pr], op=ALU.mult, sem="c_v")
        p.wait("vector", "c_v")
        p.op("vector", "tensor_tensor", x1t[pr], xt[pr], sgt[pr], op=ALU.add, sem="c_v")
        dx[pr] = p.cnt["c_v"]
        p.wait("sync", "c_v")
        p.dma("sync", x1_d[b, t0:t0 + 128, :], x1t[pr], f"c_sx1{pr}")
        dx1[pr] = p.cnt[f"c_sx1{pr}"]

        bn, st = bns[pr], sts[pr]
        if it >= NB:
            p.wait("vector", "c_hx", dst_[pr])
        p.wait("vector", "c_v")
        p.op("vector", "bn_stats", bn[:, 0:6], x1t[pr][:, 0:512])
        p.op("vector", "bn_stats", bn[:, 6:12], x1t[pr][:, 512:1024], sem="c_v")
        p.wait("vector", "c_v")
        p.op("vector", "bn_aggr", st[:, 0:2], bn, sem="c_v")
        p.wait("vector", "c_v")
        p.op("vector", "tensor_scalar_add", st[:, 2:3], st[:, 1:2], EPS, sem="c_v")
        p.wait("scalar", "c_v")
        p.op("scalar", "activation", st[:, 3:4], st[:, 2:3], AF.Sqrt, sem="c_s")
        p.wait("vector", "c_s")
        p.op("vector", "reciprocal", st[:, 4:5], st[:, 3:4], sem="c_v")
        p.wait("vector", "c_v")
        p.op("vector", "tensor_scalar", st[:, 5:6], st[:, 0:1], st[:, 4:5], -1.0,
             op0=ALU.mult, op1=ALU.mult, sem="c_v")
        p.wait("scalar", "c_v")
        p.op("scalar", "activation", xn2t[pr], x1t[pr], AF.Identity,
             scale=st[:, 4:5], bias=st[:, 5:6], sem="c_hx")
        dst_[pr] = p.cnt["c_hx"]
        p.wait("vector", "c_hx")
        p.op("vector", "tensor_tensor", s01[:, it % 64:it % 64 + 1] if False else s01[:, it:it + 1] if it < 64 else None,
             xn2t[pr][:, 0:1], xn2t[pr][:, 1:2], op=ALU.add, sem="c_s01")
        p.op("vector", "tensor_copy", xn2b[pr], xn2t[pr], sem="s4xb")
        p.wait("gpsimd", "s4xb")
        p.dma("gpsimd", xn2b_d[b, t0:t0 + 128, :], xn2b[pr], f"c_sxb{pr}")
        dx2b[pr] = p.cnt[f"c_sxb{pr}"]

    p.wait("vector", "c_s01")
    def fence(name="h_v"):
        p.wait("vector", name)
    p.op("vector", "tensor_scalar_mul", hshA, s01, 0.125, sem="h_v"); fence()
    p.op("vector", "tensor_scalar", hshB, hshA, 1.0, None, op0=ALU.is_ge, sem="h_v"); fence()
    for k in range(2, 8):
        p.op("vector", "tensor_scalar", hshC, hshA, float(k), None, op0=ALU.is_ge, sem="h_v"); fence()
        p.op("vector", "tensor_tensor", hshD, hshB, hshC, op=ALU.add, sem="h_v"); fence()
        hshB, hshD = hshD, hshB
    for k in range(0, 8):
        p.op("vector", "tensor_scalar", hshC, hshA, float(-k), None, op0=ALU.is_lt, sem="h_v"); fence()
        p.op("vector", "tensor_tensor", hshD, hshB, hshC, op=ALU.subtract, sem="h_v"); fence()
        hshB, hshD = hshD, hshB
    p.op("vector", "scalar_tensor_tensor", hshA, hshB, -8.0, s01,
         op0=ALU.mult, op1=ALU.add, sem="h_v"); fence()
    p.op("vector", "tensor_scalar", hshB, hshA, 1.0, None, op0=ALU.is_ge, sem="h_v"); fence()
    for k in range(2, 8):
        p.op("vector", "tensor_scalar", hshC, hshA, float(k), None, op0=ALU.is_ge, sem="h_v"); fence()
        p.op("vector", "tensor_tensor", hshD, hshB, hshC, op=ALU.add, sem="h_v"); fence()
        hshB, hshD = hshD, hshB
    p.op("vector", "tensor_scalar", hshC, s01, 0.0, None, op0=ALU.is_lt, sem="h_v"); fence()
    p.op("vector", "tensor_tensor", hshA, hshB, hshC, op=ALU.add, sem="h_v"); fence()
    p.op("vector", "tensor_scalar", hshC, hshA, 8.0, None, op0=ALU.is_ge, sem="h_v"); fence()
    p.op("vector", "scalar_tensor_tensor", hshB, hshC, -8.0, hshA,
         op0=ALU.mult, op1=ALU.add, sem="h_v"); fence()
    dws = 0
    for e in range(8):
        p.op("vector", "tensor_scalar", hshC, hshB, float(e), None, op0=ALU.is_equal, sem="h_v"); fence()
        p.op("vector", "tensor_scalar", hshD, hshB, float((e + 7) % 8), None,
             op0=ALU.is_equal, sem="h_v"); fence()
        p.op("vector", "tensor_tensor", wrow, hshC, hshD, op=ALU.add, sem="h_v"); fence()
        p.wait("vector", "h_ws", dws)
        p.op("vector", "tensor_scalar_mul", wrow2, wrow, 0.5, sem="h_v"); fence()
        p.wait("gpsimd", "h_v")
        p.dma("gpsimd", wsel_d[e].rearrange("(tt q) -> q tt", q=128), wrow2, "h_ws",
              allow_slow_non_contiguous=True)
        dws = p.cnt["h_ws"]

    trb = [ctx.sbuf(f"s4_tr{i}", (128, T), BF16) for i in range(2)]
    done_tr = [0, 0]
    for nm in [f"c_sxb{i}" for i in range(NB)]:
        p.wait("sync", nm)
    i = 0
    for b in range(BPC):
        for ki in range(8):
            pr = i % 2
            if i >= 2:
                p.wait("sync", f"c_trs{pr}", done_tr[pr])
            ldv = p.dmat("sync", trb[pr], xn2b_d[b, :, ki * 128:(ki + 1) * 128], f"c_trl{pr}")
            p.wait("scalar", f"c_trl{pr}", ldv)
            p.dma("scalar", xn2fm_d[b, ki * 128:(ki + 1) * 128, :], trb[pr], f"c_trs{pr}")
            done_tr[pr] = p.cnt[f"c_trs{pr}"]
            i += 1


def stage5(ctx):
    p = ctx.p
    xn2fm_d = ctx.d("xn2_fm_bf", (BPC, D, T), BF16, "s4", ("s5", "s7"))
    q_d = ctx.d("q_fm", (BPC, D, T), BF16, "s5", ("s6",))
    k_d = ctx.d("k_fm", (BPC, 256, T), BF16, "s5", ("s6",))
    v_d = ctx.d("v1_tm", (BPC, T, 256), BF16, "s5", ("s6",))
    wq_d = ctx.d("wq_T", (D, D), F32, None, ())
    wk_d = ctx.d("wk_T", (D, 256), F32, None, ())
    wv_d = ctx.d("wv_T", (D, 256), F32, None, ())

    wq = ctx.sbuf("s5_wq", (128, 8 * 1024), BF16)
    wk = ctx.sbuf("s5_wk", (128, 8 * 256), BF16)
    wv = ctx.sbuf("s5_wv", (128, 8 * 256), BF16)
    p.dma("gpsimd", wq.rearrange("q (ki c) -> q ki c", ki=8),
          wq_d.rearrange("(ki q) c -> q ki c", q=128), "a_wl")
    p.dma("gpsimd", wk.rearrange("q (ki c) -> q ki c", ki=8),
          wk_d.rearrange("(ki q) c -> q ki c", q=128), "a_wl")
    p.dma("gpsimd", wv.rearrange("q (ki c) -> q ki c", ki=8),
          wv_d.rearrange("(ki q) c -> q ki c", q=128), "a_wl")

    pA = ctx.psum("pA", (128, 512), F32)
    pB = ctx.psum("pB", (128, 512), F32)
    hbuf = [ctx.sbuf(f"s3_h{i}", (128, 8 * 512), BF16) for i in range(2)]
    evq = [ctx.sbuf(f"s5_ev{i}", (128, 512), BF16) for i in range(2)]

    p.wait("tensor", "a_wl")
    dhb = [0, 0]
    dev = [0, 0]
    devd = [0, 0]
    for gtt in range(16):
        b, ts_ = divmod(gtt, 4)
        pr = gtt % 2
        p.wait("sync", "a_mm", dhb[pr])
        ldv = p.dma("sync", hbuf[pr].rearrange("q (ki t) -> q ki t", ki=8),
                    xn2fm_d[b].rearrange("(ki q) t -> q ki t", q=128)
                    [:, :, ts_ * 512:(ts_ + 1) * 512], f"a_hl{pr}")
        p.wait("tensor", f"a_hl{pr}", ldv)
        jobs = []
        for mj in range(8):
            jobs.append(("q", mj, q_d[b][mj * 128:(mj + 1) * 128, ts_ * 512:(ts_ + 1) * 512],
                         [wq[:, ki * 1024 + mj * 128: ki * 1024 + mj * 128 + 128] for ki in range(8)]))
        for mj in range(2):
            jobs.append(("k", mj, k_d[b][mj * 128:(mj + 1) * 128, ts_ * 512:(ts_ + 1) * 512],
                         [wk[:, ki * 256 + mj * 128: ki * 256 + mj * 128 + 128] for ki in range(8)]))
        for ji, (kind, mj, dram_out, wblocks) in enumerate(jobs):
            pp = [pA, pB][ji % 2]
            p.wait("tensor", "a_ev", dev[ji % 2])
            for ki in range(8):
                p.op("tensor", "matmul", pp, wblocks[ki],
                     hbuf[pr][:, ki * 512:(ki + 1) * 512], start=(ki == 0), stop=(ki == 7),
                     sem="a_mm" if ki == 7 else None)
            p.wait("scalar", "a_mm")
            p.wait("scalar", f"a_st{ji % 2}", devd[ji % 2])
            p.op("scalar", "copy", evq[ji % 2], pp, sem="a_ev")
            dev[ji % 2] = p.cnt["a_ev"]
            p.wait("gpsimd", "a_ev")
            p.dma("gpsimd", dram_out, evq[ji % 2], f"a_st{ji % 2}")
            devd[ji % 2] = p.cnt[f"a_st{ji % 2}"]
        for sub in range(4):
            ji = 10 + sub
            pp = [pA, pB][ji % 2]
            p.wait("tensor", "a_ev", dev[ji % 2])
            for ki in range(8):
                p.op("tensor", "matmul", pp[:, 0:256],
                     hbuf[pr][:, ki * 512 + sub * 128: ki * 512 + sub * 128 + 128],
                     wv[:, ki * 256:(ki + 1) * 256], start=(ki == 0), stop=(ki == 7),
                     sem="a_mm" if ki == 7 else None)
            p.wait("scalar", "a_mm")
            p.wait("scalar", f"a_st{ji % 2}", devd[ji % 2])
            p.op("scalar", "copy", evq[ji % 2][:, 0:256], pp[:, 0:256], sem="a_ev")
            dev[ji % 2] = p.cnt["a_ev"]
            p.wait("gpsimd", "a_ev")
            p.dma("gpsimd", v_d[b][ts_ * 512 + sub * 128: ts_ * 512 + sub * 128 + 128, :],
                  evq[ji % 2][:, 0:256], f"a_st{ji % 2}")
            devd[ji % 2] = p.cnt[f"a_st{ji % 2}"]
        dhb[pr] = p.cnt["a_mm"]


def stage6(ctx):
    p = ctx.p
    q_d = ctx.d("q_fm", (BPC, D, T), BF16, "s5", ("s6",))
    k_d = ctx.d("k_fm", (BPC, 256, T), BF16, "s5", ("s6",))
    v_d = ctx.d("v1_tm", (BPC, T, 256), BF16, "s5", ("s6",))
    o_d = ctx.d("o1_fm", (BPC, D, T), BF16, "s6", ("s6",))
    s2_d = ctx.d("s2_fm_bf", (BPC, D, T), BF16, "s6", ("s8",))
    am_d = ctx.d("amask", (4, 128, 512), F32, None, ())
    wo_d = ctx.d("wo_T", (D, D), F32, None, ())
    identb = load_const(ctx, "ident_bf", (128, 128), BF16, sem="at_c")

    amask = ctx.sbuf("s6_am", (128, 4 * 512), F32)
    p.dma("sync", amask.rearrange("q (m c) -> q m c", m=4),
          am_d.rearrange("m q c -> q m c"), "at_c")
    wo = ctx.sbuf("s5_wq", (128, 8 * 1024), BF16)
    p.dma("gpsimd", wo.rearrange("q (ki c) -> q ki c", ki=8),
          wo_d.rearrange("(ki q) c -> q ki c", q=128), "at_cg")

    pA = ctx.psum("pA", (128, 512), F32)
    pB = ctx.psum("pB", (128, 512), F32)
    pT = ctx.psum("pT2", (128, 512), BF16)
    pT3 = ctx.psum("pT3", (128, 512), BF16)
    pD = ctx.psum("pD", (128, 512), F32)

    qh = ctx.sbuf("s6_qh", (64, T), BF16)
    kg = ctx.sbuf("s6_kg", (64, T), BF16)
    vb = ctx.sbuf("s6_vb", (128, 16 * 64), BF16)
    Ssb = ctx.sbuf("s6_S", (128, T), F32)
    Psb = ctx.sbuf("s6_P", (128, T), BF16)
    P2sb = ctx.sbuf("s6_P2", (128, T), BF16)
    PtA = ctx.sbuf("s6_PtA", (128, 16 * 128), BF16)
    ob = ctx.sbuf("s6_ob", (64, T), BF16)
    sst = ctx.sbuf("s6_sst", (128, 8), F32)
    hbuf = [ctx.sbuf(f"s3_h{i}", (128, 8 * 512), BF16) for i in range(2)]
    evq = [ctx.sbuf(f"s5_ev{i}", (128, 512), BF16) for i in range(2)]

    p.wait("tensor", "at_c")
    p.wait("tensor", "at_cg")
    dsc = [0, 0]
    dptev = [0, 0]
    dob = 0
    dqh = 0
    for b in range(BPC):
        for h in range(16):
            g = h // 4
            p.wait("sync", "at_mm", dqh)
            p.wait("sync", "at_obst", p.cnt.get("at_obst", 0))
            l1 = p.dma("sync", qh, q_d[b][h * 64:(h + 1) * 64, :], "at_l")
            l2 = p.dma("sync", kg, k_d[b][g * 64:(g + 1) * 64, :], "at_l")
            l3 = p.dma("sync", vb.rearrange("q (tt e) -> q tt e", tt=16),
                       v_d[b].rearrange("(tt q) j -> q tt j", q=128)
                       [:, :, g * 64:(g + 1) * 64], "at_l")
            p.wait("tensor", "at_l", l3)
            for qt in range(16):
                nk5 = qt // 4 + 1
                L = nk5 * 512
                for kt in range(nk5):
                    pp = [pA, pB][kt % 2]
                    p.wait("tensor", "at_sev", dsc[kt % 2])
                    p.op("tensor", "matmul", pp, qh[:, qt * 128:(qt + 1) * 128],
                         kg[:, kt * 512:(kt + 1) * 512], start=True, stop=True, sem="at_mm")
                    p.wait("vector", "at_mm")
                    if kt == qt // 4:
                        p.op("vector", "scalar_tensor_tensor", Ssb[:, kt * 512:(kt + 1) * 512],
                             pp, 0.125, amask[:, (qt % 4) * 512:(qt % 4) * 512 + 512],
                             op0=ALU.mult, op1=ALU.add, sem="at_sev")
                    else:
                        p.op("vector", "tensor_scalar_mul", Ssb[:, kt * 512:(kt + 1) * 512],
                             pp, 0.125, sem="at_sev")
                    dsc[kt % 2] = p.cnt["at_sev"]
                p.wait("vector", "at_sev")
                p.op("vector", "reduce_max", sst[:, 0:1], Ssb[:, 0:L], AX.X,
                     negate=True, sem="at_v")
                p.wait("scalar", "at_v")
                p.op("scalar", "activation", Psb[:, 0:L], Ssb[:, 0:L], AF.Exp,
                     bias=sst[:, 0:1], accum_out=sst[:, 1:2], sem="at_s")
                p.wait("vector", "at_s")
                p.op("vector", "reciprocal", sst[:, 2:3], sst[:, 1:2], sem="at_v")
                p.wait("vector", "at_v")
                p.op("vector", "tensor_scalar", P2sb[:, 0:L], Psb[:, 0:L], sst[:, 2:3],
                     None, op0=ALU.mult, sem="at_p2")
                nkt = qt + 1
                p.wait("tensor", "at_p2")
                ptbase = p.cnt.get("at_ptev", 0)
                first_ev = True
                for ktt in range(nkt):
                    ptk = [pT, pT3][ktt % 2][:, 0:128]
                    p.wait("tensor", "at_ptev", max(0, ptbase + ktt - 1))
                    p.op("tensor", "transpose", ptk, P2sb[:, ktt * 128:(ktt + 1) * 128],
                         identb, sem="at_tr")
                    p.wait("scalar", "at_tr")
                    if first_ev:
                        p.wait("scalar", "at_mmo")
                        first_ev = False
                    p.op("scalar", "copy", PtA[:, ktt * 128:(ktt + 1) * 128], ptk,
                         sem="at_ptev")
                p.wait("tensor", "at_oev")
                for ktt in range(nkt):
                    p.wait("tensor", "at_ptev", ptbase + ktt + 1)
                    p.op("tensor", "matmul", pD[0:64, 0:128], vb[:, ktt * 64:(ktt + 1) * 64],
                         PtA[:, ktt * 128:(ktt + 1) * 128], start=(ktt == 0),
                         stop=(ktt == nkt - 1), sem="at_mmo" if ktt == nkt - 1 else None)
                p.wait("scalar", "at_mmo")
                p.op("scalar", "copy", ob[:, qt * 128:(qt + 1) * 128], pD[0:64, 0:128],
                     sem="at_oev")
            dqh = p.cnt["at_mm"]
            p.wait("gpsimd", "at_oev")
            p.dma("gpsimd", o_d[b][h * 64:(h + 1) * 64, :], ob, "at_obst")

    p.wait("sync", "at_obst")
    dhb = [0, 0]
    dev = [0, 0]
    devd = [0, 0]
    for gtt in range(16):
        b, ts_ = divmod(gtt, 4)
        pr = gtt % 2
        p.wait("sync", "at_mm", dhb[pr])
        ldv = p.dma("sync", hbuf[pr].rearrange("q (ki t) -> q ki t", ki=8),
                    o_d[b].rearrange("(ki q) t -> q ki t", q=128)
                    [:, :, ts_ * 512:(ts_ + 1) * 512], f"at_ol{pr}")
        p.wait("tensor", f"at_ol{pr}", ldv)
        for mj in range(8):
            pp = [pA, pB][mj % 2]
            p.wait("tensor", "at_ev2", dev[mj % 2])
            p.wait("tensor", "at_sev")
            for ki in range(8):
                p.op("tensor", "matmul", pp, wo[:, ki * 1024 + mj * 128: ki * 1024 + mj * 128 + 128],
                     hbuf[pr][:, ki * 512:(ki + 1) * 512], start=(ki == 0), stop=(ki == 7),
                     sem="at_mm" if ki == 7 else None)
            p.wait("scalar", "at_mm")
            p.wait("scalar", f"at_s2st{mj % 2}", devd[mj % 2])
            p.op("scalar", "copy", evq[mj % 2], pp, sem="at_ev2")
            dev[mj % 2] = p.cnt["at_ev2"]
            p.wait("gpsimd", "at_ev2")
            p.dma("gpsimd", s2_d[b][mj * 128:(mj + 1) * 128, ts_ * 512:(ts_ + 1) * 512],
                  evq[mj % 2], f"at_s2st{mj % 2}")
            devd[mj % 2] = p.cnt[f"at_s2st{mj % 2}"]
        dhb[pr] = p.cnt["at_mm"]


def stage7(ctx):
    p = ctx.p
    xn2fm_d = ctx.d("xn2_fm_bf", (BPC, D, T), BF16, "s4", ("s5", "s7"))
    acc1 = ctx.d("acc1_fm", (BPC, D, T), F32, "s7", ("s7",))
    g1_d = ctx.d("g1_bf", (BPC, D, T), BF16, "s7", ("s8",))
    wsel_d = ctx.d("wsel8", (8, NT), F32, "s4", ("s7",))
    W1_d = ctx.d("W1", (8, D, 4096), F32, None, ())
    W2_d = ctx.d("W2", (8, 4096, D), F32, None, ())
    b1_d = ctx.d("b1c", (8, 128, 32), F32, None, ())
    b2_d = ctx.d("b2c", (8, 128, 8), F32, None, ())
    ones_row = load_const(ctx, "ones_row", (1, 128), sem="l_c")

    pA = ctx.psum("pA", (128, 512), F32)
    pB = ctx.psum("pB", (128, 512), F32)
    pC = ctx.psum("pC", (128, 512), F32)
    hbuf = [ctx.sbuf(f"s3_h{i}", (128, 8 * 512), BF16) for i in range(2)]
    h1b = ctx.sbuf("s3_h1big", (128, 16 * 512), BF16)
    wqA = ctx.sbuf("s3_wqA", (128, 128 * 128), BF16)
    wqB = ctx.sbuf("s3_wqB", (128, 128 * 128), BF16)
    b1sb = ctx.sbuf("s7_b1", (128, 32), F32)
    b2sb = ctx.sbuf("s7_b2", (128, 8), F32)
    grow = ctx.sbuf("s3_grow", (1, 512), F32)
    gbc = ctx.sbuf("s3_gbc", (128, 512), F32)
    ev = [ctx.sbuf(f"s3_ev{i}", (128, 512), F32) for i in range(2)]
    gl = [ctx.sbuf(f"s3_gl{i}", (128, 512), F32) for i in range(2)]
    gq = [ctx.sbuf(f"s3_gq{i}", (128, 512), BF16) for i in range(2)]

    dn = {"hb": [0, 0], "ff": [0, 0], "ev": [0, 0], "evd": [0, 0],
          "gbc": 0, "gbc_war": 0, "h1": 0, "gmm": 0, "accb": [], "wl": 0, "bl": 0}

    p.wait("tensor", "l_c")
    for e in range(8):
        for ha in range(2):
            p.wait("gpsimd", "l_mm", dn["wl"])
            p.wait("gpsimd", "l_mm2", dn["wl2"] if "wl2" in dn else 0)
            w1v = p.dma("gpsimd", wqA.rearrange("q (ki f) -> q ki f", ki=8),
                        W1_d[e].rearrange("(ki q) f -> q ki f", q=128)
                        [:, :, ha * 2048:(ha + 1) * 2048], "l_w1")
            w2v = p.dma("gpsimd", wqB.rearrange("q (fk m) -> q fk m", fk=16),
                        W2_d[e][ha * 2048:(ha + 1) * 2048, :]
                        .rearrange("(fk q) m -> q fk m", q=128), "l_w2")
            if ha == 0:
                p.wait("sync", "l_h1ev", dn["bl"])
                p.wait("sync", "l_ev0", dn["bl2"] if "bl2" in dn else 0)
                b1v = p.dma("sync", b1sb, b1_d[e], "l_b1")
                b2v = p.dma("sync", b2sb, b2_d[e], "l_b1")
            p.wait("tensor", "l_w1", w1v)
            p.wait("tensor", "l_w2", w2v)
            p.wait("scalar", "l_b1")
            for gtt in range(16):
                b, ts_ = divmod(gtt, 4)
                pr = gtt % 2
                p.wait("sync", "l_mm", dn["hb"][pr])
                ldv = p.dma("sync", hbuf[pr].rearrange("q (ki t) -> q ki t", ki=8),
                            xn2fm_d[b].rearrange("(ki q) t -> q ki t", q=128)
                            [:, :, ts_ * 512:(ts_ + 1) * 512], f"l_hl{pr}")
                if True:
                    p.wait("sync", "l_mmg", dn["gmm"])
                    glv = p.dma("sync", grow, wsel_d[e:e + 1, gtt * 512:(gtt + 1) * 512],
                                "l_gl")
                    p.wait("tensor", "l_gl", glv)
                    p.wait("tensor", "l_gbc", dn["gbc"])
                    p.op("tensor", "matmul", pC[:, 0:512], ones_row[0:1, 0:128],
                         grow, start=True, stop=True, sem="l_mmg")
                    dn["gmm"] = p.cnt["l_mmg"]
                    p.wait("scalar", "l_mmg")
                    p.wait("scalar", "l_ev", dn["gbc_war"])
                    p.op("scalar", "copy", gbc, pC[:, 0:512], sem="l_gbc")
                    dn["gbc"] = p.cnt["l_gbc"]
                p.wait("tensor", f"l_hl{pr}", ldv)
                p.wait("scalar", "l_mm2", dn["h1"])
                for fj in range(16):
                    pp = [pA, pB][fj % 2]
                    p.wait("tensor", "l_h1ev", dn["ff"][fj % 2])
                    p.wait("tensor", "l_ev", dn["ev"][fj % 2])
                    for ki in range(8):
                        p.op("tensor", "matmul", pp,
                             wqA[:, (ki * 16 + fj) * 128:(ki * 16 + fj) * 128 + 128],
                             hbuf[pr][:, ki * 512:(ki + 1) * 512],
                             start=(ki == 0), stop=(ki == 7),
                             sem="l_mm" if ki == 7 else None)
                    p.wait("scalar", "l_mm")
                    p.op("scalar", "activation", h1b[:, fj * 512:(fj + 1) * 512], pp,
                         AF.Relu, bias=b1sb[:, ha * 16 + fj: ha * 16 + fj + 1],
                         sem="l_h1ev")
                    dn["ff"][fj % 2] = p.cnt["l_h1ev"]
                dn["hb"][pr] = p.cnt["l_mm"]
                h1base = p.cnt["l_h1ev"] - 16
                for mj in range(8):
                    pp = [pA, pB][mj % 2]
                    p.wait("tensor", "l_ev", dn["ev"][mj % 2])
                    for fk in range(16):
                        if mj == 0:
                            p.wait("tensor", "l_h1ev", h1base + fk + 1)
                        p.op("tensor", "matmul", pp,
                             wqB[:, (fk * 8 + mj) * 128:(fk * 8 + mj) * 128 + 128],
                             h1b[:, fk * 512:(fk + 1) * 512],
                             start=(fk == 0), stop=(fk == 15),
                             sem="l_mm2" if fk == 15 else None)
                    evb = ev[mj % 2]
                    glb = gl[mj % 2]
                    p.wait("scalar", "l_mm2")
                    p.wait("scalar", f"l_acst{mj % 2}", dn["evd"][mj % 2])
                    if ha == 1:
                        p.op("scalar", "activation", glb, pp, AF.Identity,
                             bias=b2sb[:, mj:mj + 1], sem="l_ev0")
                    else:
                        p.op("scalar", "copy", glb, pp, sem="l_ev0")
                    p.wait("vector", "l_ev0")
                    p.wait("vector", "l_gbc", dn["gbc"])
                    p.wait("vector", f"l_acst{mj % 2}", dn["evd"][mj % 2])
                    p.op("vector", "tensor_tensor", evb, glb, gbc, op=ALU.mult, sem="l_ev")
                    dn["ev"][mj % 2] = p.cnt["l_ev"]
                    p.wait("gpsimd", "l_ev")
                    if dn["accb"]:
                        for nm, vv in dn["accb"]:
                            p.wait("gpsimd", nm, vv)
                        dn["accb"] = []
                    first = (e == 0 and ha == 0)
                    p.dma("gpsimd", acc1[b][mj * 128:(mj + 1) * 128, ts_ * 512:(ts_ + 1) * 512],
                          evb, f"l_acst{mj % 2}",
                          **({} if first else {"accum_op": ALU.add}))
                    dn["evd"][mj % 2] = p.cnt[f"l_acst{mj % 2}"]
                dn["h1"] = p.cnt["l_mm2"]
                dn["gbc_war"] = p.cnt["l_ev"]
            dn["wl"] = p.cnt["l_mm"]
            dn["wl2"] = p.cnt["l_mm2"]
            if ha == 1:
                dn["bl"] = p.cnt["l_h1ev"]
                dn["bl2"] = p.cnt["l_ev0"]
            dn["accb"] = [("l_acst0", p.cnt.get("l_acst0", 0)),
                          ("l_acst1", p.cnt.get("l_acst1", 0))]

    p.wait("sync", "l_acst0")
    p.wait("sync", "l_acst1")
    dng = [0, 0]
    dng2 = [0, 0]
    for b in range(BPC):
        for pt_ in range(8):
            for ts_ in range(4):
                pr = ts_ % 2
                p.wait("sync", "l_sil", dng[pr])
                lv = p.dma("sync", gl[pr], acc1[b][pt_ * 128:(pt_ + 1) * 128,
                                                   ts_ * 512:(ts_ + 1) * 512], f"l_gld{pr}")
                p.wait("scalar", f"l_gld{pr}", lv)
                p.wait("scalar", f"l_gst{pr}", dng2[pr])
                p.op("scalar", "activation", gq[pr], gl[pr], AF.Silu, sem="l_sil")
                dng[pr] = p.cnt["l_sil"]
                p.wait("gpsimd", "l_sil")
                p.dma("gpsimd", g1_d[b][pt_ * 128:(pt_ + 1) * 128, ts_ * 512:(ts_ + 1) * 512],
                      gq[pr], f"l_gst{pr}")
                dng2[pr] = p.cnt[f"l_gst{pr}"]


def stage8(ctx):
    p = ctx.p
    x1_d = ctx.d("x1_tm", (BPC, T, D), F32, "s4", ("s8",))
    s2_d = ctx.d("s2_fm_bf", (BPC, D, T), BF16, "s6", ("s8",))
    g1_d = ctx.d("g1_bf", (BPC, D, T), BF16, "s7", ("s8",))
    out_d = ctx.d("out_tm", (BPC, T, D), F32, "s8", ())

    NB = 3
    x1t = [ctx.sbuf(f"s8_x1{i}", (128, D), F32) for i in range(NB)]
    s2t = [ctx.sbuf(f"s8_s2{i}", (128, D), BF16) for i in range(NB)]
    g1t = [ctx.sbuf(f"s8_g1{i}", (128, D), BF16) for i in range(NB)]
    sgt = [ctx.sbuf(f"s8_sg{i}", (128, D), F32) for i in range(NB)]
    x2t = [ctx.sbuf(f"s8_x2{i}", (128, D), F32) for i in range(NB)]

    dv = [0] * NB
    dst8 = [0] * NB
    for it in range(NT // 128):
        b, r = divmod(it, T // 128)
        pr = it % NB
        t0 = r * 128
        if it >= NB:
            p.wait("sync", "o_v", dv[pr])
        lx = p.dma("sync", x1t[pr], x1_d[b, t0:t0 + 128, :], f"o_lx{pr}")
        ls = p.dmat("sync", s2t[pr], s2_d[b][:, t0:t0 + 128], f"o_ls{pr}")
        lg = p.dmat("sync", g1t[pr], g1_d[b][:, t0:t0 + 128], f"o_lg{pr}")
        p.wait("vector", f"o_lx{pr}", lx)
        p.wait("vector", f"o_ls{pr}", ls)
        p.wait("vector", f"o_lg{pr}", lg)
        if it >= NB:
            p.wait("vector", f"o_st{pr}", dst8[pr])
        p.op("vector", "tensor_tensor", sgt[pr], s2t[pr], g1t[pr], op=ALU.mult, sem="o_v")
        p.wait("vector", "o_v")
        p.op("vector", "tensor_tensor", x2t[pr], x1t[pr], sgt[pr], op=ALU.add, sem="o_v")
        dv[pr] = p.cnt["o_v"]
        p.wait("sync", "o_v")
        p.dma("sync", out_d[b, t0:t0 + 128, :], x2t[pr], f"o_st{pr}")
        dst8[pr] = p.cnt[f"o_st{pr}"]


STAGE_FNS = {"s1": stage1, "s2": stage2, "s3": stage3, "s4": stage4,
             "s5": stage5, "s6": stage6, "s7": stage7, "s8": stage8}


def build_launch(stages, core_input_names):
    ctx = Ctx(stages, core_input_names)
    with ctx.stack:
        dram_common(ctx)
        first = True
        for s in ALL_STAGES:
            if s in ctx.stages:
                if not first:
                    ctx.p.barrier()
                ctx.new_stage()
                STAGE_FNS[s](ctx)
                first = False
        check_deadlock(ctx.p)
        ctx.finalize()
    return ctx


LAUNCH_PLAN = [tuple(ALL_STAGES)]


def kernel(**inputs):
    in_maps = host_prepare(inputs)
    core_input_names = set(in_maps[0].keys())
    state = [dict(m) for m in in_maps]
    for stages in LAUNCH_PLAN:
        ctx = build_launch(stages, core_input_names)
        launch_maps = [{k: st[k] for k in ctx.ext_in} for st in state]
        res = run_bass_kernel_spmd(ctx.nc, launch_maps, core_ids=list(range(NCORES)))
        for c in range(NCORES):
            for name in ctx.ext_out:
                state[c][name] = np.asarray(res.results[c][name])
    out = np.concatenate([state[c]["out_tm"] for c in range(NCORES)], axis=0)
    return out.astype(np.float32)
